# version 10
# baseline (speedup 1.0000x reference)
import os
import numpy as np
import concourse.bass as bass
import concourse.mybir as mybir
from concourse.bass_utils import run_bass_kernel_spmd
from contextlib import ExitStack

F32 = mybir.dt.float32
BF16 = mybir.dt.bfloat16
I32 = mybir.dt.int32
AF = mybir.ActivationFunctionType
ALU = mybir.AluOpType
AX = mybir.AxisListType

BATCH, SEQ, DM = 4, 2048, 4096
HEADS, DK = 32, 128
CB, CS = 65536, 8
TOK = BATCH * SEQ
NCORE = 8
EH, ES = 16, 2048
ISH, NIC, IRE = 1408, 11, 1376
TT64 = TOK // 128
QT = SEQ // 128
HC = DM // 128
HALF = SEQ // 2
SCALE = float(1.0 / np.sqrt(DK))
EPS = 1e-5

GU_UNITS = [(w, ic) for ic in range(NIC) for w in ("g", "u")]
WD_UNITS = [("d", hc) for hc in range(HC)]
UNITS = GU_UNITS + WD_UNITS


class SemC:
    def __init__(self, name, step=1):
        self.name, self.step, self.n, self.h = name, step, 0, None

    def inc(self, instr, k=1):
        if instr is not None:
            instr.then_inc(self.h, self.step * k)
        self.n += self.step * k
        return instr


class Marks:
    def __init__(self):
        self.d = {}
        self.dry = True

    def set(self, name, sem):
        if self.dry:
            self.d[name] = (sem, sem.n)
        else:
            assert name in self.d and self.d[name] == (sem, sem.n), \
                (name, self.d.get(name), sem.name, sem.n)

    def alias(self, name, src):
        if self.dry:
            self.d[name] = self.d[src]

    def wait(self, eng, name):
        if self.dry:
            return
        sem, v = self.d[name]
        if v > 0:
            eng.wait_ge(sem.h, v)


class DryInstr:
    def then_inc(self, *a):
        return self


class DryEng:
    def __getattr__(self, name):
        def f(*a, **k):
            return DryInstr()
        return f


def build_program():
    nc = bass.Bass("TRN2", target_bir_lowering=False)

    xfull = nc.dram_tensor("xfull", [TOK, DM], F32, kind="ExternalInput")
    xres = nc.dram_tensor("xres", [HALF, DM], F32, kind="ExternalInput")
    wq = nc.dram_tensor("wq", [EH, HC, 128, 128], F32, kind="ExternalInput")
    wk = nc.dram_tensor("wk", [EH, HC, 128, 128], F32, kind="ExternalInput")
    wv = nc.dram_tensor("wv", [EH, HC, 128, 128], F32, kind="ExternalInput")
    wo = nc.dram_tensor("wo", [8, EH, 128, 512], F32, kind="ExternalInput")
    idxg = nc.dram_tensor("idxg", [NIC, 128, 512], I32, kind="ExternalInput")
    idxu = nc.dram_tensor("idxu", [NIC, 128, 512], I32, kind="ExternalInput")
    idxd = nc.dram_tensor("idxd", [HC, 128, 176], I32, kind="ExternalInput")
    cbt = nc.dram_tensor("cbt", [CB, CS], F32, kind="ExternalInput")
    out = nc.dram_tensor("o", [HALF, DM], F32, kind="ExternalOutput")

    tab = nc.dram_tensor("tab", [CB, CS], BF16)
    xnd = nc.dram_tensor("xnd", [128, HC * TOK], BF16)
    qd = nc.dram_tensor("qd", [EH, 128, SEQ], BF16)
    kd = nc.dram_tensor("kd", [EH, 128, SEQ], BF16)
    vd = nc.dram_tensor("vd", [EH, 128, SEQ], BF16)
    wgT = nc.dram_tensor("wgT", [NIC, HC, 128, 128], BF16)
    wuT = nc.dram_tensor("wuT", [NIC, HC, 128, 128], BF16)
    hhd = nc.dram_tensor("hhd", [NIC, 128, TOK], BF16)
    attnd = nc.dram_tensor("attnd", [SEQ, DM], F32)
    rsin = [nc.dram_tensor(f"rsin{j}", [TOK, 1024], F32) for j in range(4)]
    rsout = [nc.dram_tensor(f"rsout{j}", [HALF, 1024], F32) for j in range(4)]
    DBG = bool(os.environ.get("MK_DEBUG"))
    if DBG:
        dbg_xn = nc.dram_tensor("dbg_xn", [128, 4096], BF16, kind="ExternalOutput")
        dbg_q = nc.dram_tensor("dbg_q", [128, SEQ], BF16, kind="ExternalOutput")
        dbg_k = nc.dram_tensor("dbg_k", [128, SEQ], BF16, kind="ExternalOutput")
        dbg_v = nc.dram_tensor("dbg_v", [128, SEQ], BF16, kind="ExternalOutput")
        dbg_at = nc.dram_tensor("dbg_at", [256, DM], F32, kind="ExternalOutput")
        dbg_wg = nc.dram_tensor("dbg_wg", [HC * 128, 128], BF16, kind="ExternalOutput")
        dbg_hh = nc.dram_tensor("dbg_hh", [128, 2048], BF16, kind="ExternalOutput")
        dbg_rs = nc.dram_tensor("dbg_rs", [256, DM], F32, kind="ExternalOutput")
        dbg_tab = nc.dram_tensor("dbg_tab", [1024, CS], BF16, kind="ExternalOutput")
        dbg_a = nc.dram_tensor("dbg_a", [128, EH * SEQ], BF16, kind="ExternalOutput")

    ctx = ExitStack()
    with ctx:
        E = ctx.enter_context
        AB = 204 * 1024
        arena = E(nc.sbuf_tensor("arena", [128, AB], mybir.dt.uint8))
        hf, hb, hi = arena.bitcast(F32), arena.bitcast(BF16), arena.bitcast(I32)
        WF, WB = AB // 4, AB // 2

        class TV:
            def __init__(self, h, W, c0, cols):
                self.h, self.W, self.c0, self.cols = h, W, c0, cols

            def ap(self, p0=0, pn=128, f0=0, fn=None, extra=None):
                fn = self.cols - f0 if fn is None else fn
                pat = [[self.W, pn]] + (extra if extra else [[1, fn]])
                return bass.AP(self.h, p0 * self.W + self.c0 + f0, pat)

            def __getitem__(self, key):
                ps_, fs = key
                p0 = ps_.start or 0
                pn = (128 if ps_.stop is None else ps_.stop) - p0
                f0 = fs.start or 0
                fn = (self.cols if fs.stop is None else fs.stop) - f0
                return self.ap(p0, pn, f0, fn)

        off = [0]

        def alloc(dt, cols, align=512):
            if cols <= 128:
                align = 64
            esz = 4 if dt in (F32, I32) else 2
            off[0] = (off[0] + align - 1) // align * align
            base = off[0]
            off[0] += cols * esz
            assert off[0] <= AB, f"arena overflow {off[0]}"
            h = hf if dt is F32 else hi if dt is I32 else hb
            return TV(h, WF if esz == 4 else WB, base // esz, cols)

        g_ident = alloc(BF16, 128)
        g_ones = alloc(BF16, 128)
        g_zer = alloc(F32, 2048)
        g_stage = [alloc(BF16, 4096) for _ in range(3)]
        g_evac = [alloc(BF16, 1024) for _ in range(2)]
        g_idx = [alloc(I32, 512) for _ in range(3)]
        g_s1 = alloc(F32, 1)
        g_s2 = alloc(F32, 1)
        g_mu = alloc(F32, 1)
        g_var = alloc(F32, 1)
        g_mu2 = alloc(F32, 1)
        g_rstd = alloc(F32, 1)
        g_nmr = alloc(F32, 1)
        g_nmx = alloc(F32, 1)
        g_srow = alloc(F32, 1)
        g_rec = alloc(F32, 1)
        g_diag = alloc(BF16, 128)
        g_eps = alloc(F32, 1)
        glob_end = off[0]

        off[0] = glob_end
        a0_x = [alloc(F32, 4096), alloc(F32, 4096)]
        a0_xn = alloc(BF16, 4096)
        a0_ev = [alloc(BF16, 4096), alloc(BF16, 4096)]
        a0_sq = alloc(F32, 4096)
        a0_tabf = alloc(F32, 4096)
        a0_tabb = alloc(BF16, 4096)
        off[0] = glob_end
        a1_xnb = alloc(BF16, HC * SEQ)
        a1_wf = alloc(F32, 2048)
        a1_wb = alloc(BF16, HC * 128)
        a1_out = alloc(BF16, SEQ)
        off[0] = glob_end
        b_a = alloc(BF16, EH * SEQ)
        b_pt = alloc(BF16, 16 * 128)
        b_q = [alloc(BF16, SEQ), alloc(BF16, SEQ)]
        b_k = [alloc(BF16, SEQ), alloc(BF16, SEQ)]
        b_v = [alloc(BF16, SEQ), alloc(BF16, SEQ)]
        b_vtm = alloc(BF16, SEQ)
        b_p = alloc(BF16, SEQ)
        b_end = off[0]
        off[0] = (b_a.c0 + b_a.cols) * 2
        b2_wf = alloc(F32, EH * 512)
        b2_wb = alloc(BF16, EH * 512)
        b2_ev = alloc(F32, 512)
        off[0] = glob_end
        c_wdT = alloc(BF16, NIC * 4096)
        c_xn = alloc(BF16, HC * 512)
        _cw = alloc(BF16, HC * 128)
        _cu = alloc(BF16, HC * 128)
        c_wg = [_cw, _cw]
        c_wu = [_cu, _cu]
        c_sg = alloc(BF16, 512)
        c_hh = alloc(BF16, 512)
        c_gu_end = off[0]
        off[0] = c_wdT.c0 * 2 + c_wdT.cols * 2
        c_ht = [alloc(BF16, NIC * 128), alloc(BF16, NIC * 128)]
        c_rs = [alloc(F32, 4096), alloc(F32, 4096)]
        c_ev = alloc(F32, 2048)
        off[0] = (c_wdT.c0 + c_wdT.cols) * 2
        d_rs = alloc(F32, 4096)
        d_x = alloc(F32, 4096)
        d_o = alloc(F32, 4096)
        assert off[0] <= AB, off[0]

        ps_main = E(nc.psum_tensor("ps_main", [128, 3072], F32))
        ps_stg = E(nc.psum_tensor("ps_stg", [128, 2048], BF16))
        PS_W, PSB_W = 3072, 2048

        def ps(bank, cols, f0=0):
            return bass.AP(ps_main, bank * 512 + f0, [[PS_W, 128], [1, cols]])

        def psb(half, cols, f0=0):
            return bass.AP(ps_stg, half * 1024 + f0, [[PSB_W, 128], [1, cols]])

        sems = []

        def S(name, step=1):
            s = SemC(name, step)
            sems.append(s)
            return s

        m_tab = S("m_tab", 16)
        m_idx = [S("m_idx0", 16), S("m_idx1", 16)]
        m_g = S("m_g", 16)
        m_ck = S("m_ck", 1)
        m_x = [S("m_x0", 16), S("m_x1", 16)]
        m_xw = [S("m_xw0", 16), S("m_xw1", 16)]
        m_xnb = S("m_xnb", 16)
        m_w = S("m_w", 16)
        m_qw = S("m_qw", 16)
        m_qr = [S("m_qr0", 16), S("m_qr1", 16)]
        m_wo = S("m_wo", 16)
        m_z = S("m_z", 16)
        m_pw = S("m_pw", 16)
        m_cp = S("m_cp", 16)
        m_sw = [S("m_sw0", 16), S("m_sw1", 16)]
        m_cxn = S("m_cxn", 16)
        m_cw = S("m_cw", 16)
        m_dh = [S("m_dh0", 16), S("m_dh1", 16)]
        m_hw = S("m_hw", 16)
        m_rr = [S("m_rr0", 16), S("m_rr1", 16)]
        m_rw = S("m_rw", 16)
        m_rwc = [S(f"m_rwc{j}", 16) for j in range(4)]
        m_rs = S("m_rs", 1)
        m_do = S("m_do", 16)
        m_out = S("m_out", 16)
        e_pe = S("e_pe", 1)
        e_dve = S("e_dve", 1)
        e_act = S("e_act", 1)
        e_gp = S("e_gp", 1)

        M = Marks()

        pe_positions = []
        for t in range(TT64):
            pe_positions.append(("a0", t, 14.0))
        for u in range(3 * EH):
            pe_positions.append(("a1", u, 72.0))
        for h in range(EH):
            pe_positions.append(("b", h, 240.0))
        for og in range(8):
            pe_positions.append(("b2", og, 140.0))
        for lin in range(16 * NIC):
            pe_positions.append(("cg", lin, 15.0))
        g_done = []
        acc = 250.0
        for (w, ic) in UNITS:
            acc += (512 if w != "d" else 176) * 1.12 + 30.0
            g_done.append(acc)
        ASSIGN = {}
        ptime = 0.0
        k = 0
        NGU = len(GU_UNITS)
        for (ph, idx, dt) in pe_positions:
            if ph == "cg":
                break
            ptime += dt
            while k < NGU:
                if ptime < g_done[k] or (ph == "b2" and idx == 7):
                    break
                ASSIGN.setdefault((ph, idx), []).append(k)
                k += 1
        PRE_CG = list(range(k, NGU))

        psb_pe = [0]
        psb_dve = [0]

        def psb_begin(te):
            i = psb_pe[0]
            if i >= 2:
                M.wait(te, f"psbe{i - 2}")
            psb_pe[0] += 1
            return i % 2

        def psb_evac_mark():
            i = psb_dve[0]
            M.set(f"psbe{i}", e_dve)
            psb_dve[0] += 1

        def wait_self(dry, eng, sem):
            if not dry:
                eng.wait_ge(sem.h, sem.n)

        def stage_pe(te, k):
            w, ic = UNITS[k]
            if w == "d":
                M.wait(te, f"wd_pad{k}")
                for grp in range(2):
                    n = 8 if grp == 0 else 3
                    half = psb_begin(te)
                    for blk in range(n):
                        icc = grp * 8 + blk
                        ii = te.transpose(psb(half, 128, f0=blk * 128),
                                          g_stage[k % 3].ap(f0=icc * 128, fn=128),
                                          g_ident[:, :])
                    e_pe.inc(ii)
                    M.set(f"st_tr{k}_{grp}", e_pe)
            else:
                M.wait(te, f"gath{k}")
                for grp in range(4):
                    half = psb_begin(te)
                    for blk in range(8):
                        hcb = grp * 8 + blk
                        ii = te.transpose(psb(half, 128, f0=blk * 128),
                                          g_stage[k % 3].ap(f0=hcb * 128, fn=128),
                                          g_ident[:, :])
                    e_pe.inc(ii)
                    M.set(f"st_tr{k}_{grp}", e_pe)

        def stage_dve(ve, dry, k):
            e = e_dve
            w, ic = UNITS[k]
            if w == "d":
                M.wait(ve, f"gath{k}")
                e.inc(ve.memset(g_stage[k % 3].ap(f0=1376, fn=32), 0.0))
                M.set(f"wd_pad{k}", e)
                for grp in range(2):
                    n = 8 if grp == 0 else 3
                    M.wait(ve, f"st_tr{k}_{grp}")
                    e.inc(ve.tensor_copy(
                        bass.AP(c_wdT.h, c_wdT.c0 + (grp * 8) * 4096 + ic * 128,
                                [[c_wdT.W, 128], [4096, n], [1, 128]]),
                        psb(psb_dve[0] % 2, n * 128)))
                    psb_evac_mark()
                M.set(f"wd_ev{k}", e)
            else:
                dstt = wgT if w == "g" else wuT
                for grp in range(4):
                    j = k * 4 + grp
                    M.wait(ve, f"st_tr{k}_{grp}")
                    if j >= 2:
                        pl = j - 2
                        M.wait(ve, f"st_wr{pl // 4}_{pl % 4}")
                    e.inc(ve.tensor_copy(g_evac[grp % 2][:, :],
                                         psb(psb_dve[0] % 2, 1024)))
                    psb_evac_mark()
                    M.set(f"st_ev{k}_{grp}", e)
            M.set(f"stage_done{k}", e)

        def stage_act(sc, k):
            w, ic = UNITS[k]
            if w == "d":
                return
            dstt = wgT if w == "g" else wuT
            for grp in range(4):
                j = k * 4 + grp
                M.wait(sc, f"st_ev{k}_{grp}")
                m_sw[j % 2].inc(sc.dma_start(
                    out=bass.AP(dstt, (ic * HC + grp * 8) * 16384,
                                [[128, 128], [16384, 8], [1, 128]]),
                    in_=g_evac[grp % 2].ap(extra=[[128, 8], [1, 128]])))
                M.set(f"st_wr{k}_{grp}", m_sw[j % 2])

        def stage_at(eng, dry, which, ph, idx):
            for k in ASSIGN.get((ph, idx), []):
                if which == "pe":
                    stage_pe(eng, k)
                elif which == "act":
                    stage_act(eng, k)
                else:
                    stage_dve(eng, dry, k)

        def body_sync(sy, dry):
            m_tab.inc(sy.dma_start(
                out=a0_tabf[:, :],
                in_=cbt.ap().rearrange("(p a) b -> p (a b)", p=128)))
            M.set("cb_loaded", m_tab)
            M.wait(sy, "cb_cast")
            m_tab.inc(sy.dma_start(
                out=tab.ap().rearrange("(p a) b -> p (a b)", p=128),
                in_=a0_tabb[:, :]))
            M.set("tab_ready", m_tab)
            M.wait(sy, "zer_ready")
            for r in range(TT64):
                for j in range(4):
                    m_z.inc(sy.dma_start(
                        out=bass.AP(rsin[j], r * 128 * 1024, [[1024, 128], [1, 1024]]),
                        in_=g_zer[:, 0:1024]))
            M.set("zero_done", m_z)
            for t in range(2):
                m_x[t % 2].inc(sy.dma_start(
                    out=a0_x[t][:, :],
                    in_=bass.AP(xfull, t * 128 * DM, [[DM, 128], [1, DM]])))
                M.set(f"a0_x{t}", m_x[t % 2])
            for t in range(TT64):
                if t + 2 < TT64:
                    M.wait(sy, f"a0_done{t}")
                    m_x[t % 2].inc(sy.dma_start(
                        out=a0_x[t % 2][:, :],
                        in_=bass.AP(xfull, (t + 2) * 128 * DM, [[DM, 128], [1, DM]])))
                    M.set(f"a0_x{t + 2}", m_x[t % 2])
                M.wait(sy, f"a0_ev{t}")
                for q4 in range(4):
                    m_xw[t % 2].inc(sy.dma_start(
                        out=bass.AP(xnd, (q4 * 8) * TOK + t * 128,
                                    [[HC * TOK, 128], [TOK, 8], [1, 128]]),
                        in_=a0_ev[t % 2].ap(f0=q4 * 1024, extra=[[128, 8], [1, 128]])))
                M.set(f"a0_wr{t}", m_xw[t % 2])
            M.set("xn_all0", m_xw[0])
            M.set("xn_all1", m_xw[1])
            M.wait(sy, "xn_all0")
            M.wait(sy, "xn_all1")
            if dry:
                m_xnb.inc(None, 4)
            else:
                with sy.register("rb") as rb:
                    pid = sy.partition_id()
                    sy.reg_alu(rb, pid, 1, ALU.logical_shift_right)
                    sy.reg_alu(rb, rb, SEQ, ALU.mult)
                    with sy.register("rb2") as rb2:
                        for q4 in range(4):
                            sy.reg_alu(rb2, rb, (q4 * 8) * TOK, ALU.add)
                            m_xnb.inc(sy.dma_start(
                                out=a1_xnb.ap(f0=q4 * 8 * SEQ,
                                              extra=[[SEQ, 8], [1, SEQ]]),
                                in_=bass.AP(xnd, rb2,
                                            [[HC * TOK, 128], [TOK, 8], [1, SEQ]])))
            M.set("xnb", m_xnb)
            for oi in range(3):
                wsrc = (wq, wk, wv)[oi]
                for ec in range(EH):
                    for half in range(2):
                        wl = (oi * EH + ec) * 2 + half
                        if wl > 0:
                            M.wait(sy, f"a1_cast{wl - 1}")
                        m_w.inc(sy.dma_start(
                            out=a1_wf[:, :],
                            in_=bass.AP(wsrc, (ec * HC + half * 16) * 16384,
                                        [[128, 128], [16384, 16], [1, 128]])))
                        M.set(f"a1_w{wl}", m_w)
                    M.wait(sy, f"a1_out{oi}_{ec}")
                    m_qw.inc(sy.dma_start(
                        out=bass.AP((qd, kd, vd)[oi], ec * 128 * SEQ,
                                    [[SEQ, 128], [1, SEQ]]),
                        in_=a1_out[:, :]))
                    M.set(f"qkv_st{oi}_{ec}", m_qw)
            M.set("qkv_all", m_qw)
            M.wait(sy, "qkv_all")
            for h in range(EH):
                if h >= 2:
                    M.wait(sy, f"b_done{h - 2}")
                for oi, src in enumerate((qd, kd, vd)):
                    m_qr[h % 2].inc(sy.dma_start(
                        out=(b_q, b_k, b_v)[oi][h % 2][:, :],
                        in_=bass.AP(src, h * 128 * SEQ, [[SEQ, 128], [1, SEQ]])))
                M.set(f"b_qkv{h}", m_qr[h % 2])
            M.wait(sy, f"b_done{EH - 1}")
            for og in range(8):
                if og >= 1:
                    M.wait(sy, f"b2_cast{og - 1}")
                m_wo.inc(sy.dma_start(
                    out=b2_wf[:, :],
                    in_=bass.AP(wo, og * EH * 65536,
                                [[512, 128], [65536, EH], [1, 512]])))
                M.set(f"wo{og}", m_wo)
                for tg in range(4):
                    for tt in range(4):
                        M.wait(sy, f"b2_ev{og}_{tg}_{tt}")
                        m_pw.inc(sy.dma_start(
                            out=bass.AP(attnd, (tg * 4 + tt) * 128 * DM + og * 512,
                                        [[DM, 128], [1, 512]]),
                            in_=b2_ev[:, :]))
                        M.set(f"b2_pw{og}_{tg}_{tt}", m_pw)
            if not dry:
                sy.wait_ge(m_pw.h, m_pw.n)
            M.wait(sy, "zero_done")
            if dry:
                m_cp.inc(None, 4)
            else:
                with sy.register("ro") as ro:
                    pid2 = sy.partition_id()
                    sy.reg_alu(ro, pid2, 1, ALU.logical_shift_right)
                    sy.reg_alu(ro, ro, SEQ * 1024, ALU.mult)
                    for j in range(4):
                        m_cp.inc(sy.dma_start(
                            out=bass.AP(rsin[j], ro, [[1024, SEQ], [1, 1024]]),
                            in_=bass.AP(attnd, j * 1024, [[DM, SEQ], [1, 1024]])))
            M.set("part_done", m_cp)
            if DBG:
                m_cp.inc(sy.dma_start(out=dbg_a.ap(), in_=b_a[:, :]))
                if not dry:
                    sy.wait_ge(m_cp.h, m_cp.n)
            M.wait(sy, "gu_strips0")
            M.wait(sy, "gu_strips1")
            for tsb in range(16):
                if tsb >= 1:
                    M.wait(sy, f"c_umm{tsb - 1}_{NIC - 1}")
                for q4 in range(4):
                    m_cxn.inc(sy.dma_start(
                        out=c_xn.ap(f0=q4 * 8 * 512, extra=[[512, 8], [1, 512]]),
                        in_=bass.AP(xnd, (q4 * 8) * TOK + tsb * 512,
                                    [[HC * TOK, 128], [TOK, 8], [1, 512]])))
                M.set(f"c_xn{tsb}", m_cxn)
                for ic in range(NIC):
                    if tsb * NIC + ic >= 1:
                        pl = tsb * NIC + ic - 1
                        M.wait(sy, f"c_hh{pl // NIC}_{pl % NIC}")
                    m_cw.inc(sy.dma_start(
                        out=c_wg[ic % 2][:, :],
                        in_=bass.AP(wgT, ic * HC * 16384,
                                    [[128, 128], [16384, HC], [1, 128]])))
                    m_cw.inc(sy.dma_start(
                        out=c_wu[ic % 2][:, :],
                        in_=bass.AP(wuT, ic * HC * 16384,
                                    [[128, 128], [16384, HC], [1, 128]])))
                    M.set(f"c_w{tsb}_{ic}", m_cw)
                    M.wait(sy, f"c_hhv{tsb}_{ic}")
                    m_hw.inc(sy.dma_start(
                        out=bass.AP(hhd, ic * 128 * TOK + tsb * 512,
                                    [[TOK, 128], [1, 512]]),
                        in_=c_hh[:, :]))
                    M.set(f"c_hh{tsb}_{ic}", m_hw)
            M.set("hh_all", m_hw)
            M.wait(sy, "hh_all")
            M.wait(sy, "part_done")
            M.wait(sy, "zero_done")
            for hb in range(4):
                for tt in range(TT64):
                    lin = hb * TT64 + tt
                    if lin >= 2:
                        pl = lin - 2
                        M.wait(sy, f"c_dev{pl // TT64}_{pl % TT64}")
                    m_dh[lin % 2].inc(sy.dma_start(
                        out=c_ht[lin % 2].ap(extra=[[128, NIC], [1, 128]]),
                        in_=bass.AP(hhd, tt * 128,
                                    [[TOK, 128], [128 * TOK, NIC], [1, 128]])))
                    M.set(f"c_dl{hb}_{tt}", m_dh[lin % 2])
                    m_rr[lin % 2].inc(sy.dma_start(
                        out=c_rs[lin % 2][:, 0:1024],
                        in_=bass.AP(rsin[hb], tt * 128 * 1024,
                                    [[1024, 128], [1, 1024]])))
                    M.set(f"c_dr{hb}_{tt}", m_rr[lin % 2])
                    M.wait(sy, f"c_dev{hb}_{tt}")
                    m_rwc[hb].inc(sy.dma_start(
                        out=bass.AP(rsin[hb], tt * 128 * 1024,
                                    [[1024, 128], [1, 1024]]),
                        in_=c_ev[:, 0:1024]))
                    M.set(f"c_dw{hb}_{tt}", m_rwc[hb])
                M.set(f"rs_blk{hb}", m_rwc[hb])
            if DBG:
                m_do.inc(sy.dma_start(out=dbg_xn.ap(), in_=bass.AP(xnd, 0, [[HC * TOK, 128], [1, 4096]])))
                m_do.inc(sy.dma_start(out=dbg_q.ap(), in_=bass.AP(qd, 0, [[SEQ, 128], [1, SEQ]])))
                m_do.inc(sy.dma_start(out=dbg_k.ap(), in_=bass.AP(kd, 0, [[SEQ, 128], [1, SEQ]])))
                m_do.inc(sy.dma_start(out=dbg_v.ap(), in_=bass.AP(vd, 0, [[SEQ, 128], [1, SEQ]])))
                m_do.inc(sy.dma_start(out=dbg_at.ap(), in_=bass.AP(attnd, 0, [[DM, 256], [1, DM]])))
                m_do.inc(sy.dma_start(out=dbg_wg.ap(), in_=bass.AP(wgT, 0, [[128, HC * 128], [1, 128]])))
                m_do.inc(sy.dma_start(out=dbg_hh.ap(), in_=bass.AP(hhd, 0, [[TOK, 128], [1, 2048]])))
                m_do.inc(sy.dma_start(out=dbg_rs.ap(), in_=bass.AP(rsin, 0, [[DM, 256], [1, DM]])))
                m_do.inc(sy.dma_start(out=dbg_tab.ap(), in_=bass.AP(tab, 0, [[CS, 1024], [1, CS]])))
            M.wait(sy, "rs_done")
            for tt in range(8):
                if tt >= 1:
                    M.wait(sy, f"d_o{tt - 1}")
                for j in range(4):
                    m_do.inc(sy.dma_start(
                        out=d_rs.ap(extra=[[1024, 1], [1, 1024]], f0=j * 1024)
                        if False else
                        bass.AP(d_rs.h, d_rs.c0 + j * 1024, [[d_rs.W, 128], [1, 1024]]),
                        in_=bass.AP(rsout[j], tt * 128 * 1024, [[1024, 128], [1, 1024]])))
                m_do.inc(sy.dma_start(
                    out=d_x[:, :],
                    in_=bass.AP(xres, tt * 128 * DM, [[DM, 128], [1, DM]])))
                M.set(f"d_ld{tt}", m_do)
                M.wait(sy, f"d_o{tt}")
                m_out.inc(sy.dma_start(
                    out=bass.AP(out, tt * 128 * DM, [[DM, 128], [1, DM]]),
                    in_=d_o[:, :]))
                M.set(f"d_st{tt}", m_out)
            if not dry:
                sy.wait_ge(m_out.h, m_out.n)

        def body_gpsimd(gp, dry):
            M.wait(gp, "ones_ready")
            e_gp.inc(gp.affine_select(g_ident[:, :], g_ones[:, :], [[1, 128]],
                                      ALU.is_equal, 0.0, base=0,
                                      channel_multiplier=-1))
            M.set("ident", e_gp)
            M.wait(gp, "tab_ready")
            for k in range(2):
                w, ic = UNITS[k]
                cols = 512 if w != "d" else 176
                src = {"g": idxg, "u": idxu, "d": idxd}[w]
                m_idx[k % 2].inc(gp.dma_start(
                    out=g_idx[k % 3][:, 0:cols],
                    in_=bass.AP(src, ic * 128 * cols, [[cols, 128], [1, cols]])))
                M.set(f"idx{k}", m_idx[k % 2])
            for k, (w, ic) in enumerate(UNITS):
                cols = 512 if w != "d" else 176
                M.wait(gp, f"idx{k}")
                if k >= 3:
                    M.wait(gp, f"stage_done{k - 3}")
                for j in range(cols):
                    m_g.inc(gp.indirect_dma_start(
                        g_stage[k % 3][:, j * CS:(j + 1) * CS],
                        None,
                        tab.ap(),
                        bass.IndirectOffsetOnAxis(g_idx[k % 3][:, j:j + 1], 0)))
                if not dry:
                    gp.wait_ge(m_g.h, m_g.n)
                if dry:
                    m_ck.inc(None)
                else:
                    m_ck.inc(gp.nop())
                M.set(f"gath{k}", m_ck)
                if k + 2 < len(UNITS):
                    w2, ic2 = UNITS[k + 2]
                    cols2 = 512 if w2 != "d" else 176
                    src2 = {"g": idxg, "u": idxu, "d": idxd}[w2]
                    m_idx[k % 2].inc(gp.dma_start(
                        out=g_idx[(k + 2) % 3][:, 0:cols2],
                        in_=bass.AP(src2, ic2 * 128 * cols2,
                                    [[cols2, 128], [1, cols2]])))
                    M.set(f"idx{k + 2}", m_idx[k % 2])
            for hb in range(4):
                M.wait(gp, f"rs_blk{hb}")
                m_rs.inc(gp.collective_compute(
                    "ReduceScatter", ALU.add, replica_groups=[list(range(NCORE))],
                    ins=[rsin[hb].ap().opt()],
                    outs=[rsout[hb].ap().opt()]))
            M.set("rs_done", m_rs)
            if not dry:
                gp.wait_ge(m_rs.h, m_rs.n)

        def body_vector(ve, dry):
            e = e_dve
            e.inc(ve.memset(g_zer[:, :], 0.0))
            M.set("zer_ready", e)
            e.inc(ve.memset(g_ones[:, :], 1.0))
            M.set("ones_ready", e)
            e.inc(ve.memset(g_eps[:, :], EPS))
            M.set("eps_ready", e)
            M.wait(ve, "cb_loaded")
            e.inc(ve.tensor_copy(a0_tabb[:, :], a0_tabf[:, :]))
            M.set("cb_cast", e)
            for t in range(TT64):
                xt = a0_x[t % 2]
                M.wait(ve, f"a0_x{t}")
                e.inc(ve.reduce_sum(g_s1[:, :], xt[:, :], axis=AX.X))
                M.wait(ve, f"a0_sq{t}")
                wait_self(dry, ve, e)
                e.inc(ve.tensor_scalar(g_mu[:, :], g_s1[:, :], 1.0 / DM, None,
                                       ALU.mult))
                e.inc(ve.tensor_scalar(g_var[:, :], g_s2[:, :], 1.0 / DM, None,
                                       ALU.mult))
                wait_self(dry, ve, e)
                e.inc(ve.tensor_tensor(g_mu2[:, :], g_mu[:, :], g_mu[:, :],
                                       ALU.mult))
                wait_self(dry, ve, e)
                e.inc(ve.tensor_tensor(g_var[:, :], g_var[:, :], g_mu2[:, :],
                                       ALU.subtract))
                M.set(f"a0_var{t}", e)
                M.wait(ve, f"a0_rstd{t}")
                e.inc(ve.scalar_tensor_tensor(g_nmr[:, :], g_mu[:, :], -1.0,
                                              g_rstd[:, :], ALU.mult, ALU.mult))
                wait_self(dry, ve, e)
                if t >= 1:
                    M.wait(ve, f"a0_trdone{t - 1}")
                e.inc(ve.tensor_scalar(a0_xn[:, :], xt[:, :], g_rstd[:, :],
                                       g_nmr[:, :], ALU.mult, ALU.add))
                M.set(f"a0_xn{t}", e)
                if t >= 2:
                    M.wait(ve, f"a0_wr{t - 2}")
                for grp in range(4):
                    M.wait(ve, f"a0_tr{t}_{grp}")
                    e.inc(ve.tensor_copy(a0_ev[t % 2].ap(f0=grp * 1024, fn=1024),
                                         psb(psb_dve[0] % 2, 1024)))
                    psb_evac_mark()
                M.set(f"a0_ev{t}", e)
                M.set(f"a0_done{t}", e)
                stage_at(ve, dry, "dve", "a0", t)
            for u in range(3 * EH):
                stage_at(ve, dry, "dve", "a1", u)
            for h in range(EH):
                for grp in range(2):
                    M.wait(ve, f"b_vtr{h}_{grp}")
                    e.inc(ve.tensor_copy(b_vtm.ap(f0=grp * 1024, fn=1024),
                                         psb(psb_dve[0] % 2, 1024)))
                    psb_evac_mark()
                M.set(f"b_vtm{h}", e)
                for qt in range(QT):
                    M.wait(ve, f"b_s{h}_{qt}")
                    e.inc(ve.reduce_max(g_nmx[:, :], ps(0, 2048), axis=AX.X,
                                        negate=True))
                    M.set(f"b_nmx{h}_{qt}", e)
                    M.wait(ve, f"b_exp{h}_{qt}")
                    e.inc(ve.reciprocal(g_rec[:, :], g_srow[:, :]))
                    wait_self(dry, ve, e)
                    e.inc(ve.tensor_scalar(g_diag[:, :], g_ident[:, :],
                                           g_rec[:, :], None, ALU.mult))
                    M.set(f"b_diag{h}_{qt}", e)
                    for pb in range(4):
                        M.wait(ve, f"b_pt{h}_{qt}_{pb}")
                        e.inc(ve.tensor_copy(b_pt.ap(f0=pb * 512, fn=512),
                                             ps(4, 512)))
                        M.set(f"b_ptev{h}_{qt}_{pb}", e)
                if True:
                    stage_at(ve, dry, "dve", "b", h)
            for og in range(8):
                for tg in range(4):
                    for tt in range(4):
                        M.wait(ve, f"b2_mm{og}_{tg}_{tt}")
                        li = (og * 4 + tg) * 4 + tt
                        if li >= 1:
                            pl = li - 1
                            M.wait(ve, f"b2_pw{pl // 16}_{pl % 16 // 4}_{pl % 4}")
                        e.inc(ve.tensor_copy(b2_ev[:, :], ps(tt, 512)))
                        M.set(f"b2_ev{og}_{tg}_{tt}", e)
                stage_at(ve, dry, "dve", "b2", og)
            for k in PRE_CG:
                stage_dve(ve, dry, k)
            for tsb in range(16):
                for ic in range(NIC):
                    bk = (ic % 2) * 2
                    M.wait(ve, f"c_sg{tsb}_{ic}")
                    M.wait(ve, f"c_umm{tsb}_{ic}")
                    if tsb + ic > 0:
                        pl = tsb * NIC + ic - 1
                        M.wait(ve, f"c_hh{pl // NIC}_{pl % NIC}")
                    e.inc(ve.tensor_tensor(c_hh[:, :], c_sg[:, :],
                                           ps(bk + 1, 512), ALU.mult))
                    M.set(f"c_hhv{tsb}_{ic}", e)
                    stage_at(ve, dry, "dve", "cg", tsb * NIC + ic)
            for hb in range(4):
                for j in range(8):
                    stage_dve(ve, dry, len(GU_UNITS) + hb * 8 + j)
                for tt in range(TT64):
                    lin = hb * TT64 + tt
                    M.wait(ve, f"c_dmm{hb}_{tt}")
                    M.wait(ve, f"c_dr{hb}_{tt}")
                    if lin > 0:
                        pl = lin - 1
                        M.wait(ve, f"c_dw{pl // TT64}_{pl % TT64}")
                    e.inc(ve.tensor_tensor(c_ev[:, 0:1024],
                                           ps(2 * (lin % 2), 1024),
                                           c_rs[lin % 2].ap(f0=0, fn=1024),
                                           ALU.add))
                    M.set(f"c_dev{hb}_{tt}", e)
            for tt in range(8):
                M.wait(ve, f"d_ld{tt}")
                if tt >= 1:
                    M.wait(ve, f"d_st{tt - 1}")
                e.inc(ve.tensor_tensor(d_o[:, :], d_rs[:, :], d_x[:, :], ALU.add))
                M.set(f"d_o{tt}", e)

        def body_scalar(sc, dry):
            e = e_act
            for t in range(TT64):
                xt = a0_x[t % 2]
                M.wait(sc, f"a0_x{t}")
                if t >= 1:
                    M.wait(sc, f"a0_var{t - 1}")
                e.inc(sc.activation(a0_sq[:, :], xt[:, :], AF.Square,
                                    accum_out=g_s2[:, :]))
                M.set(f"a0_sq{t}", e)
                M.wait(sc, f"a0_var{t}")
                if t == 0:
                    M.wait(sc, "eps_ready")
                e.inc(sc.activation(g_rstd[:, :], g_var[:, :], AF.Ln,
                                    bias=g_eps[:, :], scale=1.0))
                wait_self(dry, sc, e)
                e.inc(sc.activation(g_rstd[:, :], g_rstd[:, :], AF.Exp,
                                    scale=-0.5))
                M.set(f"a0_rstd{t}", e)
                stage_at(sc, dry, "act", "a0", t)
            for oi in range(3):
                for ec in range(EH):
                    for half in range(2):
                        wl = (oi * EH + ec) * 2 + half
                        M.wait(sc, f"a1_w{wl}")
                        if wl >= 1:
                            M.wait(sc, f"a1_mmwb{wl - 1}")
                        e.inc(sc.copy(a1_wb.ap(f0=half * 2048, fn=2048),
                                      a1_wf[:, :]))
                        M.set(f"a1_cast{wl}", e)
                    M.wait(sc, f"a1_mm{oi}_{ec}")
                    if oi + ec > 0:
                        pl = oi * EH + ec - 1
                        M.wait(sc, f"qkv_st{pl // EH}_{pl % EH}")
                    e.inc(sc.activation(a1_out[:, :], ps(0, 2048), AF.Copy,
                                        scale=SCALE if oi == 0 else 1.0))
                    M.set(f"a1_out{oi}_{ec}", e)
                    stage_at(sc, dry, "act", "a1", oi * EH + ec)
            for h in range(EH):
                for qt in range(QT):
                    M.wait(sc, f"b_nmx{h}_{qt}")
                    e.inc(sc.activation(b_p[:, :], ps(0, 2048), AF.Exp,
                                        bias=g_nmx[:, :], scale=1.0,
                                        accum_out=g_srow[:, :]))
                    M.set(f"b_exp{h}_{qt}", e)
                    M.wait(sc, f"b_av{h}_{qt}")
                    e.inc(sc.copy(b_a.ap(f0=h * SEQ + qt * 128, fn=128),
                                  ps(5, 128)))
                    M.set(f"b_avev{h}_{qt}", e)
                    if qt == QT - 1:
                        stage_at(sc, dry, "act", "b", h)
            for og in range(8):
                M.wait(sc, f"wo{og}")
                if og >= 1:
                    M.wait(sc, f"b2_lastmm{og - 1}")
                for q in range(4):
                    e.inc(sc.copy(b2_wb.ap(f0=q * 2048, fn=2048),
                                  b2_wf.ap(f0=q * 2048, fn=2048)))
                M.set(f"b2_cast{og}", e)
                stage_at(sc, dry, "act", "b2", og)
            for k in PRE_CG:
                stage_act(sc, k)
            M.set("gu_strips0", m_sw[0])
            M.set("gu_strips1", m_sw[1])
            for tsb in range(16):
                for ic in range(NIC):
                    bk = (ic % 2) * 2
                    M.wait(sc, f"c_gmm{tsb}_{ic}")
                    e.inc(sc.activation(c_sg[:, :], ps(bk, 512), AF.Silu))
                    M.set(f"c_sg{tsb}_{ic}", e)
                    stage_at(sc, dry, "act", "cg", tsb * NIC + ic)

        def body_tensor(te, dry):
            e = e_pe
            M.wait(te, "ident")
            for t in range(TT64):
                M.wait(te, f"a0_xn{t}")
                for grp in range(4):
                    half = psb_begin(te)
                    for blk in range(8):
                        hcb = grp * 8 + blk
                        ii = te.transpose(psb(half, 128, f0=blk * 128),
                                          a0_xn.ap(f0=hcb * 128, fn=128),
                                          g_ident[:, :])
                    e.inc(ii)
                    M.set(f"a0_tr{t}_{grp}", e)
                M.set(f"a0_trdone{t}", e)
                stage_at(te, dry, "pe", "a0", t)
            M.wait(te, "xnb")
            for oi in range(3):
                for ec in range(EH):
                    if oi + ec > 0:
                        pl = oi * EH + ec - 1
                        M.wait(te, f"a1_out{pl // EH}_{pl % EH}")
                    for half in range(2):
                        wl = (oi * EH + ec) * 2 + half
                        M.wait(te, f"a1_cast{wl}")
                        for hcb in range(half * 16, half * 16 + 16):
                            for nn in range(4):
                                ii = te.matmul(
                                    ps(nn, 512),
                                    a1_wb.ap(f0=hcb * 128, fn=128),
                                    a1_xnb.ap(f0=hcb * SEQ + nn * 512, fn=512),
                                    start=(hcb == 0), stop=(hcb == HC - 1))
                        e.inc(ii)
                        M.set(f"a1_mmwb{wl}", e)
                    M.set(f"a1_mm{oi}_{ec}", e)
                    stage_at(te, dry, "pe", "a1", oi * EH + ec)
            for h in range(EH):
                M.wait(te, f"b_qkv{h}")
                for grp in range(2):
                    half = psb_begin(te)
                    for blk in range(8):
                        kc = grp * 8 + blk
                        ii = te.transpose(psb(half, 128, f0=blk * 128),
                                          b_v[h % 2].ap(f0=kc * 128, fn=128),
                                          g_ident[:, :])
                    e.inc(ii)
                    M.set(f"b_vtr{h}_{grp}", e)
                M.wait(te, f"b_vtm{h}")
                for qt in range(QT):
                    if h + qt > 0:
                        pq = (h, qt - 1) if qt else (h - 1, QT - 1)
                        M.wait(te, f"b_exp{pq[0]}_{pq[1]}")
                    else:
                        M.wait(te, f"a1_out{2}_{EH - 1}")
                    for nn in range(4):
                        ii = te.matmul(ps(nn, 512),
                                       b_q[h % 2].ap(f0=qt * 128, fn=128),
                                       b_k[h % 2].ap(f0=nn * 512, fn=512),
                                       start=True, stop=True)
                    e.inc(ii)
                    M.set(f"b_s{h}_{qt}", e)
                    M.wait(te, f"b_diag{h}_{qt}")
                    for pb in range(4):
                        if pb >= 1:
                            M.wait(te, f"b_ptev{h}_{qt}_{pb - 1}")
                        for u in range(4):
                            kc = pb * 4 + u
                            ii = te.matmul(ps(4, 128, f0=u * 128),
                                           b_p.ap(f0=kc * 128, fn=128),
                                           g_diag[:, :], start=True, stop=True)
                        e.inc(ii)
                        M.set(f"b_pt{h}_{qt}_{pb}", e)
                    M.wait(te, f"b_ptev{h}_{qt}_3")
                    if h + qt > 0:
                        pq = (h, qt - 1) if qt else (h - 1, QT - 1)
                        M.wait(te, f"b_avev{pq[0]}_{pq[1]}")
                    for kc in range(16):
                        ii = te.matmul(ps(5, 128),
                                       b_vtm.ap(f0=kc * 128, fn=128),
                                       b_pt.ap(f0=kc * 128, fn=128),
                                       start=(kc == 0), stop=(kc == 15))
                    e.inc(ii)
                    M.set(f"b_av{h}_{qt}", e)
                M.set(f"b_done{h}", e)
                stage_at(te, dry, "pe", "b", h)
            for og in range(8):
                M.wait(te, f"b2_cast{og}")
                if og == 0:
                    M.wait(te, f"b_exp{EH - 1}_{QT - 1}")
                for tg in range(4):
                    for tt in range(4):
                        if og + tg > 0:
                            ptg = (og, tg - 1) if tg else (og - 1, 3)
                            M.wait(te, f"b2_ev{ptg[0]}_{ptg[1]}_{tt}")
                        for ec in range(EH):
                            ii = te.matmul(
                                ps(tt, 512),
                                b_a.ap(f0=ec * SEQ + (tg * 4 + tt) * 128, fn=128),
                                b2_wb.ap(f0=ec * 512, fn=512),
                                start=(ec == 0), stop=(ec == EH - 1))
                        e.inc(ii)
                        M.set(f"b2_mm{og}_{tg}_{tt}", e)
                M.set(f"b2_lastmm{og}", e)
                stage_at(te, dry, "pe", "b2", og)
            for k in PRE_CG:
                stage_pe(te, k)
            for tsb in range(16):
                M.wait(te, f"c_xn{tsb}")
                for ic in range(NIC):
                    M.wait(te, f"c_w{tsb}_{ic}")
                    bk = (ic % 2) * 2
                    if tsb * NIC + ic >= 2:
                        pl = tsb * NIC + ic - 2
                        M.wait(te, f"c_hhv{pl // NIC}_{pl % NIC}")
                    else:
                        M.wait(te, "b2_ev7_3_3")
                    for op in range(2):
                        wbuf = (c_wg if op == 0 else c_wu)[ic % 2]
                        for hcb in range(HC):
                            ii = te.matmul(ps(bk + op, 512),
                                           wbuf.ap(f0=hcb * 128, fn=128),
                                           c_xn.ap(f0=hcb * 512, fn=512),
                                           start=(hcb == 0), stop=(hcb == HC - 1))
                        e.inc(ii)
                        if op == 0:
                            M.set(f"c_gmm{tsb}_{ic}", e)
                    M.set(f"c_umm{tsb}_{ic}", e)
                    stage_at(te, dry, "pe", "cg", tsb * NIC + ic)
            M.wait(te, f"c_hhv{15}_{NIC - 1}")
            for hb in range(4):
                for j in range(8):
                    stage_pe(te, len(GU_UNITS) + hb * 8 + j)
                M.wait(te, f"wd_ev{len(GU_UNITS) + 8 * (hb + 1) - 1}")
                for tt in range(TT64):
                    lin = hb * TT64 + tt
                    M.wait(te, f"c_dl{hb}_{tt}")
                    if lin >= 2:
                        pl = lin - 2
                        M.wait(te, f"c_dev{pl // TT64}_{pl % TT64}")
                    for ic in range(NIC):
                        for nn in range(2):
                            ii = te.matmul(
                                ps(2 * (lin % 2) + nn, 512),
                                c_ht[lin % 2].ap(f0=ic * 128, fn=128),
                                bass.AP(c_wdT.h,
                                        c_wdT.c0 + ic * 4096 + hb * 1024 + nn * 512,
                                        [[c_wdT.W, 128], [1, 512]]),
                                start=(ic == 0), stop=(ic == NIC - 1))
                    e.inc(ii)
                    M.set(f"c_dmm{hb}_{tt}", e)

        M.dry = True
        dz = DryEng()
        psb_pe[0] = 0
        psb_dve[0] = 0
        for s in sems:
            s.n = 0
        body_vector(dz, True)
        body_scalar(dz, True)
        body_tensor(dz, True)
        body_gpsimd(dz, True)
        body_sync(dz, True)

        M.dry = False
        psb_pe[0] = 0
        psb_dve[0] = 0
        for s in sems:
            s.n = 0
            s.h = E(nc.semaphore(s.name))

        block = E(nc.Block())

        @block.vector
        def _(ve):
            body_vector(ve, False)

        @block.scalar
        def _(sc):
            body_scalar(sc, False)

        @block.tensor
        def _(te):
            body_tensor(te, False)

        @block.gpsimd
        def _(gp):
            body_gpsimd(gp, False)

        @block.sync
        def _(sy):
            body_sync(sy, False)

    return nc


def kernel(L0_attn_x, L0_attn_w_q, L0_attn_w_k, L0_attn_w_v, L0_attn_w_o,
           L0_ffn_codebook, L0_ffn_idx_gate, L0_ffn_idx_up, L0_ffn_idx_down):
    x = np.ascontiguousarray(np.asarray(L0_attn_x, dtype=np.float32)).reshape(TOK, DM)
    wq_ = np.asarray(L0_attn_w_q, np.float32)
    wk_ = np.asarray(L0_attn_w_k, np.float32)
    wv_ = np.asarray(L0_attn_w_v, np.float32)
    wo_ = np.asarray(L0_attn_w_o, np.float32)
    cb = np.ascontiguousarray(np.asarray(L0_ffn_codebook, np.float32))
    ig = np.asarray(L0_ffn_idx_gate, np.int32)
    iu = np.asarray(L0_ffn_idx_up, np.int32)
    idn = np.asarray(L0_ffn_idx_down, np.int32)

    def blk_w(w, s):
        wt = w.T[:, s * ES:(s + 1) * ES]
        return np.ascontiguousarray(
            wt.reshape(HC, 128, EH, 128).transpose(2, 0, 1, 3))

    def blk_wo(s):
        wt = wo_.T[s * ES:(s + 1) * ES, :]
        return np.ascontiguousarray(
            wt.reshape(EH, 128, 8, 512).transpose(2, 0, 1, 3))

    def shard_gu(idx, c):
        sh = idx[c * IRE:(c + 1) * IRE]
        sh = np.concatenate([sh, np.zeros((ISH - IRE, 512), np.int32)], 0)
        return np.ascontiguousarray(sh.reshape(NIC, 128, 512))

    def shard_d(c):
        sh = idn[:, c * (IRE // CS):(c + 1) * (IRE // CS)]
        sh = np.concatenate([sh, np.zeros((DM, (ISH - IRE) // CS), np.int32)], 1)
        return np.ascontiguousarray(sh.reshape(HC, 128, 176))

    in_maps = []
    for c in range(NCORE):
        s = c % 2
        in_maps.append({
            "xfull": x,
            "xres": np.ascontiguousarray(x[c * HALF:(c + 1) * HALF]),
            "wq": blk_w(wq_, s), "wk": blk_w(wk_, s), "wv": blk_w(wv_, s),
            "wo": blk_wo(s),
            "idxg": shard_gu(ig, c), "idxu": shard_gu(iu, c), "idxd": shard_d(c),
            "cbt": cb,
        })

    nc = build_program()
    res = run_bass_kernel_spmd(nc, in_maps, core_ids=list(range(NCORE)))
    outp = np.empty((TOK, DM), np.float32)
    for c in range(NCORE):
        outp[c * HALF:(c + 1) * HALF] = res.results[c]["o"]
    return outp.reshape(BATCH, SEQ, DM)


# version 11
# speedup vs baseline: 1.1052x; 1.1052x over previous
import os
import numpy as np
import concourse.bass as bass
import concourse.mybir as mybir
from concourse.bass_utils import run_bass_kernel_spmd
from contextlib import ExitStack

F32 = mybir.dt.float32
BF16 = mybir.dt.bfloat16
I32 = mybir.dt.int32
AF = mybir.ActivationFunctionType
ALU = mybir.AluOpType
AX = mybir.AxisListType

BATCH, SEQ, DM = 4, 2048, 4096
HEADS, DK = 32, 128
CB, CS = 65536, 8
TOK = BATCH * SEQ
NCORE = 8
EH, ES = 16, 2048
ISH, NIC, IRE = 1408, 11, 1376
TT64 = TOK // 128
QT = SEQ // 128
HC = DM // 128
HALF = SEQ // 2
SCALE = float(1.0 / np.sqrt(DK))
EPS = 1e-5

GU_UNITS = [(w, ic) for ic in range(NIC) for w in ("g", "u")]
WD_UNITS = [("d", hc) for hc in range(HC)]
UNITS = GU_UNITS + WD_UNITS


class SemC:
    def __init__(self, name, step=1):
        self.name, self.step, self.n, self.h = name, step, 0, None

    def inc(self, instr, k=1):
        if instr is not None:
            instr.then_inc(self.h, self.step * k)
        self.n += self.step * k
        return instr


class Marks:
    def __init__(self):
        self.d = {}
        self.dry = True

    def set(self, name, sem):
        if self.dry:
            self.d[name] = (sem, sem.n)
        else:
            assert name in self.d and self.d[name] == (sem, sem.n), \
                (name, self.d.get(name), sem.name, sem.n)

    def alias(self, name, src):
        if self.dry:
            self.d[name] = self.d[src]

    def wait(self, eng, name):
        if self.dry:
            return
        sem, v = self.d[name]
        if v > 0:
            eng.wait_ge(sem.h, v)


class DryInstr:
    def then_inc(self, *a):
        return self


class DryEng:
    def __getattr__(self, name):
        def f(*a, **k):
            return DryInstr()
        return f


def build_program():
    nc = bass.Bass("TRN2", target_bir_lowering=False)

    xfull = nc.dram_tensor("xfull", [TOK, DM], F32, kind="ExternalInput")
    xres = nc.dram_tensor("xres", [HALF, DM], F32, kind="ExternalInput")
    wq = nc.dram_tensor("wq", [EH, HC, 128, 128], F32, kind="ExternalInput")
    wk = nc.dram_tensor("wk", [EH, HC, 128, 128], F32, kind="ExternalInput")
    wv = nc.dram_tensor("wv", [EH, HC, 128, 128], F32, kind="ExternalInput")
    wo = nc.dram_tensor("wo", [8, EH, 128, 512], F32, kind="ExternalInput")
    idxg = nc.dram_tensor("idxg", [NIC, 128, 512], I32, kind="ExternalInput")
    idxu = nc.dram_tensor("idxu", [NIC, 128, 512], I32, kind="ExternalInput")
    idxd = nc.dram_tensor("idxd", [HC, 128, 176], I32, kind="ExternalInput")
    cbt = nc.dram_tensor("cbt", [CB, CS], F32, kind="ExternalInput")
    out = nc.dram_tensor("o", [HALF, DM], F32, kind="ExternalOutput")

    tab = nc.dram_tensor("tab", [CB, CS], BF16)
    xnd = nc.dram_tensor("xnd", [128, HC * TOK], BF16)
    qd = nc.dram_tensor("qd", [EH, 128, SEQ], BF16)
    kd = nc.dram_tensor("kd", [EH, 128, SEQ], BF16)
    vd = nc.dram_tensor("vd", [EH, 128, SEQ], BF16)
    wgT = nc.dram_tensor("wgT", [NIC, HC, 128, 128], BF16)
    wuT = nc.dram_tensor("wuT", [NIC, HC, 128, 128], BF16)
    hhd = nc.dram_tensor("hhd", [NIC, 128, TOK], BF16)
    attnd = nc.dram_tensor("attnd", [SEQ, DM], F32)
    rsin = [nc.dram_tensor(f"rsin{j}", [TOK, 1024], F32) for j in range(4)]
    rsout = [nc.dram_tensor(f"rsout{j}", [HALF, 1024], F32) for j in range(4)]
    DBG = bool(os.environ.get("MK_DEBUG"))
    if DBG:
        dbg_xn = nc.dram_tensor("dbg_xn", [128, 4096], BF16, kind="ExternalOutput")
        dbg_q = nc.dram_tensor("dbg_q", [128, SEQ], BF16, kind="ExternalOutput")
        dbg_k = nc.dram_tensor("dbg_k", [128, SEQ], BF16, kind="ExternalOutput")
        dbg_v = nc.dram_tensor("dbg_v", [128, SEQ], BF16, kind="ExternalOutput")
        dbg_at = nc.dram_tensor("dbg_at", [256, DM], F32, kind="ExternalOutput")
        dbg_wg = nc.dram_tensor("dbg_wg", [HC * 128, 128], BF16, kind="ExternalOutput")
        dbg_hh = nc.dram_tensor("dbg_hh", [128, 2048], BF16, kind="ExternalOutput")
        dbg_rs = nc.dram_tensor("dbg_rs", [256, DM], F32, kind="ExternalOutput")
        dbg_tab = nc.dram_tensor("dbg_tab", [1024, CS], BF16, kind="ExternalOutput")
        dbg_a = nc.dram_tensor("dbg_a", [128, EH * SEQ], BF16, kind="ExternalOutput")

    ctx = ExitStack()
    with ctx:
        E = ctx.enter_context
        AB = 204 * 1024
        arena = E(nc.sbuf_tensor("arena", [128, AB], mybir.dt.uint8))
        hf, hb, hi = arena.bitcast(F32), arena.bitcast(BF16), arena.bitcast(I32)
        WF, WB = AB // 4, AB // 2

        class TV:
            def __init__(self, h, W, c0, cols):
                self.h, self.W, self.c0, self.cols = h, W, c0, cols

            def ap(self, p0=0, pn=128, f0=0, fn=None, extra=None):
                fn = self.cols - f0 if fn is None else fn
                pat = [[self.W, pn]] + (extra if extra else [[1, fn]])
                return bass.AP(self.h, p0 * self.W + self.c0 + f0, pat)

            def __getitem__(self, key):
                ps_, fs = key
                p0 = ps_.start or 0
                pn = (128 if ps_.stop is None else ps_.stop) - p0
                f0 = fs.start or 0
                fn = (self.cols if fs.stop is None else fs.stop) - f0
                return self.ap(p0, pn, f0, fn)

        off = [0]

        def alloc(dt, cols, align=512):
            if cols <= 128:
                align = 64
            esz = 4 if dt in (F32, I32) else 2
            off[0] = (off[0] + align - 1) // align * align
            base = off[0]
            off[0] += cols * esz
            assert off[0] <= AB, f"arena overflow {off[0]}"
            h = hf if dt is F32 else hi if dt is I32 else hb
            return TV(h, WF if esz == 4 else WB, base // esz, cols)

        g_ident = alloc(BF16, 128)
        g_ones = alloc(BF16, 128)
        g_zer = alloc(F32, 2048)
        g_stage = [alloc(BF16, 4096) for _ in range(3)]
        g_evac = [alloc(BF16, 1024) for _ in range(2)]
        g_idx = [alloc(I32, 512) for _ in range(3)]
        g_s1 = alloc(F32, 1)
        g_s2 = alloc(F32, 1)
        g_mu = alloc(F32, 1)
        g_var = alloc(F32, 1)
        g_mu2 = alloc(F32, 1)
        g_rstd = alloc(F32, 1)
        g_nmr = alloc(F32, 1)
        g_nmx = alloc(F32, 1)
        g_srow = alloc(F32, 1)
        g_rec = alloc(F32, 1)
        g_diag = alloc(BF16, 128)
        g_eps = alloc(F32, 1)
        glob_end = off[0]

        off[0] = glob_end
        a0_x = [alloc(F32, 4096), alloc(F32, 4096)]
        a0_xn = alloc(BF16, 4096)
        a0_ev = [alloc(BF16, 4096), alloc(BF16, 4096)]
        a0_sq = alloc(F32, 4096)
        a0_tabf = alloc(F32, 4096)
        a0_tabb = alloc(BF16, 4096)
        off[0] = glob_end
        a1_xnb = alloc(BF16, HC * SEQ)
        a1_wf = alloc(F32, 2048)
        a1_wb = alloc(BF16, HC * 128)
        a1_out = alloc(BF16, SEQ)
        off[0] = glob_end
        b_a = alloc(BF16, EH * SEQ)
        b_pt = alloc(BF16, 16 * 128)
        b_q = [alloc(BF16, SEQ), alloc(BF16, SEQ)]
        b_k = [alloc(BF16, SEQ), alloc(BF16, SEQ)]
        b_v = [alloc(BF16, SEQ), alloc(BF16, SEQ)]
        b_vtm = alloc(BF16, SEQ)
        b_p = alloc(BF16, SEQ)
        b_end = off[0]
        off[0] = (b_a.c0 + b_a.cols) * 2
        b2_wf = alloc(F32, EH * 512)
        b2_wb = alloc(BF16, EH * 512)
        b2_ev = alloc(F32, 512)
        off[0] = glob_end
        c_wdT = alloc(BF16, NIC * 4096)
        c_xn = alloc(BF16, HC * 512)
        _cw = alloc(BF16, HC * 128)
        _cu = alloc(BF16, HC * 128)
        c_wg = [_cw, _cw]
        c_wu = [_cu, _cu]
        c_sg = alloc(BF16, 512)
        c_hh = alloc(BF16, 512)
        c_gu_end = off[0]
        off[0] = c_wdT.c0 * 2 + c_wdT.cols * 2
        c_ht = [alloc(BF16, NIC * 128), alloc(BF16, NIC * 128)]
        c_rs = [alloc(F32, 4096), alloc(F32, 4096)]
        c_ev = alloc(F32, 2048)
        off[0] = (c_wdT.c0 + c_wdT.cols) * 2
        d_rs = alloc(F32, 4096)
        d_x = alloc(F32, 4096)
        d_o = alloc(F32, 4096)
        assert off[0] <= AB, off[0]

        ps_main = E(nc.psum_tensor("ps_main", [128, 3072], F32))
        ps_stg = E(nc.psum_tensor("ps_stg", [128, 2048], BF16))
        PS_W, PSB_W = 3072, 2048

        def ps(bank, cols, f0=0):
            return bass.AP(ps_main, bank * 512 + f0, [[PS_W, 128], [1, cols]])

        def psb(half, cols, f0=0):
            return bass.AP(ps_stg, half * 1024 + f0, [[PSB_W, 128], [1, cols]])

        sems = []

        def S(name, step=1):
            s = SemC(name, step)
            sems.append(s)
            return s

        m_tab = S("m_tab", 16)
        m_idx = [S("m_idx0", 16), S("m_idx1", 16)]
        m_g = S("m_g", 16)
        m_ck = S("m_ck", 1)
        m_x = [S("m_x0", 16), S("m_x1", 16)]
        m_xw = [S("m_xw0", 16), S("m_xw1", 16)]
        m_xnb = S("m_xnb", 16)
        m_w = S("m_w", 16)
        m_qw = S("m_qw", 16)
        m_qr = [S("m_qr0", 16), S("m_qr1", 16)]
        m_wo = S("m_wo", 16)
        m_z = S("m_z", 16)
        m_pw = S("m_pw", 16)
        m_cp = S("m_cp", 16)
        m_sw = [S("m_sw0", 16), S("m_sw1", 16)]
        m_cxn = S("m_cxn", 16)
        m_cw = S("m_cw", 16)
        m_dh = [S("m_dh0", 16), S("m_dh1", 16)]
        m_hw = S("m_hw", 16)
        m_rr = [S("m_rr0", 16), S("m_rr1", 16)]
        m_rw = S("m_rw", 16)
        m_rwc = [S(f"m_rwc{j}", 16) for j in range(4)]
        m_rs = S("m_rs", 1)
        m_do = S("m_do", 16)
        m_out = S("m_out", 16)
        e_pe = S("e_pe", 1)
        e_dve = S("e_dve", 1)
        e_act = S("e_act", 1)
        e_gp = S("e_gp", 1)

        M = Marks()

        pe_positions = []
        for t in range(TT64):
            pe_positions.append(("a0", t, 14.0))
        for u in range(3 * EH):
            pe_positions.append(("a1", u, 72.0))
        for h in range(EH):
            pe_positions.append(("b", h, 240.0))
        for og in range(8):
            pe_positions.append(("b2", og, 140.0))
        for lin in range(16 * NIC):
            pe_positions.append(("cg", lin, 15.0))
        g_done = []
        acc = 250.0
        for (w, ic) in UNITS:
            acc += (512 if w != "d" else 176) * 1.12 + 30.0
            g_done.append(acc)
        ASSIGN = {}
        ptime = 0.0
        k = 0
        NGU = len(GU_UNITS)
        for (ph, idx, dt) in pe_positions:
            if ph == "cg":
                break
            ptime += dt
            while k < NGU:
                if ptime < g_done[k] or (ph == "b2" and idx == 7):
                    break
                ASSIGN.setdefault((ph, idx), []).append(k)
                k += 1
        PRE_CG = list(range(k, NGU))
        nwd = len(WD_UNITS)
        for j in range(nwd):
            pos = min(int((j + 1) * (16 * NIC) / (nwd + 1.0)), 16 * NIC - 1)
            ASSIGN.setdefault(("cg", pos), []).append(NGU + j)

        psb_pe = [0]
        psb_dve = [0]

        def psb_begin(te):
            i = psb_pe[0]
            if i >= 2:
                M.wait(te, f"psbe{i - 2}")
            psb_pe[0] += 1
            return i % 2

        def psb_evac_mark():
            i = psb_dve[0]
            M.set(f"psbe{i}", e_dve)
            psb_dve[0] += 1

        def wait_self(dry, eng, sem):
            if not dry:
                eng.wait_ge(sem.h, sem.n)

        def stage_pe(te, k):
            w, ic = UNITS[k]
            if w == "d":
                M.wait(te, f"wd_pad{k}")
                for grp in range(2):
                    n = 8 if grp == 0 else 3
                    half = psb_begin(te)
                    for blk in range(n):
                        icc = grp * 8 + blk
                        ii = te.transpose(psb(half, 128, f0=blk * 128),
                                          g_stage[k % 3].ap(f0=icc * 128, fn=128),
                                          g_ident[:, :])
                    e_pe.inc(ii)
                    M.set(f"st_tr{k}_{grp}", e_pe)
            else:
                M.wait(te, f"gath{k}")
                for grp in range(4):
                    half = psb_begin(te)
                    for blk in range(8):
                        hcb = grp * 8 + blk
                        ii = te.transpose(psb(half, 128, f0=blk * 128),
                                          g_stage[k % 3].ap(f0=hcb * 128, fn=128),
                                          g_ident[:, :])
                    e_pe.inc(ii)
                    M.set(f"st_tr{k}_{grp}", e_pe)

        def stage_dve(ve, dry, k):
            e = e_dve
            w, ic = UNITS[k]
            if w == "d":
                M.wait(ve, f"gath{k}")
                e.inc(ve.memset(g_stage[k % 3].ap(f0=1376, fn=32), 0.0))
                M.set(f"wd_pad{k}", e)
                for grp in range(2):
                    n = 8 if grp == 0 else 3
                    M.wait(ve, f"st_tr{k}_{grp}")
                    e.inc(ve.tensor_copy(
                        bass.AP(c_wdT.h, c_wdT.c0 + (grp * 8) * 4096 + ic * 128,
                                [[c_wdT.W, 128], [4096, n], [1, 128]]),
                        psb(psb_dve[0] % 2, n * 128)))
                    psb_evac_mark()
                M.set(f"wd_ev{k}", e)
            else:
                dstt = wgT if w == "g" else wuT
                for grp in range(4):
                    j = k * 4 + grp
                    M.wait(ve, f"st_tr{k}_{grp}")
                    if j >= 2:
                        pl = j - 2
                        M.wait(ve, f"st_wr{pl // 4}_{pl % 4}")
                    e.inc(ve.tensor_copy(g_evac[grp % 2][:, :],
                                         psb(psb_dve[0] % 2, 1024)))
                    psb_evac_mark()
                    M.set(f"st_ev{k}_{grp}", e)
            M.set(f"stage_done{k}", e)

        def stage_act(sc, k):
            w, ic = UNITS[k]
            if w == "d":
                return
            dstt = wgT if w == "g" else wuT
            for grp in range(4):
                j = k * 4 + grp
                M.wait(sc, f"st_ev{k}_{grp}")
                m_sw[j % 2].inc(sc.dma_start(
                    out=bass.AP(dstt, (ic * HC + grp * 8) * 16384,
                                [[128, 128], [16384, 8], [1, 128]]),
                    in_=g_evac[grp % 2].ap(extra=[[128, 8], [1, 128]])))
                M.set(f"st_wr{k}_{grp}", m_sw[j % 2])

        def stage_at(eng, dry, which, ph, idx):
            for k in ASSIGN.get((ph, idx), []):
                if which == "pe":
                    stage_pe(eng, k)
                elif which == "act":
                    stage_act(eng, k)
                else:
                    stage_dve(eng, dry, k)

        def body_sync(sy, dry):
            m_tab.inc(sy.dma_start(
                out=a0_tabf[:, :],
                in_=cbt.ap().rearrange("(p a) b -> p (a b)", p=128)))
            M.set("cb_loaded", m_tab)
            M.wait(sy, "cb_cast")
            m_tab.inc(sy.dma_start(
                out=tab.ap().rearrange("(p a) b -> p (a b)", p=128),
                in_=a0_tabb[:, :]))
            M.set("tab_ready", m_tab)
            M.wait(sy, "zer_ready")
            for r in range(TT64):
                for j in range(4):
                    m_z.inc(sy.dma_start(
                        out=bass.AP(rsin[j], r * 128 * 1024, [[1024, 128], [1, 1024]]),
                        in_=g_zer[:, 0:1024]))
            M.set("zero_done", m_z)
            for t in range(2):
                m_x[t % 2].inc(sy.dma_start(
                    out=a0_x[t][:, :],
                    in_=bass.AP(xfull, t * 128 * DM, [[DM, 128], [1, DM]])))
                M.set(f"a0_x{t}", m_x[t % 2])
            for t in range(TT64):
                if t + 2 < TT64:
                    M.wait(sy, f"a0_done{t}")
                    m_x[t % 2].inc(sy.dma_start(
                        out=a0_x[t % 2][:, :],
                        in_=bass.AP(xfull, (t + 2) * 128 * DM, [[DM, 128], [1, DM]])))
                    M.set(f"a0_x{t + 2}", m_x[t % 2])
                M.wait(sy, f"a0_ev{t}")
                for q4 in range(4):
                    m_xw[t % 2].inc(sy.dma_start(
                        out=bass.AP(xnd, (q4 * 8) * TOK + t * 128,
                                    [[HC * TOK, 128], [TOK, 8], [1, 128]]),
                        in_=a0_ev[t % 2].ap(f0=q4 * 1024, extra=[[128, 8], [1, 128]])))
                M.set(f"a0_wr{t}", m_xw[t % 2])
            M.set("xn_all0", m_xw[0])
            M.set("xn_all1", m_xw[1])
            M.wait(sy, "xn_all0")
            M.wait(sy, "xn_all1")
            if dry:
                m_xnb.inc(None, 4)
            else:
                with sy.register("rb") as rb:
                    pid = sy.partition_id()
                    sy.reg_alu(rb, pid, 1, ALU.logical_shift_right)
                    sy.reg_alu(rb, rb, SEQ, ALU.mult)
                    with sy.register("rb2") as rb2:
                        for q4 in range(4):
                            sy.reg_alu(rb2, rb, (q4 * 8) * TOK, ALU.add)
                            m_xnb.inc(sy.dma_start(
                                out=a1_xnb.ap(f0=q4 * 8 * SEQ,
                                              extra=[[SEQ, 8], [1, SEQ]]),
                                in_=bass.AP(xnd, rb2,
                                            [[HC * TOK, 128], [TOK, 8], [1, SEQ]])))
            M.set("xnb", m_xnb)
            for oi in range(3):
                wsrc = (wq, wk, wv)[oi]
                for ec in range(EH):
                    for half in range(2):
                        wl = (oi * EH + ec) * 2 + half
                        if wl > 0:
                            M.wait(sy, f"a1_cast{wl - 1}")
                        m_w.inc(sy.dma_start(
                            out=a1_wf[:, :],
                            in_=bass.AP(wsrc, (ec * HC + half * 16) * 16384,
                                        [[128, 128], [16384, 16], [1, 128]])))
                        M.set(f"a1_w{wl}", m_w)
                    M.wait(sy, f"a1_out{oi}_{ec}")
                    m_qw.inc(sy.dma_start(
                        out=bass.AP((qd, kd, vd)[oi], ec * 128 * SEQ,
                                    [[SEQ, 128], [1, SEQ]]),
                        in_=a1_out[:, :]))
                    M.set(f"qkv_st{oi}_{ec}", m_qw)
            M.set("qkv_all", m_qw)
            M.wait(sy, "qkv_all")
            for h in range(EH):
                if h >= 2:
                    M.wait(sy, f"b_done{h - 2}")
                for oi, src in enumerate((qd, kd, vd)):
                    m_qr[h % 2].inc(sy.dma_start(
                        out=(b_q, b_k, b_v)[oi][h % 2][:, :],
                        in_=bass.AP(src, h * 128 * SEQ, [[SEQ, 128], [1, SEQ]])))
                M.set(f"b_qkv{h}", m_qr[h % 2])
            M.wait(sy, f"b_done{EH - 1}")
            for og in range(8):
                if og >= 1:
                    M.wait(sy, f"b2_cast{og - 1}")
                m_wo.inc(sy.dma_start(
                    out=b2_wf[:, :],
                    in_=bass.AP(wo, og * EH * 65536,
                                [[512, 128], [65536, EH], [1, 512]])))
                M.set(f"wo{og}", m_wo)
                for tg in range(4):
                    for tt in range(4):
                        M.wait(sy, f"b2_ev{og}_{tg}_{tt}")
                        m_pw.inc(sy.dma_start(
                            out=bass.AP(attnd, (tg * 4 + tt) * 128 * DM + og * 512,
                                        [[DM, 128], [1, 512]]),
                            in_=b2_ev[:, :]))
                        M.set(f"b2_pw{og}_{tg}_{tt}", m_pw)
            if not dry:
                sy.wait_ge(m_pw.h, m_pw.n)
            M.wait(sy, "zero_done")
            if dry:
                m_cp.inc(None, 4)
            else:
                with sy.register("ro") as ro:
                    pid2 = sy.partition_id()
                    sy.reg_alu(ro, pid2, 1, ALU.logical_shift_right)
                    sy.reg_alu(ro, ro, SEQ * 1024, ALU.mult)
                    for j in range(4):
                        m_cp.inc(sy.dma_start(
                            out=bass.AP(rsin[j], ro, [[1024, SEQ], [1, 1024]]),
                            in_=bass.AP(attnd, j * 1024, [[DM, SEQ], [1, 1024]])))
            M.set("part_done", m_cp)
            if DBG:
                m_cp.inc(sy.dma_start(out=dbg_a.ap(), in_=b_a[:, :]))
                if not dry:
                    sy.wait_ge(m_cp.h, m_cp.n)
            M.wait(sy, "gu_strips0")
            M.wait(sy, "gu_strips1")
            for tsb in range(16):
                if tsb >= 1:
                    M.wait(sy, f"c_umm{tsb - 1}_{NIC - 1}")
                for q4 in range(4):
                    m_cxn.inc(sy.dma_start(
                        out=c_xn.ap(f0=q4 * 8 * 512, extra=[[512, 8], [1, 512]]),
                        in_=bass.AP(xnd, (q4 * 8) * TOK + tsb * 512,
                                    [[HC * TOK, 128], [TOK, 8], [1, 512]])))
                M.set(f"c_xn{tsb}", m_cxn)
                for ic in range(NIC):
                    if tsb * NIC + ic >= 1:
                        pl = tsb * NIC + ic - 1
                        M.wait(sy, f"c_hh{pl // NIC}_{pl % NIC}")
                    m_cw.inc(sy.dma_start(
                        out=c_wg[ic % 2][:, :],
                        in_=bass.AP(wgT, ic * HC * 16384,
                                    [[128, 128], [16384, HC], [1, 128]])))
                    m_cw.inc(sy.dma_start(
                        out=c_wu[ic % 2][:, :],
                        in_=bass.AP(wuT, ic * HC * 16384,
                                    [[128, 128], [16384, HC], [1, 128]])))
                    M.set(f"c_w{tsb}_{ic}", m_cw)
                    M.wait(sy, f"c_hhv{tsb}_{ic}")
                    m_hw.inc(sy.dma_start(
                        out=bass.AP(hhd, ic * 128 * TOK + tsb * 512,
                                    [[TOK, 128], [1, 512]]),
                        in_=c_hh[:, :]))
                    M.set(f"c_hh{tsb}_{ic}", m_hw)
            M.set("hh_all", m_hw)
            M.wait(sy, "hh_all")
            M.wait(sy, "part_done")
            M.wait(sy, "zero_done")
            for hb in range(4):
                for tt in range(TT64):
                    lin = hb * TT64 + tt
                    if lin >= 2:
                        pl = lin - 2
                        M.wait(sy, f"c_dev{pl // TT64}_{pl % TT64}")
                    m_dh[lin % 2].inc(sy.dma_start(
                        out=c_ht[lin % 2].ap(extra=[[128, NIC], [1, 128]]),
                        in_=bass.AP(hhd, tt * 128,
                                    [[TOK, 128], [128 * TOK, NIC], [1, 128]])))
                    M.set(f"c_dl{hb}_{tt}", m_dh[lin % 2])
                    m_rr[lin % 2].inc(sy.dma_start(
                        out=c_rs[lin % 2][:, 0:1024],
                        in_=bass.AP(rsin[hb], tt * 128 * 1024,
                                    [[1024, 128], [1, 1024]])))
                    M.set(f"c_dr{hb}_{tt}", m_rr[lin % 2])
                    M.wait(sy, f"c_dev{hb}_{tt}")
                    m_rwc[hb].inc(sy.dma_start(
                        out=bass.AP(rsin[hb], tt * 128 * 1024,
                                    [[1024, 128], [1, 1024]]),
                        in_=c_ev[:, 0:1024]))
                    M.set(f"c_dw{hb}_{tt}", m_rwc[hb])
                M.set(f"rs_blk{hb}", m_rwc[hb])
            if DBG:
                m_do.inc(sy.dma_start(out=dbg_xn.ap(), in_=bass.AP(xnd, 0, [[HC * TOK, 128], [1, 4096]])))
                m_do.inc(sy.dma_start(out=dbg_q.ap(), in_=bass.AP(qd, 0, [[SEQ, 128], [1, SEQ]])))
                m_do.inc(sy.dma_start(out=dbg_k.ap(), in_=bass.AP(kd, 0, [[SEQ, 128], [1, SEQ]])))
                m_do.inc(sy.dma_start(out=dbg_v.ap(), in_=bass.AP(vd, 0, [[SEQ, 128], [1, SEQ]])))
                m_do.inc(sy.dma_start(out=dbg_at.ap(), in_=bass.AP(attnd, 0, [[DM, 256], [1, DM]])))
                m_do.inc(sy.dma_start(out=dbg_wg.ap(), in_=bass.AP(wgT, 0, [[128, HC * 128], [1, 128]])))
                m_do.inc(sy.dma_start(out=dbg_hh.ap(), in_=bass.AP(hhd, 0, [[TOK, 128], [1, 2048]])))
                m_do.inc(sy.dma_start(out=dbg_rs.ap(), in_=bass.AP(rsin, 0, [[DM, 256], [1, DM]])))
                m_do.inc(sy.dma_start(out=dbg_tab.ap(), in_=bass.AP(tab, 0, [[CS, 1024], [1, CS]])))
            M.wait(sy, "rs_done")
            for tt in range(8):
                if tt >= 1:
                    M.wait(sy, f"d_o{tt - 1}")
                for j in range(4):
                    m_do.inc(sy.dma_start(
                        out=d_rs.ap(extra=[[1024, 1], [1, 1024]], f0=j * 1024)
                        if False else
                        bass.AP(d_rs.h, d_rs.c0 + j * 1024, [[d_rs.W, 128], [1, 1024]]),
                        in_=bass.AP(rsout[j], tt * 128 * 1024, [[1024, 128], [1, 1024]])))
                m_do.inc(sy.dma_start(
                    out=d_x[:, :],
                    in_=bass.AP(xres, tt * 128 * DM, [[DM, 128], [1, DM]])))
                M.set(f"d_ld{tt}", m_do)
                M.wait(sy, f"d_o{tt}")
                m_out.inc(sy.dma_start(
                    out=bass.AP(out, tt * 128 * DM, [[DM, 128], [1, DM]]),
                    in_=d_o[:, :]))
                M.set(f"d_st{tt}", m_out)
            if not dry:
                sy.wait_ge(m_out.h, m_out.n)

        def body_gpsimd(gp, dry):
            M.wait(gp, "ones_ready")
            e_gp.inc(gp.affine_select(g_ident[:, :], g_ones[:, :], [[1, 128]],
                                      ALU.is_equal, 0.0, base=0,
                                      channel_multiplier=-1))
            M.set("ident", e_gp)
            M.wait(gp, "tab_ready")
            for k in range(2):
                w, ic = UNITS[k]
                cols = 512 if w != "d" else 176
                src = {"g": idxg, "u": idxu, "d": idxd}[w]
                m_idx[k % 2].inc(gp.dma_start(
                    out=g_idx[k % 3][:, 0:cols],
                    in_=bass.AP(src, ic * 128 * cols, [[cols, 128], [1, cols]])))
                M.set(f"idx{k}", m_idx[k % 2])
            for k, (w, ic) in enumerate(UNITS):
                cols = 512 if w != "d" else 176
                M.wait(gp, f"idx{k}")
                if k >= 3:
                    M.wait(gp, f"stage_done{k - 3}")
                for j in range(cols):
                    m_g.inc(gp.indirect_dma_start(
                        g_stage[k % 3][:, j * CS:(j + 1) * CS],
                        None,
                        tab.ap(),
                        bass.IndirectOffsetOnAxis(g_idx[k % 3][:, j:j + 1], 0)))
                if not dry:
                    gp.wait_ge(m_g.h, m_g.n)
                if dry:
                    m_ck.inc(None)
                else:
                    m_ck.inc(gp.nop())
                M.set(f"gath{k}", m_ck)
                if k + 2 < len(UNITS):
                    w2, ic2 = UNITS[k + 2]
                    cols2 = 512 if w2 != "d" else 176
                    src2 = {"g": idxg, "u": idxu, "d": idxd}[w2]
                    m_idx[k % 2].inc(gp.dma_start(
                        out=g_idx[(k + 2) % 3][:, 0:cols2],
                        in_=bass.AP(src2, ic2 * 128 * cols2,
                                    [[cols2, 128], [1, cols2]])))
                    M.set(f"idx{k + 2}", m_idx[k % 2])
            for hb in range(4):
                M.wait(gp, f"rs_blk{hb}")
                m_rs.inc(gp.collective_compute(
                    "ReduceScatter", ALU.add, replica_groups=[list(range(NCORE))],
                    ins=[rsin[hb].ap().opt()],
                    outs=[rsout[hb].ap().opt()]))
            M.set("rs_done", m_rs)
            if not dry:
                gp.wait_ge(m_rs.h, m_rs.n)

        def body_vector(ve, dry):
            e = e_dve
            e.inc(ve.memset(g_zer[:, :], 0.0))
            M.set("zer_ready", e)
            e.inc(ve.memset(g_ones[:, :], 1.0))
            M.set("ones_ready", e)
            e.inc(ve.memset(g_eps[:, :], EPS))
            M.set("eps_ready", e)
            M.wait(ve, "cb_loaded")
            e.inc(ve.tensor_copy(a0_tabb[:, :], a0_tabf[:, :]))
            M.set("cb_cast", e)
            for t in range(TT64):
                xt = a0_x[t % 2]
                M.wait(ve, f"a0_x{t}")
                e.inc(ve.reduce_sum(g_s1[:, :], xt[:, :], axis=AX.X))
                M.wait(ve, f"a0_sq{t}")
                wait_self(dry, ve, e)
                e.inc(ve.tensor_scalar(g_mu[:, :], g_s1[:, :], 1.0 / DM, None,
                                       ALU.mult))
                e.inc(ve.tensor_scalar(g_var[:, :], g_s2[:, :], 1.0 / DM, None,
                                       ALU.mult))
                wait_self(dry, ve, e)
                e.inc(ve.tensor_tensor(g_mu2[:, :], g_mu[:, :], g_mu[:, :],
                                       ALU.mult))
                wait_self(dry, ve, e)
                e.inc(ve.tensor_tensor(g_var[:, :], g_var[:, :], g_mu2[:, :],
                                       ALU.subtract))
                M.set(f"a0_var{t}", e)
                M.wait(ve, f"a0_rstd{t}")
                e.inc(ve.scalar_tensor_tensor(g_nmr[:, :], g_mu[:, :], -1.0,
                                              g_rstd[:, :], ALU.mult, ALU.mult))
                wait_self(dry, ve, e)
                if t >= 1:
                    M.wait(ve, f"a0_trdone{t - 1}")
                e.inc(ve.tensor_scalar(a0_xn[:, :], xt[:, :], g_rstd[:, :],
                                       g_nmr[:, :], ALU.mult, ALU.add))
                M.set(f"a0_xn{t}", e)
                if t >= 2:
                    M.wait(ve, f"a0_wr{t - 2}")
                for grp in range(4):
                    M.wait(ve, f"a0_tr{t}_{grp}")
                    e.inc(ve.tensor_copy(a0_ev[t % 2].ap(f0=grp * 1024, fn=1024),
                                         psb(psb_dve[0] % 2, 1024)))
                    psb_evac_mark()
                M.set(f"a0_ev{t}", e)
                M.set(f"a0_done{t}", e)
                stage_at(ve, dry, "dve", "a0", t)
            for u in range(3 * EH):
                stage_at(ve, dry, "dve", "a1", u)
            for h in range(EH):
                for grp in range(2):
                    M.wait(ve, f"b_vtr{h}_{grp}")
                    e.inc(ve.tensor_copy(b_vtm.ap(f0=grp * 1024, fn=1024),
                                         psb(psb_dve[0] % 2, 1024)))
                    psb_evac_mark()
                M.set(f"b_vtm{h}", e)
                for qt in range(QT):
                    M.wait(ve, f"b_s{h}_{qt}")
                    e.inc(ve.reduce_max(g_nmx[:, :], ps(0, 2048), axis=AX.X,
                                        negate=True))
                    M.set(f"b_nmx{h}_{qt}", e)
                    M.wait(ve, f"b_exp{h}_{qt}")
                    e.inc(ve.reciprocal(g_rec[:, :], g_srow[:, :]))
                    wait_self(dry, ve, e)
                    e.inc(ve.tensor_scalar(g_diag[:, :], g_ident[:, :],
                                           g_rec[:, :], None, ALU.mult))
                    M.set(f"b_diag{h}_{qt}", e)
                    for pb in range(4):
                        M.wait(ve, f"b_pt{h}_{qt}_{pb}")
                        e.inc(ve.tensor_copy(b_pt.ap(f0=pb * 512, fn=512),
                                             ps(4, 512)))
                        M.set(f"b_ptev{h}_{qt}_{pb}", e)
                if True:
                    stage_at(ve, dry, "dve", "b", h)
            for og in range(8):
                for tg in range(4):
                    for tt in range(4):
                        M.wait(ve, f"b2_mm{og}_{tg}_{tt}")
                        li = (og * 4 + tg) * 4 + tt
                        if li >= 1:
                            pl = li - 1
                            M.wait(ve, f"b2_pw{pl // 16}_{pl % 16 // 4}_{pl % 4}")
                        e.inc(ve.tensor_copy(b2_ev[:, :], ps(tt, 512)))
                        M.set(f"b2_ev{og}_{tg}_{tt}", e)
                stage_at(ve, dry, "dve", "b2", og)
            for k in PRE_CG:
                stage_dve(ve, dry, k)
            for tsb in range(16):
                for ic in range(NIC):
                    bk = (ic % 2) * 2
                    M.wait(ve, f"c_sg{tsb}_{ic}")
                    M.wait(ve, f"c_umm{tsb}_{ic}")
                    if tsb + ic > 0:
                        pl = tsb * NIC + ic - 1
                        M.wait(ve, f"c_hh{pl // NIC}_{pl % NIC}")
                    e.inc(ve.tensor_tensor(c_hh[:, :], c_sg[:, :],
                                           ps(bk + 1, 512), ALU.mult))
                    M.set(f"c_hhv{tsb}_{ic}", e)
                    stage_at(ve, dry, "dve", "cg", tsb * NIC + ic)
            for hb in range(4):
                for tt in range(TT64):
                    lin = hb * TT64 + tt
                    M.wait(ve, f"c_dmm{hb}_{tt}")
                    M.wait(ve, f"c_dr{hb}_{tt}")
                    if lin > 0:
                        pl = lin - 1
                        M.wait(ve, f"c_dw{pl // TT64}_{pl % TT64}")
                    e.inc(ve.tensor_tensor(c_ev[:, 0:1024],
                                           ps(2 * (lin % 2), 1024),
                                           c_rs[lin % 2].ap(f0=0, fn=1024),
                                           ALU.add))
                    M.set(f"c_dev{hb}_{tt}", e)
            for tt in range(8):
                M.wait(ve, f"d_ld{tt}")
                if tt >= 1:
                    M.wait(ve, f"d_st{tt - 1}")
                e.inc(ve.tensor_tensor(d_o[:, :], d_rs[:, :], d_x[:, :], ALU.add))
                M.set(f"d_o{tt}", e)

        def body_scalar(sc, dry):
            e = e_act
            for t in range(TT64):
                xt = a0_x[t % 2]
                M.wait(sc, f"a0_x{t}")
                if t >= 1:
                    M.wait(sc, f"a0_var{t - 1}")
                e.inc(sc.activation(a0_sq[:, :], xt[:, :], AF.Square,
                                    accum_out=g_s2[:, :]))
                M.set(f"a0_sq{t}", e)
                M.wait(sc, f"a0_var{t}")
                if t == 0:
                    M.wait(sc, "eps_ready")
                e.inc(sc.activation(g_rstd[:, :], g_var[:, :], AF.Ln,
                                    bias=g_eps[:, :], scale=1.0))
                wait_self(dry, sc, e)
                e.inc(sc.activation(g_rstd[:, :], g_rstd[:, :], AF.Exp,
                                    scale=-0.5))
                M.set(f"a0_rstd{t}", e)
                stage_at(sc, dry, "act", "a0", t)
            for oi in range(3):
                for ec in range(EH):
                    for half in range(2):
                        wl = (oi * EH + ec) * 2 + half
                        M.wait(sc, f"a1_w{wl}")
                        if wl >= 1:
                            M.wait(sc, f"a1_mmwb{wl - 1}")
                        e.inc(sc.copy(a1_wb.ap(f0=half * 2048, fn=2048),
                                      a1_wf[:, :]))
                        M.set(f"a1_cast{wl}", e)
                    M.wait(sc, f"a1_mm{oi}_{ec}")
                    if oi + ec > 0:
                        pl = oi * EH + ec - 1
                        M.wait(sc, f"qkv_st{pl // EH}_{pl % EH}")
                    e.inc(sc.activation(a1_out[:, :], ps(0, 2048), AF.Copy,
                                        scale=SCALE if oi == 0 else 1.0))
                    M.set(f"a1_out{oi}_{ec}", e)
                    stage_at(sc, dry, "act", "a1", oi * EH + ec)
            for h in range(EH):
                for qt in range(QT):
                    M.wait(sc, f"b_nmx{h}_{qt}")
                    e.inc(sc.activation(b_p[:, :], ps(0, 2048), AF.Exp,
                                        bias=g_nmx[:, :], scale=1.0,
                                        accum_out=g_srow[:, :]))
                    M.set(f"b_exp{h}_{qt}", e)
                    M.wait(sc, f"b_av{h}_{qt}")
                    e.inc(sc.copy(b_a.ap(f0=h * SEQ + qt * 128, fn=128),
                                  ps(5, 128)))
                    M.set(f"b_avev{h}_{qt}", e)
                    if qt == QT - 1:
                        stage_at(sc, dry, "act", "b", h)
            for og in range(8):
                M.wait(sc, f"wo{og}")
                if og >= 1:
                    M.wait(sc, f"b2_lastmm{og - 1}")
                for q in range(4):
                    e.inc(sc.copy(b2_wb.ap(f0=q * 2048, fn=2048),
                                  b2_wf.ap(f0=q * 2048, fn=2048)))
                M.set(f"b2_cast{og}", e)
                stage_at(sc, dry, "act", "b2", og)
            for k in PRE_CG:
                stage_act(sc, k)
            M.set("gu_strips0", m_sw[0])
            M.set("gu_strips1", m_sw[1])
            for tsb in range(16):
                for ic in range(NIC):
                    bk = (ic % 2) * 2
                    M.wait(sc, f"c_gmm{tsb}_{ic}")
                    e.inc(sc.activation(c_sg[:, :], ps(bk, 512), AF.Silu))
                    M.set(f"c_sg{tsb}_{ic}", e)
                    stage_at(sc, dry, "act", "cg", tsb * NIC + ic)

        def body_tensor(te, dry):
            e = e_pe
            M.wait(te, "ident")
            for t in range(TT64):
                M.wait(te, f"a0_xn{t}")
                for grp in range(4):
                    half = psb_begin(te)
                    for blk in range(8):
                        hcb = grp * 8 + blk
                        ii = te.transpose(psb(half, 128, f0=blk * 128),
                                          a0_xn.ap(f0=hcb * 128, fn=128),
                                          g_ident[:, :])
                    e.inc(ii)
                    M.set(f"a0_tr{t}_{grp}", e)
                M.set(f"a0_trdone{t}", e)
                stage_at(te, dry, "pe", "a0", t)
            M.wait(te, "xnb")
            for oi in range(3):
                for ec in range(EH):
                    if oi + ec > 0:
                        pl = oi * EH + ec - 1
                        M.wait(te, f"a1_out{pl // EH}_{pl % EH}")
                    for half in range(2):
                        wl = (oi * EH + ec) * 2 + half
                        M.wait(te, f"a1_cast{wl}")
                        for hcb in range(half * 16, half * 16 + 16):
                            for nn in range(4):
                                ii = te.matmul(
                                    ps(nn, 512),
                                    a1_wb.ap(f0=hcb * 128, fn=128),
                                    a1_xnb.ap(f0=hcb * SEQ + nn * 512, fn=512),
                                    start=(hcb == 0), stop=(hcb == HC - 1))
                        e.inc(ii)
                        M.set(f"a1_mmwb{wl}", e)
                    M.set(f"a1_mm{oi}_{ec}", e)
                    stage_at(te, dry, "pe", "a1", oi * EH + ec)
            for h in range(EH):
                M.wait(te, f"b_qkv{h}")
                for grp in range(2):
                    half = psb_begin(te)
                    for blk in range(8):
                        kc = grp * 8 + blk
                        ii = te.transpose(psb(half, 128, f0=blk * 128),
                                          b_v[h % 2].ap(f0=kc * 128, fn=128),
                                          g_ident[:, :])
                    e.inc(ii)
                    M.set(f"b_vtr{h}_{grp}", e)
                M.wait(te, f"b_vtm{h}")
                for qt in range(QT):
                    if h + qt > 0:
                        pq = (h, qt - 1) if qt else (h - 1, QT - 1)
                        M.wait(te, f"b_exp{pq[0]}_{pq[1]}")
                    else:
                        M.wait(te, f"a1_out{2}_{EH - 1}")
                    for nn in range(4):
                        ii = te.matmul(ps(nn, 512),
                                       b_q[h % 2].ap(f0=qt * 128, fn=128),
                                       b_k[h % 2].ap(f0=nn * 512, fn=512),
                                       start=True, stop=True)
                    e.inc(ii)
                    M.set(f"b_s{h}_{qt}", e)
                    M.wait(te, f"b_diag{h}_{qt}")
                    for pb in range(4):
                        if pb >= 1:
                            M.wait(te, f"b_ptev{h}_{qt}_{pb - 1}")
                        for u in range(4):
                            kc = pb * 4 + u
                            ii = te.matmul(ps(4, 128, f0=u * 128),
                                           b_p.ap(f0=kc * 128, fn=128),
                                           g_diag[:, :], start=True, stop=True)
                        e.inc(ii)
                        M.set(f"b_pt{h}_{qt}_{pb}", e)
                    M.wait(te, f"b_ptev{h}_{qt}_3")
                    if h + qt > 0:
                        pq = (h, qt - 1) if qt else (h - 1, QT - 1)
                        M.wait(te, f"b_avev{pq[0]}_{pq[1]}")
                    for kc in range(16):
                        ii = te.matmul(ps(5, 128),
                                       b_vtm.ap(f0=kc * 128, fn=128),
                                       b_pt.ap(f0=kc * 128, fn=128),
                                       start=(kc == 0), stop=(kc == 15))
                    e.inc(ii)
                    M.set(f"b_av{h}_{qt}", e)
                M.set(f"b_done{h}", e)
                stage_at(te, dry, "pe", "b", h)
            for og in range(8):
                M.wait(te, f"b2_cast{og}")
                if og == 0:
                    M.wait(te, f"b_exp{EH - 1}_{QT - 1}")
                for tg in range(4):
                    for tt in range(4):
                        if og + tg > 0:
                            ptg = (og, tg - 1) if tg else (og - 1, 3)
                            M.wait(te, f"b2_ev{ptg[0]}_{ptg[1]}_{tt}")
                        for ec in range(EH):
                            ii = te.matmul(
                                ps(tt, 512),
                                b_a.ap(f0=ec * SEQ + (tg * 4 + tt) * 128, fn=128),
                                b2_wb.ap(f0=ec * 512, fn=512),
                                start=(ec == 0), stop=(ec == EH - 1))
                        e.inc(ii)
                        M.set(f"b2_mm{og}_{tg}_{tt}", e)
                M.set(f"b2_lastmm{og}", e)
                stage_at(te, dry, "pe", "b2", og)
            for k in PRE_CG:
                stage_pe(te, k)
            for tsb in range(16):
                M.wait(te, f"c_xn{tsb}")
                for ic in range(NIC):
                    M.wait(te, f"c_w{tsb}_{ic}")
                    bk = (ic % 2) * 2
                    if tsb * NIC + ic >= 2:
                        pl = tsb * NIC + ic - 2
                        M.wait(te, f"c_hhv{pl // NIC}_{pl % NIC}")
                    else:
                        M.wait(te, "b2_ev7_3_3")
                    for op in range(2):
                        wbuf = (c_wg if op == 0 else c_wu)[ic % 2]
                        for hcb in range(HC):
                            ii = te.matmul(ps(bk + op, 512),
                                           wbuf.ap(f0=hcb * 128, fn=128),
                                           c_xn.ap(f0=hcb * 512, fn=512),
                                           start=(hcb == 0), stop=(hcb == HC - 1))
                        e.inc(ii)
                        if op == 0:
                            M.set(f"c_gmm{tsb}_{ic}", e)
                    M.set(f"c_umm{tsb}_{ic}", e)
                    stage_at(te, dry, "pe", "cg", tsb * NIC + ic)
            M.wait(te, f"c_hhv{15}_{NIC - 1}")
            for hb in range(4):
                M.wait(te, f"wd_ev{len(GU_UNITS) + 8 * (hb + 1) - 1}")
                for tt in range(TT64):
                    lin = hb * TT64 + tt
                    M.wait(te, f"c_dl{hb}_{tt}")
                    if lin >= 2:
                        pl = lin - 2
                        M.wait(te, f"c_dev{pl // TT64}_{pl % TT64}")
                    for ic in range(NIC):
                        for nn in range(2):
                            ii = te.matmul(
                                ps(2 * (lin % 2) + nn, 512),
                                c_ht[lin % 2].ap(f0=ic * 128, fn=128),
                                bass.AP(c_wdT.h,
                                        c_wdT.c0 + ic * 4096 + hb * 1024 + nn * 512,
                                        [[c_wdT.W, 128], [1, 512]]),
                                start=(ic == 0), stop=(ic == NIC - 1))
                    e.inc(ii)
                    M.set(f"c_dmm{hb}_{tt}", e)

        M.dry = True
        dz = DryEng()
        psb_pe[0] = 0
        psb_dve[0] = 0
        for s in sems:
            s.n = 0
        body_vector(dz, True)
        body_scalar(dz, True)
        body_tensor(dz, True)
        body_gpsimd(dz, True)
        body_sync(dz, True)

        M.dry = False
        psb_pe[0] = 0
        psb_dve[0] = 0
        for s in sems:
            s.n = 0
            s.h = E(nc.semaphore(s.name))

        block = E(nc.Block())

        @block.vector
        def _(ve):
            body_vector(ve, False)

        @block.scalar
        def _(sc):
            body_scalar(sc, False)

        @block.tensor
        def _(te):
            body_tensor(te, False)

        @block.gpsimd
        def _(gp):
            body_gpsimd(gp, False)

        @block.sync
        def _(sy):
            body_sync(sy, False)

    return nc


def kernel(L0_attn_x, L0_attn_w_q, L0_attn_w_k, L0_attn_w_v, L0_attn_w_o,
           L0_ffn_codebook, L0_ffn_idx_gate, L0_ffn_idx_up, L0_ffn_idx_down):
    x = np.ascontiguousarray(np.asarray(L0_attn_x, dtype=np.float32)).reshape(TOK, DM)
    wq_ = np.asarray(L0_attn_w_q, np.float32)
    wk_ = np.asarray(L0_attn_w_k, np.float32)
    wv_ = np.asarray(L0_attn_w_v, np.float32)
    wo_ = np.asarray(L0_attn_w_o, np.float32)
    cb = np.ascontiguousarray(np.asarray(L0_ffn_codebook, np.float32))
    ig = np.asarray(L0_ffn_idx_gate, np.int32)
    iu = np.asarray(L0_ffn_idx_up, np.int32)
    idn = np.asarray(L0_ffn_idx_down, np.int32)

    def blk_w(w, s):
        wt = w.T[:, s * ES:(s + 1) * ES]
        return np.ascontiguousarray(
            wt.reshape(HC, 128, EH, 128).transpose(2, 0, 1, 3))

    def blk_wo(s):
        wt = wo_.T[s * ES:(s + 1) * ES, :]
        return np.ascontiguousarray(
            wt.reshape(EH, 128, 8, 512).transpose(2, 0, 1, 3))

    def shard_gu(idx, c):
        sh = idx[c * IRE:(c + 1) * IRE]
        sh = np.concatenate([sh, np.zeros((ISH - IRE, 512), np.int32)], 0)
        return np.ascontiguousarray(sh.reshape(NIC, 128, 512))

    def shard_d(c):
        sh = idn[:, c * (IRE // CS):(c + 1) * (IRE // CS)]
        sh = np.concatenate([sh, np.zeros((DM, (ISH - IRE) // CS), np.int32)], 1)
        return np.ascontiguousarray(sh.reshape(HC, 128, 176))

    in_maps = []
    for c in range(NCORE):
        s = c % 2
        in_maps.append({
            "xfull": x,
            "xres": np.ascontiguousarray(x[c * HALF:(c + 1) * HALF]),
            "wq": blk_w(wq_, s), "wk": blk_w(wk_, s), "wv": blk_w(wv_, s),
            "wo": blk_wo(s),
            "idxg": shard_gu(ig, c), "idxu": shard_gu(iu, c), "idxd": shard_d(c),
            "cbt": cb,
        })

    nc = build_program()
    res = run_bass_kernel_spmd(nc, in_maps, core_ids=list(range(NCORE)))
    outp = np.empty((TOK, DM), np.float32)
    for c in range(NCORE):
        outp[c * HALF:(c + 1) * HALF] = res.results[c]["o"]
    return outp.reshape(BATCH, SEQ, DM)


# version 12 (final)
# speedup vs baseline: 1.1698x; 1.0584x over previous
import os
import numpy as np
import concourse.bass as bass
import concourse.mybir as mybir
from concourse.bass_utils import run_bass_kernel_spmd
from contextlib import ExitStack

F32 = mybir.dt.float32
BF16 = mybir.dt.bfloat16
I32 = mybir.dt.int32
AF = mybir.ActivationFunctionType
ALU = mybir.AluOpType
AX = mybir.AxisListType

BATCH, SEQ, DM = 4, 2048, 4096
HEADS, DK = 32, 128
CB, CS = 65536, 8
TOK = BATCH * SEQ
NCORE = 8
EH, ES = 16, 2048
ISH, NIC, IRE = 1408, 11, 1376
TT64 = TOK // 128
QT = SEQ // 128
HC = DM // 128
HALF = SEQ // 2
SCALE = float(1.0 / np.sqrt(DK))
EPS = 1e-5

GU_UNITS = [(w, ic) for ic in range(NIC) for w in ("g", "u")]
WD_UNITS = [("d", hc) for hc in range(HC)]
UNITS = GU_UNITS + WD_UNITS


class SemC:
    def __init__(self, name, step=1):
        self.name, self.step, self.n, self.h = name, step, 0, None

    def inc(self, instr, k=1):
        if instr is not None:
            instr.then_inc(self.h, self.step * k)
        self.n += self.step * k
        return instr


class Marks:
    def __init__(self):
        self.d = {}
        self.dry = True

    def set(self, name, sem):
        if self.dry:
            self.d[name] = (sem, sem.n)
        else:
            assert name in self.d and self.d[name] == (sem, sem.n), \
                (name, self.d.get(name), sem.name, sem.n)

    def alias(self, name, src):
        if self.dry:
            self.d[name] = self.d[src]

    def wait(self, eng, name):
        if self.dry:
            return
        sem, v = self.d[name]
        if v > 0:
            eng.wait_ge(sem.h, v)


class DryInstr:
    def then_inc(self, *a):
        return self


class DryEng:
    def __getattr__(self, name):
        def f(*a, **k):
            return DryInstr()
        return f


def build_program():
    nc = bass.Bass("TRN2", target_bir_lowering=False)

    xfull = nc.dram_tensor("xfull", [TOK, DM], F32, kind="ExternalInput")
    xres = nc.dram_tensor("xres", [HALF, DM], F32, kind="ExternalInput")
    wq = nc.dram_tensor("wq", [EH, HC, 128, 128], F32, kind="ExternalInput")
    wk = nc.dram_tensor("wk", [EH, HC, 128, 128], F32, kind="ExternalInput")
    wv = nc.dram_tensor("wv", [EH, HC, 128, 128], F32, kind="ExternalInput")
    wo = nc.dram_tensor("wo", [8, EH, 128, 512], F32, kind="ExternalInput")
    idxg = nc.dram_tensor("idxg", [NIC, 128, 512], I32, kind="ExternalInput")
    idxu = nc.dram_tensor("idxu", [NIC, 128, 512], I32, kind="ExternalInput")
    idxd = nc.dram_tensor("idxd", [HC, 128, 176], I32, kind="ExternalInput")
    cbt = nc.dram_tensor("cbt", [CB, CS], F32, kind="ExternalInput")
    out = nc.dram_tensor("o", [HALF, DM], F32, kind="ExternalOutput")

    tab = nc.dram_tensor("tab", [CB, CS], BF16)
    xnd = nc.dram_tensor("xnd", [128, HC * TOK], BF16)
    qd = nc.dram_tensor("qd", [EH, 128, SEQ], BF16)
    kd = nc.dram_tensor("kd", [EH, 128, SEQ], BF16)
    vd = nc.dram_tensor("vd", [EH, 128, SEQ], BF16)
    wgT = nc.dram_tensor("wgT", [NIC, HC, 128, 128], BF16)
    wuT = nc.dram_tensor("wuT", [NIC, HC, 128, 128], BF16)
    hhd = nc.dram_tensor("hhd", [NIC, 128, TOK], BF16)
    attnd = nc.dram_tensor("attnd", [SEQ, DM], F32)
    rsin = [nc.dram_tensor(f"rsin{j}", [TOK, 1024], F32) for j in range(4)]
    rsout = [nc.dram_tensor(f"rsout{j}", [HALF, 1024], F32) for j in range(4)]
    DBG = bool(os.environ.get("MK_DEBUG"))
    if DBG:
        dbg_xn = nc.dram_tensor("dbg_xn", [128, 4096], BF16, kind="ExternalOutput")
        dbg_q = nc.dram_tensor("dbg_q", [128, SEQ], BF16, kind="ExternalOutput")
        dbg_k = nc.dram_tensor("dbg_k", [128, SEQ], BF16, kind="ExternalOutput")
        dbg_v = nc.dram_tensor("dbg_v", [128, SEQ], BF16, kind="ExternalOutput")
        dbg_at = nc.dram_tensor("dbg_at", [256, DM], F32, kind="ExternalOutput")
        dbg_wg = nc.dram_tensor("dbg_wg", [HC * 128, 128], BF16, kind="ExternalOutput")
        dbg_hh = nc.dram_tensor("dbg_hh", [128, 2048], BF16, kind="ExternalOutput")
        dbg_rs = nc.dram_tensor("dbg_rs", [256, DM], F32, kind="ExternalOutput")
        dbg_tab = nc.dram_tensor("dbg_tab", [1024, CS], BF16, kind="ExternalOutput")
        dbg_a = nc.dram_tensor("dbg_a", [128, EH * SEQ], BF16, kind="ExternalOutput")

    ctx = ExitStack()
    with ctx:
        E = ctx.enter_context
        AB = 204 * 1024
        arena = E(nc.sbuf_tensor("arena", [128, AB], mybir.dt.uint8))
        hf, hb, hi = arena.bitcast(F32), arena.bitcast(BF16), arena.bitcast(I32)
        WF, WB = AB // 4, AB // 2

        class TV:
            def __init__(self, h, W, c0, cols):
                self.h, self.W, self.c0, self.cols = h, W, c0, cols

            def ap(self, p0=0, pn=128, f0=0, fn=None, extra=None):
                fn = self.cols - f0 if fn is None else fn
                pat = [[self.W, pn]] + (extra if extra else [[1, fn]])
                return bass.AP(self.h, p0 * self.W + self.c0 + f0, pat)

            def __getitem__(self, key):
                ps_, fs = key
                p0 = ps_.start or 0
                pn = (128 if ps_.stop is None else ps_.stop) - p0
                f0 = fs.start or 0
                fn = (self.cols if fs.stop is None else fs.stop) - f0
                return self.ap(p0, pn, f0, fn)

        off = [0]

        def alloc(dt, cols, align=512):
            if cols <= 128:
                align = 64
            esz = 4 if dt in (F32, I32) else 2
            off[0] = (off[0] + align - 1) // align * align
            base = off[0]
            off[0] += cols * esz
            assert off[0] <= AB, f"arena overflow {off[0]}"
            h = hf if dt is F32 else hi if dt is I32 else hb
            return TV(h, WF if esz == 4 else WB, base // esz, cols)

        g_ident = alloc(BF16, 128)
        g_ones = alloc(BF16, 128)
        g_zer = alloc(F32, 2048)
        g_stage = [alloc(BF16, 4096) for _ in range(3)]
        g_evac = [alloc(BF16, 1024) for _ in range(2)]
        g_idx = [alloc(I32, 512) for _ in range(3)]
        g_s1 = alloc(F32, 1)
        g_s2 = alloc(F32, 1)
        g_mu = alloc(F32, 1)
        g_var = alloc(F32, 1)
        g_mu2 = alloc(F32, 1)
        g_rstd = alloc(F32, 1)
        g_nmr = alloc(F32, 1)
        g_nmx = alloc(F32, 1)
        g_srow = alloc(F32, 1)
        g_rec = alloc(F32, 1)
        g_diag = alloc(BF16, 128)
        g_eps = alloc(F32, 1)
        glob_end = off[0]

        off[0] = glob_end
        a0_x = [alloc(F32, 4096), alloc(F32, 4096)]
        a0_xn = alloc(BF16, 4096)
        a0_ev = [alloc(BF16, 4096), alloc(BF16, 4096)]
        a0_sq = alloc(F32, 4096)
        a0_tabf = alloc(F32, 4096)
        a0_tabb = alloc(BF16, 4096)
        off[0] = glob_end
        a1_xnb = alloc(BF16, HC * SEQ)
        a1_wf = alloc(F32, 2048)
        a1_wb = alloc(BF16, HC * 128)
        a1_out = alloc(BF16, SEQ)
        off[0] = glob_end
        b_a = alloc(BF16, EH * SEQ)
        b_pt = alloc(BF16, 16 * 128)
        b_q = [alloc(BF16, SEQ), alloc(BF16, SEQ)]
        b_k = [alloc(BF16, SEQ), alloc(BF16, SEQ)]
        b_v = [alloc(BF16, SEQ), alloc(BF16, SEQ)]
        b_vtm = alloc(BF16, SEQ)
        b_p = alloc(BF16, SEQ)
        b_end = off[0]
        off[0] = (b_a.c0 + b_a.cols) * 2
        b2_wf = alloc(F32, EH * 512)
        b2_wb = alloc(BF16, EH * 512)
        b2_ev = alloc(F32, 512)
        off[0] = glob_end
        c_wdT = alloc(BF16, NIC * 4096)
        c_xn = alloc(BF16, HC * 512)
        _cw = alloc(BF16, HC * 128)
        _cu = alloc(BF16, HC * 128)
        c_wg = [_cw, _cw]
        c_wu = [_cu, _cu]
        c_sg = alloc(BF16, 512)
        c_hh = alloc(BF16, 512)
        c_gu_end = off[0]
        off[0] = c_wdT.c0 * 2 + c_wdT.cols * 2
        c_ht = [alloc(BF16, NIC * 128), alloc(BF16, NIC * 128)]
        c_rs = [alloc(F32, 4096), alloc(F32, 4096)]
        c_ev = alloc(F32, 2048)
        off[0] = (c_wdT.c0 + c_wdT.cols) * 2
        d_rs = alloc(F32, 4096)
        d_x = alloc(F32, 4096)
        d_o = alloc(F32, 4096)
        assert off[0] <= AB, off[0]

        ps_main = E(nc.psum_tensor("ps_main", [128, 3072], F32))
        ps_stg = E(nc.psum_tensor("ps_stg", [128, 2048], BF16))
        PS_W, PSB_W = 3072, 2048

        def ps(bank, cols, f0=0):
            return bass.AP(ps_main, bank * 512 + f0, [[PS_W, 128], [1, cols]])

        def psb(half, cols, f0=0):
            return bass.AP(ps_stg, half * 1024 + f0, [[PSB_W, 128], [1, cols]])

        sems = []

        def S(name, step=1):
            s = SemC(name, step)
            sems.append(s)
            return s

        m_tab = S("m_tab", 16)
        m_idx = [S("m_idx0", 16), S("m_idx1", 16)]
        m_g = S("m_g", 16)
        m_ck = S("m_ck", 1)
        m_x = [S("m_x0", 16), S("m_x1", 16)]
        m_xw = [S("m_xw0", 16), S("m_xw1", 16)]
        m_xnb = S("m_xnb", 16)
        m_w = S("m_w", 16)
        m_qw = S("m_qw", 16)
        m_qr = [S("m_qr0", 16), S("m_qr1", 16)]
        m_wo = S("m_wo", 16)
        m_z = S("m_z", 16)
        m_pw = S("m_pw", 16)
        m_cp = S("m_cp", 16)
        m_sw = [S("m_sw0", 16), S("m_sw1", 16)]
        m_cxn = S("m_cxn", 16)
        m_cw = S("m_cw", 16)
        m_dh = [S("m_dh0", 16), S("m_dh1", 16)]
        m_hw = S("m_hw", 16)
        m_rr = [S("m_rr0", 16), S("m_rr1", 16)]
        m_rw = S("m_rw", 16)
        m_rwc = [S(f"m_rwc{j}", 16) for j in range(4)]
        m_rs = S("m_rs", 1)
        m_do = S("m_do", 16)
        m_out = S("m_out", 16)
        e_pe = S("e_pe", 1)
        e_dve = S("e_dve", 1)
        e_act = S("e_act", 1)
        e_gp = S("e_gp", 1)

        M = Marks()

        pe_positions = []
        for t in range(TT64):
            pe_positions.append(("a0", t, 14.0))
        for u in range(3 * EH):
            pe_positions.append(("a1", u, 72.0))
        for h in range(EH):
            pe_positions.append(("b", h, 240.0))
        for og in range(8):
            pe_positions.append(("b2", og, 140.0))
        for lin in range(16 * NIC):
            pe_positions.append(("cg", lin, 15.0))
        g_done = []
        acc = 250.0
        for (w, ic) in UNITS:
            acc += (512 if w != "d" else 176) * 1.12 + 30.0
            g_done.append(acc)
        ASSIGN = {}
        ptime = 0.0
        k = 0
        NGU = len(GU_UNITS)
        for (ph, idx, dt) in pe_positions:
            if ph == "cg":
                break
            ptime += dt
            while k < NGU:
                if ptime < g_done[k] or (ph == "b2" and idx == 7):
                    break
                ASSIGN.setdefault((ph, idx), []).append(k)
                k += 1
        PRE_CG = list(range(k, NGU))
        NWD_CG = 25
        for j in range(NWD_CG):
            pos = min(int((j + 1) * (16 * NIC) / (NWD_CG + 1.0)), 16 * NIC - 1)
            ASSIGN.setdefault(("cg", pos), []).append(NGU + j)
        WD_LATE = {1: [25, 26], 2: [27, 28], 3: [29, 30, 31]}

        psb_pe = [0]
        psb_dve = [0]

        def psb_begin(te):
            i = psb_pe[0]
            if i >= 2:
                M.wait(te, f"psbe{i - 2}")
            psb_pe[0] += 1
            return i % 2

        def psb_evac_mark():
            i = psb_dve[0]
            M.set(f"psbe{i}", e_dve)
            psb_dve[0] += 1

        def wait_self(dry, eng, sem):
            if not dry:
                eng.wait_ge(sem.h, sem.n)

        def stage_pe(te, k):
            w, ic = UNITS[k]
            if w == "d":
                M.wait(te, f"wd_pad{k}")
                for grp in range(2):
                    n = 8 if grp == 0 else 3
                    half = psb_begin(te)
                    for blk in range(n):
                        icc = grp * 8 + blk
                        ii = te.transpose(psb(half, 128, f0=blk * 128),
                                          g_stage[k % 3].ap(f0=icc * 128, fn=128),
                                          g_ident[:, :])
                    e_pe.inc(ii)
                    M.set(f"st_tr{k}_{grp}", e_pe)
            else:
                M.wait(te, f"gath{k}")
                for grp in range(4):
                    half = psb_begin(te)
                    for blk in range(8):
                        hcb = grp * 8 + blk
                        ii = te.transpose(psb(half, 128, f0=blk * 128),
                                          g_stage[k % 3].ap(f0=hcb * 128, fn=128),
                                          g_ident[:, :])
                    e_pe.inc(ii)
                    M.set(f"st_tr{k}_{grp}", e_pe)

        def stage_dve(ve, dry, k):
            e = e_dve
            w, ic = UNITS[k]
            if w == "d":
                M.wait(ve, f"gath{k}")
                e.inc(ve.memset(g_stage[k % 3].ap(f0=1376, fn=32), 0.0))
                M.set(f"wd_pad{k}", e)
                for grp in range(2):
                    n = 8 if grp == 0 else 3
                    M.wait(ve, f"st_tr{k}_{grp}")
                    e.inc(ve.tensor_copy(
                        bass.AP(c_wdT.h, c_wdT.c0 + (grp * 8) * 4096 + ic * 128,
                                [[c_wdT.W, 128], [4096, n], [1, 128]]),
                        psb(psb_dve[0] % 2, n * 128)))
                    psb_evac_mark()
                M.set(f"wd_ev{k}", e)
            else:
                dstt = wgT if w == "g" else wuT
                for grp in range(4):
                    j = k * 4 + grp
                    M.wait(ve, f"st_tr{k}_{grp}")
                    if j >= 2:
                        pl = j - 2
                        M.wait(ve, f"st_wr{pl // 4}_{pl % 4}")
                    e.inc(ve.tensor_copy(g_evac[grp % 2][:, :],
                                         psb(psb_dve[0] % 2, 1024)))
                    psb_evac_mark()
                    M.set(f"st_ev{k}_{grp}", e)
            M.set(f"stage_done{k}", e)

        def stage_act(sc, k):
            w, ic = UNITS[k]
            if w == "d":
                return
            dstt = wgT if w == "g" else wuT
            for grp in range(4):
                j = k * 4 + grp
                M.wait(sc, f"st_ev{k}_{grp}")
                m_sw[j % 2].inc(sc.dma_start(
                    out=bass.AP(dstt, (ic * HC + grp * 8) * 16384,
                                [[128, 128], [16384, 8], [1, 128]]),
                    in_=g_evac[grp % 2].ap(extra=[[128, 8], [1, 128]])))
                M.set(f"st_wr{k}_{grp}", m_sw[j % 2])

        def stage_at(eng, dry, which, ph, idx):
            for k in ASSIGN.get((ph, idx), []):
                if which == "pe":
                    stage_pe(eng, k)
                elif which == "act":
                    stage_act(eng, k)
                else:
                    stage_dve(eng, dry, k)

        def body_sync(sy, dry):
            m_tab.inc(sy.dma_start(
                out=a0_tabf[:, :],
                in_=cbt.ap().rearrange("(p a) b -> p (a b)", p=128)))
            M.set("cb_loaded", m_tab)
            M.wait(sy, "cb_cast")
            m_tab.inc(sy.dma_start(
                out=tab.ap().rearrange("(p a) b -> p (a b)", p=128),
                in_=a0_tabb[:, :]))
            M.set("tab_ready", m_tab)
            M.wait(sy, "zer_ready")
            for r in range(TT64):
                for j in range(4):
                    m_z.inc(sy.dma_start(
                        out=bass.AP(rsin[j], r * 128 * 1024, [[1024, 128], [1, 1024]]),
                        in_=g_zer[:, 0:1024]))
            M.set("zero_done", m_z)
            for t in range(2):
                m_x[t % 2].inc(sy.dma_start(
                    out=a0_x[t][:, :],
                    in_=bass.AP(xfull, t * 128 * DM, [[DM, 128], [1, DM]])))
                M.set(f"a0_x{t}", m_x[t % 2])
            for t in range(TT64):
                if t + 2 < TT64:
                    M.wait(sy, f"a0_done{t}")
                    m_x[t % 2].inc(sy.dma_start(
                        out=a0_x[t % 2][:, :],
                        in_=bass.AP(xfull, (t + 2) * 128 * DM, [[DM, 128], [1, DM]])))
                    M.set(f"a0_x{t + 2}", m_x[t % 2])
                M.wait(sy, f"a0_ev{t}")
                for q4 in range(4):
                    m_xw[t % 2].inc(sy.dma_start(
                        out=bass.AP(xnd, (q4 * 8) * TOK + t * 128,
                                    [[HC * TOK, 128], [TOK, 8], [1, 128]]),
                        in_=a0_ev[t % 2].ap(f0=q4 * 1024, extra=[[128, 8], [1, 128]])))
                M.set(f"a0_wr{t}", m_xw[t % 2])
            M.set("xn_all0", m_xw[0])
            M.set("xn_all1", m_xw[1])
            M.wait(sy, "xn_all0")
            M.wait(sy, "xn_all1")
            if dry:
                m_xnb.inc(None, 4)
            else:
                with sy.register("rb") as rb:
                    pid = sy.partition_id()
                    sy.reg_alu(rb, pid, 1, ALU.logical_shift_right)
                    sy.reg_alu(rb, rb, SEQ, ALU.mult)
                    with sy.register("rb2") as rb2:
                        for q4 in range(4):
                            sy.reg_alu(rb2, rb, (q4 * 8) * TOK, ALU.add)
                            m_xnb.inc(sy.dma_start(
                                out=a1_xnb.ap(f0=q4 * 8 * SEQ,
                                              extra=[[SEQ, 8], [1, SEQ]]),
                                in_=bass.AP(xnd, rb2,
                                            [[HC * TOK, 128], [TOK, 8], [1, SEQ]])))
            M.set("xnb", m_xnb)
            for oi in range(3):
                wsrc = (wq, wk, wv)[oi]
                for ec in range(EH):
                    for half in range(2):
                        wl = (oi * EH + ec) * 2 + half
                        if wl > 0:
                            M.wait(sy, f"a1_cast{wl - 1}")
                        m_w.inc(sy.dma_start(
                            out=a1_wf[:, :],
                            in_=bass.AP(wsrc, (ec * HC + half * 16) * 16384,
                                        [[128, 128], [16384, 16], [1, 128]])))
                        M.set(f"a1_w{wl}", m_w)
                    M.wait(sy, f"a1_out{oi}_{ec}")
                    m_qw.inc(sy.dma_start(
                        out=bass.AP((qd, kd, vd)[oi], ec * 128 * SEQ,
                                    [[SEQ, 128], [1, SEQ]]),
                        in_=a1_out[:, :]))
                    M.set(f"qkv_st{oi}_{ec}", m_qw)
            M.set("qkv_all", m_qw)
            M.wait(sy, "qkv_all")
            for h in range(EH):
                if h >= 2:
                    M.wait(sy, f"b_done{h - 2}")
                for oi, src in enumerate((qd, kd, vd)):
                    m_qr[h % 2].inc(sy.dma_start(
                        out=(b_q, b_k, b_v)[oi][h % 2][:, :],
                        in_=bass.AP(src, h * 128 * SEQ, [[SEQ, 128], [1, SEQ]])))
                M.set(f"b_qkv{h}", m_qr[h % 2])
            M.wait(sy, f"b_done{EH - 1}")
            for og in range(8):
                if og >= 1:
                    M.wait(sy, f"b2_cast{og - 1}")
                m_wo.inc(sy.dma_start(
                    out=b2_wf[:, :],
                    in_=bass.AP(wo, og * EH * 65536,
                                [[512, 128], [65536, EH], [1, 512]])))
                M.set(f"wo{og}", m_wo)
                for tg in range(4):
                    for tt in range(4):
                        M.wait(sy, f"b2_ev{og}_{tg}_{tt}")
                        m_pw.inc(sy.dma_start(
                            out=bass.AP(attnd, (tg * 4 + tt) * 128 * DM + og * 512,
                                        [[DM, 128], [1, 512]]),
                            in_=b2_ev[:, :]))
                        M.set(f"b2_pw{og}_{tg}_{tt}", m_pw)
            if not dry:
                sy.wait_ge(m_pw.h, m_pw.n)
            M.wait(sy, "zero_done")
            if dry:
                m_cp.inc(None, 4)
            else:
                with sy.register("ro") as ro:
                    pid2 = sy.partition_id()
                    sy.reg_alu(ro, pid2, 1, ALU.logical_shift_right)
                    sy.reg_alu(ro, ro, SEQ * 1024, ALU.mult)
                    for j in range(4):
                        m_cp.inc(sy.dma_start(
                            out=bass.AP(rsin[j], ro, [[1024, SEQ], [1, 1024]]),
                            in_=bass.AP(attnd, j * 1024, [[DM, SEQ], [1, 1024]])))
            M.set("part_done", m_cp)
            if DBG:
                m_cp.inc(sy.dma_start(out=dbg_a.ap(), in_=b_a[:, :]))
                if not dry:
                    sy.wait_ge(m_cp.h, m_cp.n)
            M.wait(sy, "gu_strips0")
            M.wait(sy, "gu_strips1")
            for tsb in range(16):
                if tsb >= 1:
                    M.wait(sy, f"c_umm{tsb - 1}_{NIC - 1}")
                for q4 in range(4):
                    m_cxn.inc(sy.dma_start(
                        out=c_xn.ap(f0=q4 * 8 * 512, extra=[[512, 8], [1, 512]]),
                        in_=bass.AP(xnd, (q4 * 8) * TOK + tsb * 512,
                                    [[HC * TOK, 128], [TOK, 8], [1, 512]])))
                M.set(f"c_xn{tsb}", m_cxn)
                for ic in range(NIC):
                    if tsb * NIC + ic >= 1:
                        pl = tsb * NIC + ic - 1
                        M.wait(sy, f"c_hh{pl // NIC}_{pl % NIC}")
                    m_cw.inc(sy.dma_start(
                        out=c_wg[ic % 2][:, :],
                        in_=bass.AP(wgT, ic * HC * 16384,
                                    [[128, 128], [16384, HC], [1, 128]])))
                    m_cw.inc(sy.dma_start(
                        out=c_wu[ic % 2][:, :],
                        in_=bass.AP(wuT, ic * HC * 16384,
                                    [[128, 128], [16384, HC], [1, 128]])))
                    M.set(f"c_w{tsb}_{ic}", m_cw)
                    M.wait(sy, f"c_hhv{tsb}_{ic}")
                    m_hw.inc(sy.dma_start(
                        out=bass.AP(hhd, ic * 128 * TOK + tsb * 512,
                                    [[TOK, 128], [1, 512]]),
                        in_=c_hh[:, :]))
                    M.set(f"c_hh{tsb}_{ic}", m_hw)
            M.set("hh_all", m_hw)
            M.wait(sy, "hh_all")
            M.wait(sy, "part_done")
            M.wait(sy, "zero_done")
            for hb in range(4):
                for tt in range(TT64):
                    lin = hb * TT64 + tt
                    if lin >= 2:
                        pl = lin - 2
                        M.wait(sy, f"c_dev{pl // TT64}_{pl % TT64}")
                    m_dh[lin % 2].inc(sy.dma_start(
                        out=c_ht[lin % 2].ap(extra=[[128, NIC], [1, 128]]),
                        in_=bass.AP(hhd, tt * 128,
                                    [[TOK, 128], [128 * TOK, NIC], [1, 128]])))
                    M.set(f"c_dl{hb}_{tt}", m_dh[lin % 2])
                    m_rr[lin % 2].inc(sy.dma_start(
                        out=c_rs[lin % 2][:, 0:1024],
                        in_=bass.AP(rsin[hb], tt * 128 * 1024,
                                    [[1024, 128], [1, 1024]])))
                    M.set(f"c_dr{hb}_{tt}", m_rr[lin % 2])
                    M.wait(sy, f"c_dev{hb}_{tt}")
                    m_rwc[hb].inc(sy.dma_start(
                        out=bass.AP(rsin[hb], tt * 128 * 1024,
                                    [[1024, 128], [1, 1024]]),
                        in_=c_ev[:, 0:1024]))
                    M.set(f"c_dw{hb}_{tt}", m_rwc[hb])
                M.set(f"rs_blk{hb}", m_rwc[hb])
            if DBG:
                m_do.inc(sy.dma_start(out=dbg_xn.ap(), in_=bass.AP(xnd, 0, [[HC * TOK, 128], [1, 4096]])))
                m_do.inc(sy.dma_start(out=dbg_q.ap(), in_=bass.AP(qd, 0, [[SEQ, 128], [1, SEQ]])))
                m_do.inc(sy.dma_start(out=dbg_k.ap(), in_=bass.AP(kd, 0, [[SEQ, 128], [1, SEQ]])))
                m_do.inc(sy.dma_start(out=dbg_v.ap(), in_=bass.AP(vd, 0, [[SEQ, 128], [1, SEQ]])))
                m_do.inc(sy.dma_start(out=dbg_at.ap(), in_=bass.AP(attnd, 0, [[DM, 256], [1, DM]])))
                m_do.inc(sy.dma_start(out=dbg_wg.ap(), in_=bass.AP(wgT, 0, [[128, HC * 128], [1, 128]])))
                m_do.inc(sy.dma_start(out=dbg_hh.ap(), in_=bass.AP(hhd, 0, [[TOK, 128], [1, 2048]])))
                m_do.inc(sy.dma_start(out=dbg_rs.ap(), in_=bass.AP(rsin, 0, [[DM, 256], [1, DM]])))
                m_do.inc(sy.dma_start(out=dbg_tab.ap(), in_=bass.AP(tab, 0, [[CS, 1024], [1, CS]])))
            M.wait(sy, "rs_done")
            for tt in range(8):
                if tt >= 1:
                    M.wait(sy, f"d_o{tt - 1}")
                for j in range(4):
                    m_do.inc(sy.dma_start(
                        out=d_rs.ap(extra=[[1024, 1], [1, 1024]], f0=j * 1024)
                        if False else
                        bass.AP(d_rs.h, d_rs.c0 + j * 1024, [[d_rs.W, 128], [1, 1024]]),
                        in_=bass.AP(rsout[j], tt * 128 * 1024, [[1024, 128], [1, 1024]])))
                m_do.inc(sy.dma_start(
                    out=d_x[:, :],
                    in_=bass.AP(xres, tt * 128 * DM, [[DM, 128], [1, DM]])))
                M.set(f"d_ld{tt}", m_do)
                M.wait(sy, f"d_o{tt}")
                m_out.inc(sy.dma_start(
                    out=bass.AP(out, tt * 128 * DM, [[DM, 128], [1, DM]]),
                    in_=d_o[:, :]))
                M.set(f"d_st{tt}", m_out)
            if not dry:
                sy.wait_ge(m_out.h, m_out.n)

        def body_gpsimd(gp, dry):
            M.wait(gp, "ones_ready")
            e_gp.inc(gp.affine_select(g_ident[:, :], g_ones[:, :], [[1, 128]],
                                      ALU.is_equal, 0.0, base=0,
                                      channel_multiplier=-1))
            M.set("ident", e_gp)
            M.wait(gp, "tab_ready")
            for k in range(2):
                w, ic = UNITS[k]
                cols = 512 if w != "d" else 176
                src = {"g": idxg, "u": idxu, "d": idxd}[w]
                m_idx[k % 2].inc(gp.dma_start(
                    out=g_idx[k % 3][:, 0:cols],
                    in_=bass.AP(src, ic * 128 * cols, [[cols, 128], [1, cols]])))
                M.set(f"idx{k}", m_idx[k % 2])
            for k, (w, ic) in enumerate(UNITS):
                cols = 512 if w != "d" else 176
                M.wait(gp, f"idx{k}")
                if k >= 3:
                    M.wait(gp, f"stage_done{k - 3}")
                for j in range(cols):
                    m_g.inc(gp.indirect_dma_start(
                        g_stage[k % 3][:, j * CS:(j + 1) * CS],
                        None,
                        tab.ap(),
                        bass.IndirectOffsetOnAxis(g_idx[k % 3][:, j:j + 1], 0)))
                if not dry:
                    gp.wait_ge(m_g.h, m_g.n)
                if dry:
                    m_ck.inc(None)
                else:
                    m_ck.inc(gp.nop())
                M.set(f"gath{k}", m_ck)
                if k + 2 < len(UNITS):
                    w2, ic2 = UNITS[k + 2]
                    cols2 = 512 if w2 != "d" else 176
                    src2 = {"g": idxg, "u": idxu, "d": idxd}[w2]
                    m_idx[k % 2].inc(gp.dma_start(
                        out=g_idx[(k + 2) % 3][:, 0:cols2],
                        in_=bass.AP(src2, ic2 * 128 * cols2,
                                    [[cols2, 128], [1, cols2]])))
                    M.set(f"idx{k + 2}", m_idx[k % 2])
            for hb in range(4):
                M.wait(gp, f"rs_blk{hb}")
                m_rs.inc(gp.collective_compute(
                    "ReduceScatter", ALU.add, replica_groups=[list(range(NCORE))],
                    ins=[rsin[hb].ap().opt()],
                    outs=[rsout[hb].ap().opt()]))
            M.set("rs_done", m_rs)
            if not dry:
                gp.wait_ge(m_rs.h, m_rs.n)

        def body_vector(ve, dry):
            e = e_dve
            e.inc(ve.memset(g_zer[:, :], 0.0))
            M.set("zer_ready", e)
            e.inc(ve.memset(g_ones[:, :], 1.0))
            M.set("ones_ready", e)
            e.inc(ve.memset(g_eps[:, :], EPS))
            M.set("eps_ready", e)
            M.wait(ve, "cb_loaded")
            e.inc(ve.tensor_copy(a0_tabb[:, :], a0_tabf[:, :]))
            M.set("cb_cast", e)
            for t in range(TT64):
                xt = a0_x[t % 2]
                M.wait(ve, f"a0_x{t}")
                e.inc(ve.reduce_sum(g_s1[:, :], xt[:, :], axis=AX.X))
                M.wait(ve, f"a0_sq{t}")
                wait_self(dry, ve, e)
                e.inc(ve.tensor_scalar(g_mu[:, :], g_s1[:, :], 1.0 / DM, None,
                                       ALU.mult))
                e.inc(ve.tensor_scalar(g_var[:, :], g_s2[:, :], 1.0 / DM, None,
                                       ALU.mult))
                wait_self(dry, ve, e)
                e.inc(ve.tensor_tensor(g_mu2[:, :], g_mu[:, :], g_mu[:, :],
                                       ALU.mult))
                wait_self(dry, ve, e)
                e.inc(ve.tensor_tensor(g_var[:, :], g_var[:, :], g_mu2[:, :],
                                       ALU.subtract))
                M.set(f"a0_var{t}", e)
                M.wait(ve, f"a0_rstd{t}")
                e.inc(ve.scalar_tensor_tensor(g_nmr[:, :], g_mu[:, :], -1.0,
                                              g_rstd[:, :], ALU.mult, ALU.mult))
                wait_self(dry, ve, e)
                if t >= 1:
                    M.wait(ve, f"a0_trdone{t - 1}")
                e.inc(ve.tensor_scalar(a0_xn[:, :], xt[:, :], g_rstd[:, :],
                                       g_nmr[:, :], ALU.mult, ALU.add))
                M.set(f"a0_xn{t}", e)
                if t >= 2:
                    M.wait(ve, f"a0_wr{t - 2}")
                for grp in range(4):
                    M.wait(ve, f"a0_tr{t}_{grp}")
                    e.inc(ve.tensor_copy(a0_ev[t % 2].ap(f0=grp * 1024, fn=1024),
                                         psb(psb_dve[0] % 2, 1024)))
                    psb_evac_mark()
                M.set(f"a0_ev{t}", e)
                M.set(f"a0_done{t}", e)
                stage_at(ve, dry, "dve", "a0", t)
            for u in range(3 * EH):
                stage_at(ve, dry, "dve", "a1", u)
            for h in range(EH):
                for grp in range(2):
                    M.wait(ve, f"b_vtr{h}_{grp}")
                    e.inc(ve.tensor_copy(b_vtm.ap(f0=grp * 1024, fn=1024),
                                         psb(psb_dve[0] % 2, 1024)))
                    psb_evac_mark()
                M.set(f"b_vtm{h}", e)
                for qt in range(QT):
                    M.wait(ve, f"b_s{h}_{qt}")
                    e.inc(ve.reduce_max(g_nmx[:, :], ps(0, 2048), axis=AX.X,
                                        negate=True))
                    M.set(f"b_nmx{h}_{qt}", e)
                    M.wait(ve, f"b_exp{h}_{qt}")
                    e.inc(ve.reciprocal(g_rec[:, :], g_srow[:, :]))
                    wait_self(dry, ve, e)
                    e.inc(ve.tensor_scalar(g_diag[:, :], g_ident[:, :],
                                           g_rec[:, :], None, ALU.mult))
                    M.set(f"b_diag{h}_{qt}", e)
                    for pb in range(4):
                        M.wait(ve, f"b_pt{h}_{qt}_{pb}")
                        e.inc(ve.tensor_copy(b_pt.ap(f0=pb * 512, fn=512),
                                             ps(4, 512)))
                        M.set(f"b_ptev{h}_{qt}_{pb}", e)
                if True:
                    stage_at(ve, dry, "dve", "b", h)
            for og in range(8):
                for tg in range(4):
                    for tt in range(4):
                        M.wait(ve, f"b2_mm{og}_{tg}_{tt}")
                        li = (og * 4 + tg) * 4 + tt
                        if li >= 1:
                            pl = li - 1
                            M.wait(ve, f"b2_pw{pl // 16}_{pl % 16 // 4}_{pl % 4}")
                        e.inc(ve.tensor_copy(b2_ev[:, :], ps(tt, 512)))
                        M.set(f"b2_ev{og}_{tg}_{tt}", e)
                stage_at(ve, dry, "dve", "b2", og)
            for k in PRE_CG:
                stage_dve(ve, dry, k)
            for tsb in range(16):
                for ic in range(NIC):
                    bk = (ic % 2) * 2
                    M.wait(ve, f"c_sg{tsb}_{ic}")
                    M.wait(ve, f"c_umm{tsb}_{ic}")
                    if tsb + ic > 0:
                        pl = tsb * NIC + ic - 1
                        M.wait(ve, f"c_hh{pl // NIC}_{pl % NIC}")
                    e.inc(ve.tensor_tensor(c_hh[:, :], c_sg[:, :],
                                           ps(bk + 1, 512), ALU.mult))
                    M.set(f"c_hhv{tsb}_{ic}", e)
                    stage_at(ve, dry, "dve", "cg", tsb * NIC + ic)
            for hb in range(4):
                for j in WD_LATE.get(hb, []):
                    stage_dve(ve, dry, len(GU_UNITS) + j)
                for tt in range(TT64):
                    lin = hb * TT64 + tt
                    M.wait(ve, f"c_dmm{hb}_{tt}")
                    M.wait(ve, f"c_dr{hb}_{tt}")
                    if lin > 0:
                        pl = lin - 1
                        M.wait(ve, f"c_dw{pl // TT64}_{pl % TT64}")
                    e.inc(ve.tensor_tensor(c_ev[:, 0:1024],
                                           ps(2 * (lin % 2), 1024),
                                           c_rs[lin % 2].ap(f0=0, fn=1024),
                                           ALU.add))
                    M.set(f"c_dev{hb}_{tt}", e)
            for tt in range(8):
                M.wait(ve, f"d_ld{tt}")
                if tt >= 1:
                    M.wait(ve, f"d_st{tt - 1}")
                e.inc(ve.tensor_tensor(d_o[:, :], d_rs[:, :], d_x[:, :], ALU.add))
                M.set(f"d_o{tt}", e)

        def body_scalar(sc, dry):
            e = e_act
            for t in range(TT64):
                xt = a0_x[t % 2]
                M.wait(sc, f"a0_x{t}")
                if t >= 1:
                    M.wait(sc, f"a0_var{t - 1}")
                e.inc(sc.activation(a0_sq[:, :], xt[:, :], AF.Square,
                                    accum_out=g_s2[:, :]))
                M.set(f"a0_sq{t}", e)
                M.wait(sc, f"a0_var{t}")
                if t == 0:
                    M.wait(sc, "eps_ready")
                e.inc(sc.activation(g_rstd[:, :], g_var[:, :], AF.Ln,
                                    bias=g_eps[:, :], scale=1.0))
                wait_self(dry, sc, e)
                e.inc(sc.activation(g_rstd[:, :], g_rstd[:, :], AF.Exp,
                                    scale=-0.5))
                M.set(f"a0_rstd{t}", e)
                stage_at(sc, dry, "act", "a0", t)
            for oi in range(3):
                for ec in range(EH):
                    for half in range(2):
                        wl = (oi * EH + ec) * 2 + half
                        M.wait(sc, f"a1_w{wl}")
                        if wl >= 1:
                            M.wait(sc, f"a1_mmwb{wl - 1}")
                        e.inc(sc.copy(a1_wb.ap(f0=half * 2048, fn=2048),
                                      a1_wf[:, :]))
                        M.set(f"a1_cast{wl}", e)
                    M.wait(sc, f"a1_mm{oi}_{ec}")
                    if oi + ec > 0:
                        pl = oi * EH + ec - 1
                        M.wait(sc, f"qkv_st{pl // EH}_{pl % EH}")
                    e.inc(sc.activation(a1_out[:, :], ps(0, 2048), AF.Copy,
                                        scale=SCALE if oi == 0 else 1.0))
                    M.set(f"a1_out{oi}_{ec}", e)
                    stage_at(sc, dry, "act", "a1", oi * EH + ec)
            for h in range(EH):
                for qt in range(QT):
                    M.wait(sc, f"b_nmx{h}_{qt}")
                    e.inc(sc.activation(b_p[:, :], ps(0, 2048), AF.Exp,
                                        bias=g_nmx[:, :], scale=1.0,
                                        accum_out=g_srow[:, :]))
                    M.set(f"b_exp{h}_{qt}", e)
                    M.wait(sc, f"b_av{h}_{qt}")
                    e.inc(sc.copy(b_a.ap(f0=h * SEQ + qt * 128, fn=128),
                                  ps(5, 128)))
                    M.set(f"b_avev{h}_{qt}", e)
                    if qt == QT - 1:
                        stage_at(sc, dry, "act", "b", h)
            for og in range(8):
                M.wait(sc, f"wo{og}")
                if og >= 1:
                    M.wait(sc, f"b2_lastmm{og - 1}")
                for q in range(4):
                    e.inc(sc.copy(b2_wb.ap(f0=q * 2048, fn=2048),
                                  b2_wf.ap(f0=q * 2048, fn=2048)))
                M.set(f"b2_cast{og}", e)
                stage_at(sc, dry, "act", "b2", og)
            for k in PRE_CG:
                stage_act(sc, k)
            M.set("gu_strips0", m_sw[0])
            M.set("gu_strips1", m_sw[1])
            for tsb in range(16):
                for ic in range(NIC):
                    bk = (ic % 2) * 2
                    M.wait(sc, f"c_gmm{tsb}_{ic}")
                    e.inc(sc.activation(c_sg[:, :], ps(bk, 512), AF.Silu))
                    M.set(f"c_sg{tsb}_{ic}", e)
                    stage_at(sc, dry, "act", "cg", tsb * NIC + ic)

        def body_tensor(te, dry):
            e = e_pe
            M.wait(te, "ident")
            for t in range(TT64):
                M.wait(te, f"a0_xn{t}")
                for grp in range(4):
                    half = psb_begin(te)
                    for blk in range(8):
                        hcb = grp * 8 + blk
                        ii = te.transpose(psb(half, 128, f0=blk * 128),
                                          a0_xn.ap(f0=hcb * 128, fn=128),
                                          g_ident[:, :])
                    e.inc(ii)
                    M.set(f"a0_tr{t}_{grp}", e)
                M.set(f"a0_trdone{t}", e)
                stage_at(te, dry, "pe", "a0", t)
            M.wait(te, "xnb")
            for oi in range(3):
                for ec in range(EH):
                    if oi + ec > 0:
                        pl = oi * EH + ec - 1
                        M.wait(te, f"a1_out{pl // EH}_{pl % EH}")
                    for half in range(2):
                        wl = (oi * EH + ec) * 2 + half
                        M.wait(te, f"a1_cast{wl}")
                        for hcb in range(half * 16, half * 16 + 16):
                            for nn in range(4):
                                ii = te.matmul(
                                    ps(nn, 512),
                                    a1_wb.ap(f0=hcb * 128, fn=128),
                                    a1_xnb.ap(f0=hcb * SEQ + nn * 512, fn=512),
                                    start=(hcb == 0), stop=(hcb == HC - 1))
                        e.inc(ii)
                        M.set(f"a1_mmwb{wl}", e)
                    M.set(f"a1_mm{oi}_{ec}", e)
                    stage_at(te, dry, "pe", "a1", oi * EH + ec)
            for h in range(EH):
                M.wait(te, f"b_qkv{h}")
                for grp in range(2):
                    half = psb_begin(te)
                    for blk in range(8):
                        kc = grp * 8 + blk
                        ii = te.transpose(psb(half, 128, f0=blk * 128),
                                          b_v[h % 2].ap(f0=kc * 128, fn=128),
                                          g_ident[:, :])
                    e.inc(ii)
                    M.set(f"b_vtr{h}_{grp}", e)
                M.wait(te, f"b_vtm{h}")
                for qt in range(QT):
                    if h + qt > 0:
                        pq = (h, qt - 1) if qt else (h - 1, QT - 1)
                        M.wait(te, f"b_exp{pq[0]}_{pq[1]}")
                    else:
                        M.wait(te, f"a1_out{2}_{EH - 1}")
                    for nn in range(4):
                        ii = te.matmul(ps(nn, 512),
                                       b_q[h % 2].ap(f0=qt * 128, fn=128),
                                       b_k[h % 2].ap(f0=nn * 512, fn=512),
                                       start=True, stop=True)
                    e.inc(ii)
                    M.set(f"b_s{h}_{qt}", e)
                    M.wait(te, f"b_diag{h}_{qt}")
                    for pb in range(4):
                        if pb >= 1:
                            M.wait(te, f"b_ptev{h}_{qt}_{pb - 1}")
                        for u in range(4):
                            kc = pb * 4 + u
                            ii = te.matmul(ps(4, 128, f0=u * 128),
                                           b_p.ap(f0=kc * 128, fn=128),
                                           g_diag[:, :], start=True, stop=True)
                        e.inc(ii)
                        M.set(f"b_pt{h}_{qt}_{pb}", e)
                    M.wait(te, f"b_ptev{h}_{qt}_3")
                    if h + qt > 0:
                        pq = (h, qt - 1) if qt else (h - 1, QT - 1)
                        M.wait(te, f"b_avev{pq[0]}_{pq[1]}")
                    for kc in range(16):
                        ii = te.matmul(ps(5, 128),
                                       b_vtm.ap(f0=kc * 128, fn=128),
                                       b_pt.ap(f0=kc * 128, fn=128),
                                       start=(kc == 0), stop=(kc == 15))
                    e.inc(ii)
                    M.set(f"b_av{h}_{qt}", e)
                M.set(f"b_done{h}", e)
                stage_at(te, dry, "pe", "b", h)
            for og in range(8):
                M.wait(te, f"b2_cast{og}")
                if og == 0:
                    M.wait(te, f"b_exp{EH - 1}_{QT - 1}")
                for tg in range(4):
                    for tt in range(4):
                        if og + tg > 0:
                            ptg = (og, tg - 1) if tg else (og - 1, 3)
                            M.wait(te, f"b2_ev{ptg[0]}_{ptg[1]}_{tt}")
                        for ec in range(EH):
                            ii = te.matmul(
                                ps(tt, 512),
                                b_a.ap(f0=ec * SEQ + (tg * 4 + tt) * 128, fn=128),
                                b2_wb.ap(f0=ec * 512, fn=512),
                                start=(ec == 0), stop=(ec == EH - 1))
                        e.inc(ii)
                        M.set(f"b2_mm{og}_{tg}_{tt}", e)
                M.set(f"b2_lastmm{og}", e)
                stage_at(te, dry, "pe", "b2", og)
            for k in PRE_CG:
                stage_pe(te, k)
            for tsb in range(16):
                M.wait(te, f"c_xn{tsb}")
                for ic in range(NIC):
                    M.wait(te, f"c_w{tsb}_{ic}")
                    bk = (ic % 2) * 2
                    if tsb * NIC + ic >= 2:
                        pl = tsb * NIC + ic - 2
                        M.wait(te, f"c_hhv{pl // NIC}_{pl % NIC}")
                    else:
                        M.wait(te, "b2_ev7_3_3")
                    for op in range(2):
                        wbuf = (c_wg if op == 0 else c_wu)[ic % 2]
                        for hcb in range(HC):
                            ii = te.matmul(ps(bk + op, 512),
                                           wbuf.ap(f0=hcb * 128, fn=128),
                                           c_xn.ap(f0=hcb * 512, fn=512),
                                           start=(hcb == 0), stop=(hcb == HC - 1))
                        e.inc(ii)
                        if op == 0:
                            M.set(f"c_gmm{tsb}_{ic}", e)
                    M.set(f"c_umm{tsb}_{ic}", e)
                    stage_at(te, dry, "pe", "cg", tsb * NIC + ic)
            M.wait(te, f"c_hhv{15}_{NIC - 1}")
            for hb in range(4):
                for j in WD_LATE.get(hb, []):
                    stage_pe(te, len(GU_UNITS) + j)
                M.wait(te, f"wd_ev{len(GU_UNITS) + 8 * (hb + 1) - 1}")
                for tt in range(TT64):
                    lin = hb * TT64 + tt
                    M.wait(te, f"c_dl{hb}_{tt}")
                    if lin >= 2:
                        pl = lin - 2
                        M.wait(te, f"c_dev{pl // TT64}_{pl % TT64}")
                    for ic in range(NIC):
                        for nn in range(2):
                            ii = te.matmul(
                                ps(2 * (lin % 2) + nn, 512),
                                c_ht[lin % 2].ap(f0=ic * 128, fn=128),
                                bass.AP(c_wdT.h,
                                        c_wdT.c0 + ic * 4096 + hb * 1024 + nn * 512,
                                        [[c_wdT.W, 128], [1, 512]]),
                                start=(ic == 0), stop=(ic == NIC - 1))
                    e.inc(ii)
                    M.set(f"c_dmm{hb}_{tt}", e)

        M.dry = True
        dz = DryEng()
        psb_pe[0] = 0
        psb_dve[0] = 0
        for s in sems:
            s.n = 0
        body_vector(dz, True)
        body_scalar(dz, True)
        body_tensor(dz, True)
        body_gpsimd(dz, True)
        body_sync(dz, True)

        M.dry = False
        psb_pe[0] = 0
        psb_dve[0] = 0
        for s in sems:
            s.n = 0
            s.h = E(nc.semaphore(s.name))

        block = E(nc.Block())

        @block.vector
        def _(ve):
            body_vector(ve, False)

        @block.scalar
        def _(sc):
            body_scalar(sc, False)

        @block.tensor
        def _(te):
            body_tensor(te, False)

        @block.gpsimd
        def _(gp):
            body_gpsimd(gp, False)

        @block.sync
        def _(sy):
            body_sync(sy, False)

    return nc


def kernel(L0_attn_x, L0_attn_w_q, L0_attn_w_k, L0_attn_w_v, L0_attn_w_o,
           L0_ffn_codebook, L0_ffn_idx_gate, L0_ffn_idx_up, L0_ffn_idx_down):
    x = np.ascontiguousarray(np.asarray(L0_attn_x, dtype=np.float32)).reshape(TOK, DM)
    wq_ = np.asarray(L0_attn_w_q, np.float32)
    wk_ = np.asarray(L0_attn_w_k, np.float32)
    wv_ = np.asarray(L0_attn_w_v, np.float32)
    wo_ = np.asarray(L0_attn_w_o, np.float32)
    cb = np.ascontiguousarray(np.asarray(L0_ffn_codebook, np.float32))
    ig = np.asarray(L0_ffn_idx_gate, np.int32)
    iu = np.asarray(L0_ffn_idx_up, np.int32)
    idn = np.asarray(L0_ffn_idx_down, np.int32)

    def blk_w(w, s):
        wt = w.T[:, s * ES:(s + 1) * ES]
        return np.ascontiguousarray(
            wt.reshape(HC, 128, EH, 128).transpose(2, 0, 1, 3))

    def blk_wo(s):
        wt = wo_.T[s * ES:(s + 1) * ES, :]
        return np.ascontiguousarray(
            wt.reshape(EH, 128, 8, 512).transpose(2, 0, 1, 3))

    def shard_gu(idx, c):
        sh = idx[c * IRE:(c + 1) * IRE]
        sh = np.concatenate([sh, np.zeros((ISH - IRE, 512), np.int32)], 0)
        return np.ascontiguousarray(sh.reshape(NIC, 128, 512))

    def shard_d(c):
        sh = idn[:, c * (IRE // CS):(c + 1) * (IRE // CS)]
        sh = np.concatenate([sh, np.zeros((DM, (ISH - IRE) // CS), np.int32)], 1)
        return np.ascontiguousarray(sh.reshape(HC, 128, 176))

    in_maps = []
    for c in range(NCORE):
        s = c % 2
        in_maps.append({
            "xfull": x,
            "xres": np.ascontiguousarray(x[c * HALF:(c + 1) * HALF]),
            "wq": blk_w(wq_, s), "wk": blk_w(wk_, s), "wv": blk_w(wv_, s),
            "wo": blk_wo(s),
            "idxg": shard_gu(ig, c), "idxu": shard_gu(iu, c), "idxd": shard_d(c),
            "cbt": cb,
        })

    nc = build_program()
    res = run_bass_kernel_spmd(nc, in_maps, core_ids=list(range(NCORE)))
    outp = np.empty((TOK, DM), np.float32)
    for c in range(NCORE):
        outp[c * HALF:(c + 1) * HALF] = res.results[c]["o"]
    return outp.reshape(BATCH, SEQ, DM)


# version 15
# speedup vs baseline: 1.1875x; 1.0152x over previous
import os
import numpy as np
import concourse.bass as bass
import concourse.mybir as mybir
from concourse.bass_utils import run_bass_kernel_spmd
from contextlib import ExitStack

F32 = mybir.dt.float32
BF16 = mybir.dt.bfloat16
I32 = mybir.dt.int32
AF = mybir.ActivationFunctionType
ALU = mybir.AluOpType
AX = mybir.AxisListType

BATCH, SEQ, DM = 4, 2048, 4096
HEADS, DK = 32, 128
CB, CS = 65536, 8
TOK = BATCH * SEQ
NCORE = 8
EH, ES = 16, 2048
ISH, NIC, IRE = 1408, 11, 1376
TT64 = TOK // 128
QT = SEQ // 128
HC = DM // 128
HALF = SEQ // 2
SCALE = float(1.0 / np.sqrt(DK))
EPS = 1e-5

GU_UNITS = [(w, ic) for ic in range(NIC) for w in ("g", "u")]
WD_UNITS = [("d", hc) for hc in range(HC)]
UNITS = GU_UNITS + WD_UNITS


class SemC:
    def __init__(self, name, step=1):
        self.name, self.step, self.n, self.h = name, step, 0, None

    def inc(self, instr, k=1):
        if instr is not None:
            instr.then_inc(self.h, self.step * k)
        self.n += self.step * k
        return instr


class Marks:
    def __init__(self):
        self.d = {}
        self.dry = True

    def set(self, name, sem):
        if self.dry:
            self.d[name] = (sem, sem.n)
        else:
            assert name in self.d and self.d[name] == (sem, sem.n), \
                (name, self.d.get(name), sem.name, sem.n)

    def alias(self, name, src):
        if self.dry:
            self.d[name] = self.d[src]

    def wait(self, eng, name):
        if self.dry:
            return
        sem, v = self.d[name]
        if v > 0:
            eng.wait_ge(sem.h, v)


class DryInstr:
    def then_inc(self, *a):
        return self


class DryEng:
    def __getattr__(self, name):
        def f(*a, **k):
            return DryInstr()
        return f


def build_program():
    nc = bass.Bass("TRN2", target_bir_lowering=False)

    xfull = nc.dram_tensor("xfull", [TOK, DM], F32, kind="ExternalInput")
    xres = nc.dram_tensor("xres", [HALF, DM], F32, kind="ExternalInput")
    wq = nc.dram_tensor("wq", [EH, HC, 128, 128], F32, kind="ExternalInput")
    wk = nc.dram_tensor("wk", [EH, HC, 128, 128], F32, kind="ExternalInput")
    wv = nc.dram_tensor("wv", [EH, HC, 128, 128], F32, kind="ExternalInput")
    wo = nc.dram_tensor("wo", [8, EH, 128, 512], F32, kind="ExternalInput")
    idxg = nc.dram_tensor("idxg", [NIC, 128, 512], I32, kind="ExternalInput")
    idxu = nc.dram_tensor("idxu", [NIC, 128, 512], I32, kind="ExternalInput")
    idxd = nc.dram_tensor("idxd", [HC, 128, 176], I32, kind="ExternalInput")
    cbt = nc.dram_tensor("cbt", [CB, CS], F32, kind="ExternalInput")
    out = nc.dram_tensor("o", [HALF, DM], F32, kind="ExternalOutput")

    tab = nc.dram_tensor("tab", [CB, CS], BF16)
    raw = nc.dram_tensor("raw", [len(WD_UNITS), 128, 1408], BF16)
    xnd = nc.dram_tensor("xnd", [128, HC * TOK], BF16)
    qd = nc.dram_tensor("qd", [EH, 128, SEQ], BF16)
    kd = nc.dram_tensor("kd", [EH, 128, SEQ], BF16)
    vd = nc.dram_tensor("vd", [EH, 128, SEQ], BF16)
    wgT = nc.dram_tensor("wgT", [NIC, HC, 128, 128], BF16)
    wuT = nc.dram_tensor("wuT", [NIC, HC, 128, 128], BF16)
    hhd = nc.dram_tensor("hhd", [NIC, 128, TOK], BF16)
    attnd = nc.dram_tensor("attnd", [SEQ, DM], F32)
    rsin = [nc.dram_tensor(f"rsin{j}", [TOK, 1024], F32) for j in range(4)]
    rsout = [nc.dram_tensor(f"rsout{j}", [HALF, 1024], F32) for j in range(4)]
    DBG = bool(os.environ.get("MK_DEBUG"))
    if DBG:
        dbg_xn = nc.dram_tensor("dbg_xn", [128, 4096], BF16, kind="ExternalOutput")
        dbg_q = nc.dram_tensor("dbg_q", [128, SEQ], BF16, kind="ExternalOutput")
        dbg_k = nc.dram_tensor("dbg_k", [128, SEQ], BF16, kind="ExternalOutput")
        dbg_v = nc.dram_tensor("dbg_v", [128, SEQ], BF16, kind="ExternalOutput")
        dbg_at = nc.dram_tensor("dbg_at", [256, DM], F32, kind="ExternalOutput")
        dbg_wg = nc.dram_tensor("dbg_wg", [HC * 128, 128], BF16, kind="ExternalOutput")
        dbg_hh = nc.dram_tensor("dbg_hh", [128, 2048], BF16, kind="ExternalOutput")
        dbg_rs = nc.dram_tensor("dbg_rs", [256, DM], F32, kind="ExternalOutput")
        dbg_tab = nc.dram_tensor("dbg_tab", [1024, CS], BF16, kind="ExternalOutput")
        dbg_a = nc.dram_tensor("dbg_a", [128, EH * SEQ], BF16, kind="ExternalOutput")

    ctx = ExitStack()
    with ctx:
        E = ctx.enter_context
        AB = 204 * 1024
        arena = E(nc.sbuf_tensor("arena", [128, AB], mybir.dt.uint8))
        hf, hb, hi = arena.bitcast(F32), arena.bitcast(BF16), arena.bitcast(I32)
        WF, WB = AB // 4, AB // 2

        class TV:
            def __init__(self, h, W, c0, cols):
                self.h, self.W, self.c0, self.cols = h, W, c0, cols

            def ap(self, p0=0, pn=128, f0=0, fn=None, extra=None):
                fn = self.cols - f0 if fn is None else fn
                pat = [[self.W, pn]] + (extra if extra else [[1, fn]])
                return bass.AP(self.h, p0 * self.W + self.c0 + f0, pat)

            def __getitem__(self, key):
                ps_, fs = key
                p0 = ps_.start or 0
                pn = (128 if ps_.stop is None else ps_.stop) - p0
                f0 = fs.start or 0
                fn = (self.cols if fs.stop is None else fs.stop) - f0
                return self.ap(p0, pn, f0, fn)

        off = [0]

        def alloc(dt, cols, align=512):
            if cols <= 128:
                align = 64
            esz = 4 if dt in (F32, I32) else 2
            off[0] = (off[0] + align - 1) // align * align
            base = off[0]
            off[0] += cols * esz
            assert off[0] <= AB, f"arena overflow {off[0]}"
            h = hf if dt is F32 else hi if dt is I32 else hb
            return TV(h, WF if esz == 4 else WB, base // esz, cols)

        g_ident = alloc(BF16, 128)
        g_ones = alloc(BF16, 128)
        g_zer = alloc(F32, 2048)
        g_stage = [alloc(BF16, 4096) for _ in range(3)]
        g_stage2 = [alloc(BF16, 1408) for _ in range(2)]
        g_evac = [alloc(BF16, 1024) for _ in range(2)]
        g_idx = [alloc(I32, 512) for _ in range(3)]
        g_s1 = alloc(F32, 1)
        g_s2 = alloc(F32, 1)
        g_mu = alloc(F32, 1)
        g_var = alloc(F32, 1)
        g_mu2 = alloc(F32, 1)
        g_rstd = alloc(F32, 1)
        g_nmr = alloc(F32, 1)
        g_nmx = alloc(F32, 1)
        g_srow = alloc(F32, 1)
        g_rec = alloc(F32, 1)
        g_diag = alloc(BF16, 128)
        g_eps = alloc(F32, 1)
        glob_end = off[0]

        off[0] = glob_end
        a0_x = [alloc(F32, 4096), alloc(F32, 4096)]
        a0_xn = alloc(BF16, 4096)
        a0_ev = [alloc(BF16, 4096), alloc(BF16, 4096)]
        a0_sq = alloc(F32, 4096)
        a0_tabf = alloc(F32, 4096)
        a0_tabb = alloc(BF16, 4096)
        off[0] = glob_end
        a1_xnb = alloc(BF16, HC * SEQ)
        a1_wf = alloc(F32, 2048)
        a1_wb = alloc(BF16, HC * 128)
        a1_out = alloc(BF16, SEQ)
        off[0] = glob_end
        b_a = alloc(BF16, EH * SEQ)
        b_pt = alloc(BF16, 16 * 128)
        b_q = [alloc(BF16, SEQ), alloc(BF16, SEQ)]
        b_k = [alloc(BF16, SEQ), alloc(BF16, SEQ)]
        b_v = [alloc(BF16, SEQ), alloc(BF16, SEQ)]
        b_vtm = alloc(BF16, SEQ)
        b_p = alloc(BF16, SEQ)
        b_end = off[0]
        off[0] = (b_a.c0 + b_a.cols) * 2
        b2_wf = alloc(F32, EH * 512)
        b2_wb = alloc(BF16, EH * 512)
        b2_ev = alloc(F32, 512)
        off[0] = glob_end
        c_wdT = alloc(BF16, NIC * 4096)
        c_xn = alloc(BF16, HC * 512)
        _cw = alloc(BF16, HC * 128)
        _cu = alloc(BF16, HC * 128)
        c_wg = [_cw, _cw]
        c_wu = [_cu, _cu]
        c_sg = alloc(BF16, 512)
        c_hh = alloc(BF16, 512)
        c_gu_end = off[0]
        off[0] = c_wdT.c0 * 2 + c_wdT.cols * 2
        c_ht = [alloc(BF16, NIC * 128), alloc(BF16, NIC * 128)]
        c_rs = [alloc(F32, 4096), alloc(F32, 4096)]
        c_ev = alloc(F32, 2048)
        off[0] = (c_wdT.c0 + c_wdT.cols) * 2
        d_rs = alloc(F32, 4096)
        d_x = alloc(F32, 4096)
        d_o = alloc(F32, 4096)
        assert off[0] <= AB, off[0]

        ps_main = E(nc.psum_tensor("ps_main", [128, 3072], F32))
        ps_stg = E(nc.psum_tensor("ps_stg", [128, 2048], BF16))
        PS_W, PSB_W = 3072, 2048

        def ps(bank, cols, f0=0):
            return bass.AP(ps_main, bank * 512 + f0, [[PS_W, 128], [1, cols]])

        def psb(half, cols, f0=0):
            return bass.AP(ps_stg, half * 1024 + f0, [[PSB_W, 128], [1, cols]])

        sems = []

        def S(name, step=1):
            s = SemC(name, step)
            sems.append(s)
            return s

        m_tab = S("m_tab", 16)
        m_idx = [S("m_idx0", 16), S("m_idx1", 16)]
        m_g = S("m_g", 16)
        m_ck = S("m_ck", 1)
        m_raw3 = [S(f"m_raw{j}", 16) for j in range(3)]
        m_s2 = [S("m_s2a", 16), S("m_s2b", 16)]
        m_x = [S("m_x0", 16), S("m_x1", 16)]
        m_xw = [S("m_xw0", 16), S("m_xw1", 16)]
        m_xnb = S("m_xnb", 16)
        m_w = S("m_w", 16)
        m_qw = S("m_qw", 16)
        m_qr = [S("m_qr0", 16), S("m_qr1", 16)]
        m_wo = S("m_wo", 16)
        m_z = S("m_z", 16)
        m_pw = S("m_pw", 16)
        m_cp = S("m_cp", 16)
        m_sw = [S("m_sw0", 16), S("m_sw1", 16)]
        m_cxn = S("m_cxn", 16)
        m_cw = S("m_cw", 16)
        m_dh = [S("m_dh0", 16), S("m_dh1", 16)]
        m_hw = S("m_hw", 16)
        m_rr = [S("m_rr0", 16), S("m_rr1", 16)]
        m_rw = S("m_rw", 16)
        m_rwc = [S(f"m_rwc{j}", 16) for j in range(4)]
        m_rs = S("m_rs", 1)
        m_do = S("m_do", 16)
        m_out = S("m_out", 16)
        e_pe = S("e_pe", 1)
        e_dve = S("e_dve", 1)
        e_act = S("e_act", 1)
        e_gp = S("e_gp", 1)

        M = Marks()

        pe_positions = []
        for t in range(TT64):
            pe_positions.append(("a0", t, 14.0))
        for u in range(3 * EH):
            pe_positions.append(("a1", u, 72.0))
        for h in range(EH):
            pe_positions.append(("b", h, 240.0))
        for og in range(8):
            pe_positions.append(("b2", og, 140.0))
        for lin in range(16 * NIC):
            pe_positions.append(("cg", lin, 15.0))
        g_done = []
        acc = 250.0
        for (w, ic) in UNITS:
            acc += (512 if w != "d" else 176) * 1.12 + 30.0
            g_done.append(acc)
        ASSIGN = {}
        ptime = 0.0
        k = 0
        NGU = len(GU_UNITS)
        for (ph, idx, dt) in pe_positions:
            if ph == "cg":
                break
            ptime += dt
            while k < NGU:
                if ptime < g_done[k] or (ph == "b2" and idx == 7):
                    break
                ASSIGN.setdefault((ph, idx), []).append(k)
                k += 1
        PRE_CG = list(range(k, NGU))

        psb_pe = [0]
        psb_dve = [0]

        def psb_begin(te):
            i = psb_pe[0]
            if i >= 2:
                M.wait(te, f"psbe{i - 2}")
            psb_pe[0] += 1
            return i % 2

        def psb_evac_mark():
            i = psb_dve[0]
            M.set(f"psbe{i}", e_dve)
            psb_dve[0] += 1

        def wait_self(dry, eng, sem):
            if not dry:
                eng.wait_ge(sem.h, sem.n)

        def stage_pe(te, k):
            w, ic = UNITS[k]
            if w == "d":
                M.wait(te, f"wd_pad{k}")
                for grp in range(2):
                    n = 8 if grp == 0 else 3
                    half = psb_begin(te)
                    for blk in range(n):
                        icc = grp * 8 + blk
                        ii = te.transpose(psb(half, 128, f0=blk * 128),
                                          g_stage2[(k - len(GU_UNITS)) % 2].ap(f0=icc * 128, fn=128),
                                          g_ident[:, :])
                    e_pe.inc(ii)
                    M.set(f"st_tr{k}_{grp}", e_pe)
                M.set(f"st_trdone{k}", e_pe)
            else:
                M.wait(te, f"gath{k}")
                for grp in range(4):
                    half = psb_begin(te)
                    for blk in range(8):
                        hcb = grp * 8 + blk
                        ii = te.transpose(psb(half, 128, f0=blk * 128),
                                          g_stage[k % 3].ap(f0=hcb * 128, fn=128),
                                          g_ident[:, :])
                    e_pe.inc(ii)
                    M.set(f"st_tr{k}_{grp}", e_pe)

        def stage_dve(ve, dry, k):
            e = e_dve
            w, ic = UNITS[k]
            if w == "d":
                M.wait(ve, f"s2_{k}")
                e.inc(ve.memset(g_stage2[(k - len(GU_UNITS)) % 2].ap(f0=1376, fn=32), 0.0))
                M.set(f"wd_pad{k}", e)
                for grp in range(2):
                    n = 8 if grp == 0 else 3
                    M.wait(ve, f"st_tr{k}_{grp}")
                    e.inc(ve.tensor_copy(
                        bass.AP(c_wdT.h, c_wdT.c0 + (grp * 8) * 4096 + ic * 128,
                                [[c_wdT.W, 128], [4096, n], [1, 128]]),
                        psb(psb_dve[0] % 2, n * 128)))
                    psb_evac_mark()
                M.set(f"wd_ev{k}", e)
            else:
                dstt = wgT if w == "g" else wuT
                for grp in range(4):
                    j = k * 4 + grp
                    M.wait(ve, f"st_tr{k}_{grp}")
                    if j >= 2:
                        pl = j - 2
                        M.wait(ve, f"st_wr{pl // 4}_{pl % 4}")
                    e.inc(ve.tensor_copy(g_evac[grp % 2][:, :],
                                         psb(psb_dve[0] % 2, 1024)))
                    psb_evac_mark()
                    M.set(f"st_ev{k}_{grp}", e)
            M.set(f"stage_done{k}", e)

        def stage_act(sc, k):
            w, ic = UNITS[k]
            if w == "d":
                kk = k - len(GU_UNITS)
                M.wait(sc, f"rawout{k}")
                if kk >= 2:
                    M.wait(sc, f"st_trdone{k - 2}")
                m_s2[kk % 2].inc(sc.dma_start(
                    out=g_stage2[kk % 2][:, :],
                    in_=bass.AP(raw, kk * 128 * 1408, [[1408, 128], [1, 1408]])))
                M.set(f"s2_{k}", m_s2[kk % 2])
                return
            dstt = wgT if w == "g" else wuT
            for grp in range(4):
                j = k * 4 + grp
                M.wait(sc, f"st_ev{k}_{grp}")
                m_sw[j % 2].inc(sc.dma_start(
                    out=bass.AP(dstt, (ic * HC + grp * 8) * 16384,
                                [[128, 128], [16384, 8], [1, 128]]),
                    in_=g_evac[grp % 2].ap(extra=[[128, 8], [1, 128]])))
                M.set(f"st_wr{k}_{grp}", m_sw[j % 2])

        def stage_at(eng, dry, which, ph, idx):
            for k in ASSIGN.get((ph, idx), []):
                if which == "pe":
                    stage_pe(eng, k)
                elif which == "act":
                    stage_act(eng, k)
                else:
                    stage_dve(eng, dry, k)

        def body_sync(sy, dry):
            m_tab.inc(sy.dma_start(
                out=a0_tabf[:, :],
                in_=cbt.ap().rearrange("(p a) b -> p (a b)", p=128)))
            M.set("cb_loaded", m_tab)
            M.wait(sy, "cb_cast")
            m_tab.inc(sy.dma_start(
                out=tab.ap().rearrange("(p a) b -> p (a b)", p=128),
                in_=a0_tabb[:, :]))
            M.set("tab_ready", m_tab)
            M.wait(sy, "zer_ready")
            for r in range(TT64):
                for j in range(4):
                    m_z.inc(sy.dma_start(
                        out=bass.AP(rsin[j], r * 128 * 1024, [[1024, 128], [1, 1024]]),
                        in_=g_zer[:, 0:1024]))
            M.set("zero_done", m_z)
            for t in range(2):
                m_x[t % 2].inc(sy.dma_start(
                    out=a0_x[t][:, :],
                    in_=bass.AP(xfull, t * 128 * DM, [[DM, 128], [1, DM]])))
                M.set(f"a0_x{t}", m_x[t % 2])
            for t in range(TT64):
                if t + 2 < TT64:
                    M.wait(sy, f"a0_done{t}")
                    m_x[t % 2].inc(sy.dma_start(
                        out=a0_x[t % 2][:, :],
                        in_=bass.AP(xfull, (t + 2) * 128 * DM, [[DM, 128], [1, DM]])))
                    M.set(f"a0_x{t + 2}", m_x[t % 2])
                M.wait(sy, f"a0_ev{t}")
                for q4 in range(4):
                    m_xw[t % 2].inc(sy.dma_start(
                        out=bass.AP(xnd, (q4 * 8) * TOK + t * 128,
                                    [[HC * TOK, 128], [TOK, 8], [1, 128]]),
                        in_=a0_ev[t % 2].ap(f0=q4 * 1024, extra=[[128, 8], [1, 128]])))
                M.set(f"a0_wr{t}", m_xw[t % 2])
            M.set("xn_all0", m_xw[0])
            M.set("xn_all1", m_xw[1])
            M.wait(sy, "xn_all0")
            M.wait(sy, "xn_all1")
            if dry:
                m_xnb.inc(None, 4)
            else:
                with sy.register("rb") as rb:
                    pid = sy.partition_id()
                    sy.reg_alu(rb, pid, 1, ALU.logical_shift_right)
                    sy.reg_alu(rb, rb, SEQ, ALU.mult)
                    with sy.register("rb2") as rb2:
                        for q4 in range(4):
                            sy.reg_alu(rb2, rb, (q4 * 8) * TOK, ALU.add)
                            m_xnb.inc(sy.dma_start(
                                out=a1_xnb.ap(f0=q4 * 8 * SEQ,
                                              extra=[[SEQ, 8], [1, SEQ]]),
                                in_=bass.AP(xnd, rb2,
                                            [[HC * TOK, 128], [TOK, 8], [1, SEQ]])))
            M.set("xnb", m_xnb)
            for oi in range(3):
                wsrc = (wq, wk, wv)[oi]
                for ec in range(EH):
                    for half in range(2):
                        wl = (oi * EH + ec) * 2 + half
                        if wl > 0:
                            M.wait(sy, f"a1_cast{wl - 1}")
                        m_w.inc(sy.dma_start(
                            out=a1_wf[:, :],
                            in_=bass.AP(wsrc, (ec * HC + half * 16) * 16384,
                                        [[128, 128], [16384, 16], [1, 128]])))
                        M.set(f"a1_w{wl}", m_w)
                    M.wait(sy, f"a1_out{oi}_{ec}")
                    m_qw.inc(sy.dma_start(
                        out=bass.AP((qd, kd, vd)[oi], ec * 128 * SEQ,
                                    [[SEQ, 128], [1, SEQ]]),
                        in_=a1_out[:, :]))
                    M.set(f"qkv_st{oi}_{ec}", m_qw)
            M.set("qkv_all", m_qw)
            M.wait(sy, "qkv_all")
            for h in range(EH):
                if h >= 2:
                    M.wait(sy, f"b_done{h - 2}")
                for oi, src in enumerate((qd, kd, vd)):
                    m_qr[h % 2].inc(sy.dma_start(
                        out=(b_q, b_k, b_v)[oi][h % 2][:, :],
                        in_=bass.AP(src, h * 128 * SEQ, [[SEQ, 128], [1, SEQ]])))
                M.set(f"b_qkv{h}", m_qr[h % 2])
            M.wait(sy, f"b_done{EH - 1}")
            for og in range(8):
                if og >= 1:
                    M.wait(sy, f"b2_cast{og - 1}")
                m_wo.inc(sy.dma_start(
                    out=b2_wf[:, :],
                    in_=bass.AP(wo, og * EH * 65536,
                                [[512, 128], [65536, EH], [1, 512]])))
                M.set(f"wo{og}", m_wo)
                for tg in range(4):
                    for tt in range(4):
                        M.wait(sy, f"b2_ev{og}_{tg}_{tt}")
                        m_pw.inc(sy.dma_start(
                            out=bass.AP(attnd, (tg * 4 + tt) * 128 * DM + og * 512,
                                        [[DM, 128], [1, 512]]),
                            in_=b2_ev[:, :]))
                        M.set(f"b2_pw{og}_{tg}_{tt}", m_pw)
            if not dry:
                sy.wait_ge(m_pw.h, m_pw.n)
            M.wait(sy, "zero_done")
            if dry:
                m_cp.inc(None, 4)
            else:
                with sy.register("ro") as ro:
                    pid2 = sy.partition_id()
                    sy.reg_alu(ro, pid2, 1, ALU.logical_shift_right)
                    sy.reg_alu(ro, ro, SEQ * 1024, ALU.mult)
                    for j in range(4):
                        m_cp.inc(sy.dma_start(
                            out=bass.AP(rsin[j], ro, [[1024, SEQ], [1, 1024]]),
                            in_=bass.AP(attnd, j * 1024, [[DM, SEQ], [1, 1024]])))
            M.set("part_done", m_cp)
            if DBG:
                m_cp.inc(sy.dma_start(out=dbg_a.ap(), in_=b_a[:, :]))
                if not dry:
                    sy.wait_ge(m_cp.h, m_cp.n)
            M.wait(sy, "gu_strips0")
            M.wait(sy, "gu_strips1")
            for tsb in range(16):
                if tsb >= 1:
                    M.wait(sy, f"c_umm{tsb - 1}_{NIC - 1}")
                for q4 in range(4):
                    m_cxn.inc(sy.dma_start(
                        out=c_xn.ap(f0=q4 * 8 * 512, extra=[[512, 8], [1, 512]]),
                        in_=bass.AP(xnd, (q4 * 8) * TOK + tsb * 512,
                                    [[HC * TOK, 128], [TOK, 8], [1, 512]])))
                M.set(f"c_xn{tsb}", m_cxn)
                for ic in range(NIC):
                    if tsb * NIC + ic >= 1:
                        pl = tsb * NIC + ic - 1
                        M.wait(sy, f"c_hh{pl // NIC}_{pl % NIC}")
                    m_cw.inc(sy.dma_start(
                        out=c_wg[ic % 2][:, :],
                        in_=bass.AP(wgT, ic * HC * 16384,
                                    [[128, 128], [16384, HC], [1, 128]])))
                    m_cw.inc(sy.dma_start(
                        out=c_wu[ic % 2][:, :],
                        in_=bass.AP(wuT, ic * HC * 16384,
                                    [[128, 128], [16384, HC], [1, 128]])))
                    M.set(f"c_w{tsb}_{ic}", m_cw)
                    M.wait(sy, f"c_hhv{tsb}_{ic}")
                    m_hw.inc(sy.dma_start(
                        out=bass.AP(hhd, ic * 128 * TOK + tsb * 512,
                                    [[TOK, 128], [1, 512]]),
                        in_=c_hh[:, :]))
                    M.set(f"c_hh{tsb}_{ic}", m_hw)
            M.set("hh_all", m_hw)
            M.wait(sy, "hh_all")
            M.wait(sy, "part_done")
            M.wait(sy, "zero_done")
            for hb in range(4):
                for tt in range(TT64):
                    lin = hb * TT64 + tt
                    if lin >= 2:
                        pl = lin - 2
                        M.wait(sy, f"c_dev{pl // TT64}_{pl % TT64}")
                    m_dh[lin % 2].inc(sy.dma_start(
                        out=c_ht[lin % 2].ap(extra=[[128, NIC], [1, 128]]),
                        in_=bass.AP(hhd, tt * 128,
                                    [[TOK, 128], [128 * TOK, NIC], [1, 128]])))
                    M.set(f"c_dl{hb}_{tt}", m_dh[lin % 2])
                    m_rr[lin % 2].inc(sy.dma_start(
                        out=c_rs[lin % 2][:, 0:1024],
                        in_=bass.AP(rsin[hb], tt * 128 * 1024,
                                    [[1024, 128], [1, 1024]])))
                    M.set(f"c_dr{hb}_{tt}", m_rr[lin % 2])
                    M.wait(sy, f"c_dev{hb}_{tt}")
                    m_rwc[hb].inc(sy.dma_start(
                        out=bass.AP(rsin[hb], tt * 128 * 1024,
                                    [[1024, 128], [1, 1024]]),
                        in_=c_ev[:, 0:1024]))
                    M.set(f"c_dw{hb}_{tt}", m_rwc[hb])
                M.set(f"rs_blk{hb}", m_rwc[hb])
            if DBG:
                m_do.inc(sy.dma_start(out=dbg_xn.ap(), in_=bass.AP(xnd, 0, [[HC * TOK, 128], [1, 4096]])))
                m_do.inc(sy.dma_start(out=dbg_q.ap(), in_=bass.AP(qd, 0, [[SEQ, 128], [1, SEQ]])))
                m_do.inc(sy.dma_start(out=dbg_k.ap(), in_=bass.AP(kd, 0, [[SEQ, 128], [1, SEQ]])))
                m_do.inc(sy.dma_start(out=dbg_v.ap(), in_=bass.AP(vd, 0, [[SEQ, 128], [1, SEQ]])))
                m_do.inc(sy.dma_start(out=dbg_at.ap(), in_=bass.AP(attnd, 0, [[DM, 256], [1, DM]])))
                m_do.inc(sy.dma_start(out=dbg_wg.ap(), in_=bass.AP(wgT, 0, [[128, HC * 128], [1, 128]])))
                m_do.inc(sy.dma_start(out=dbg_hh.ap(), in_=bass.AP(hhd, 0, [[TOK, 128], [1, 2048]])))
                for jj in range(4):
                    m_do.inc(sy.dma_start(
                        out=bass.AP(dbg_rs.h if hasattr(dbg_rs, "h") else dbg_rs,
                                    jj * 1024, [[DM, 256], [1, 1024]]),
                        in_=bass.AP(rsin[jj], 0, [[1024, 256], [1, 1024]])))
                m_do.inc(sy.dma_start(out=dbg_tab.ap(), in_=bass.AP(tab, 0, [[CS, 1024], [1, CS]])))
            M.wait(sy, "rs_done")
            for tt in range(8):
                if tt >= 1:
                    M.wait(sy, f"d_o{tt - 1}")
                for j in range(4):
                    m_do.inc(sy.dma_start(
                        out=d_rs.ap(extra=[[1024, 1], [1, 1024]], f0=j * 1024)
                        if False else
                        bass.AP(d_rs.h, d_rs.c0 + j * 1024, [[d_rs.W, 128], [1, 1024]]),
                        in_=bass.AP(rsout[j], tt * 128 * 1024, [[1024, 128], [1, 1024]])))
                m_do.inc(sy.dma_start(
                    out=d_x[:, :],
                    in_=bass.AP(xres, tt * 128 * DM, [[DM, 128], [1, DM]])))
                M.set(f"d_ld{tt}", m_do)
                M.wait(sy, f"d_o{tt}")
                m_out.inc(sy.dma_start(
                    out=bass.AP(out, tt * 128 * DM, [[DM, 128], [1, DM]]),
                    in_=d_o[:, :]))
                M.set(f"d_st{tt}", m_out)
            if not dry:
                sy.wait_ge(m_out.h, m_out.n)

        def body_gpsimd(gp, dry):
            M.wait(gp, "ones_ready")
            e_gp.inc(gp.affine_select(g_ident[:, :], g_ones[:, :], [[1, 128]],
                                      ALU.is_equal, 0.0, base=0,
                                      channel_multiplier=-1))
            M.set("ident", e_gp)
            M.wait(gp, "tab_ready")
            for k in range(2):
                w, ic = UNITS[k]
                cols = 512 if w != "d" else 176
                src = {"g": idxg, "u": idxu, "d": idxd}[w]
                m_idx[k % 2].inc(gp.dma_start(
                    out=g_idx[k % 3][:, 0:cols],
                    in_=bass.AP(src, ic * 128 * cols, [[cols, 128], [1, cols]])))
                M.set(f"idx{k}", m_idx[k % 2])
            NGU = len(GU_UNITS)
            for k, (w, ic) in enumerate(UNITS):
                cols = 512 if w != "d" else 176
                M.wait(gp, f"idx{k}")
                if k >= 3:
                    prev = k - 3
                    if prev < NGU:
                        M.wait(gp, f"stage_done{prev}")
                    elif not dry:
                        kkp = prev - NGU
                        gp.wait_ge(m_raw3[kkp % 3].h, (kkp // 3 + 1) * 16)
                for j in range(cols):
                    m_g.inc(gp.indirect_dma_start(
                        g_stage[k % 3][:, j * CS:(j + 1) * CS],
                        None,
                        tab.ap(),
                        bass.IndirectOffsetOnAxis(g_idx[k % 3][:, j:j + 1], 0)))
                if not dry:
                    gp.wait_ge(m_g.h, m_g.n)
                if w == "d":
                    kkw = k - NGU
                    m_raw3[kkw % 3].inc(gp.dma_start(
                        out=bass.AP(raw, kkw * 128 * 1408,
                                    [[1408, 128], [1, 1408]]),
                        in_=g_stage[k % 3][:, 0:1408]))
                    M.set(f"rawout{k}", m_raw3[kkw % 3])
                if dry:
                    m_ck.inc(None)
                else:
                    m_ck.inc(gp.nop())
                M.set(f"gath{k}", m_ck)
                if k + 2 < len(UNITS):
                    w2, ic2 = UNITS[k + 2]
                    cols2 = 512 if w2 != "d" else 176
                    src2 = {"g": idxg, "u": idxu, "d": idxd}[w2]
                    m_idx[k % 2].inc(gp.dma_start(
                        out=g_idx[(k + 2) % 3][:, 0:cols2],
                        in_=bass.AP(src2, ic2 * 128 * cols2,
                                    [[cols2, 128], [1, cols2]])))
                    M.set(f"idx{k + 2}", m_idx[k % 2])
            for hb in range(4):
                M.wait(gp, f"rs_blk{hb}")
                m_rs.inc(gp.collective_compute(
                    "ReduceScatter", ALU.add, replica_groups=[list(range(NCORE))],
                    ins=[rsin[hb].ap().opt()],
                    outs=[rsout[hb].ap().opt()]))
            M.set("rs_done", m_rs)
            if not dry:
                gp.wait_ge(m_rs.h, m_rs.n)

        def body_vector(ve, dry):
            e = e_dve
            e.inc(ve.memset(g_zer[:, :], 0.0))
            M.set("zer_ready", e)
            e.inc(ve.memset(g_ones[:, :], 1.0))
            M.set("ones_ready", e)
            e.inc(ve.memset(g_eps[:, :], EPS))
            M.set("eps_ready", e)
            M.wait(ve, "cb_loaded")
            e.inc(ve.tensor_copy(a0_tabb[:, :], a0_tabf[:, :]))
            M.set("cb_cast", e)
            for t in range(TT64):
                xt = a0_x[t % 2]
                M.wait(ve, f"a0_x{t}")
                e.inc(ve.reduce_sum(g_s1[:, :], xt[:, :], axis=AX.X))
                M.wait(ve, f"a0_sq{t}")
                wait_self(dry, ve, e)
                e.inc(ve.tensor_scalar(g_mu[:, :], g_s1[:, :], 1.0 / DM, None,
                                       ALU.mult))
                e.inc(ve.tensor_scalar(g_var[:, :], g_s2[:, :], 1.0 / DM, None,
                                       ALU.mult))
                wait_self(dry, ve, e)
                e.inc(ve.tensor_tensor(g_mu2[:, :], g_mu[:, :], g_mu[:, :],
                                       ALU.mult))
                wait_self(dry, ve, e)
                e.inc(ve.tensor_tensor(g_var[:, :], g_var[:, :], g_mu2[:, :],
                                       ALU.subtract))
                M.set(f"a0_var{t}", e)
                M.wait(ve, f"a0_rstd{t}")
                e.inc(ve.scalar_tensor_tensor(g_nmr[:, :], g_mu[:, :], -1.0,
                                              g_rstd[:, :], ALU.mult, ALU.mult))
                wait_self(dry, ve, e)
                if t >= 1:
                    M.wait(ve, f"a0_trdone{t - 1}")
                e.inc(ve.tensor_scalar(a0_xn[:, :], xt[:, :], g_rstd[:, :],
                                       g_nmr[:, :], ALU.mult, ALU.add))
                M.set(f"a0_xn{t}", e)
                if t >= 2:
                    M.wait(ve, f"a0_wr{t - 2}")
                for grp in range(4):
                    M.wait(ve, f"a0_tr{t}_{grp}")
                    e.inc(ve.tensor_copy(a0_ev[t % 2].ap(f0=grp * 1024, fn=1024),
                                         psb(psb_dve[0] % 2, 1024)))
                    psb_evac_mark()
                M.set(f"a0_ev{t}", e)
                M.set(f"a0_done{t}", e)
                stage_at(ve, dry, "dve", "a0", t)
            for u in range(3 * EH):
                stage_at(ve, dry, "dve", "a1", u)
            for h in range(EH):
                for grp in range(2):
                    M.wait(ve, f"b_vtr{h}_{grp}")
                    e.inc(ve.tensor_copy(b_vtm.ap(f0=grp * 1024, fn=1024),
                                         psb(psb_dve[0] % 2, 1024)))
                    psb_evac_mark()
                M.set(f"b_vtm{h}", e)
                for qt in range(QT):
                    M.wait(ve, f"b_s{h}_{qt}")
                    e.inc(ve.reduce_max(g_nmx[:, :], ps(0, 2048), axis=AX.X,
                                        negate=True))
                    M.set(f"b_nmx{h}_{qt}", e)
                    M.wait(ve, f"b_exp{h}_{qt}")
                    e.inc(ve.reciprocal(g_rec[:, :], g_srow[:, :]))
                    wait_self(dry, ve, e)
                    e.inc(ve.tensor_scalar(g_diag[:, :], g_ident[:, :],
                                           g_rec[:, :], None, ALU.mult))
                    M.set(f"b_diag{h}_{qt}", e)
                    for pb in range(4):
                        M.wait(ve, f"b_pt{h}_{qt}_{pb}")
                        e.inc(ve.tensor_copy(b_pt.ap(f0=pb * 512, fn=512),
                                             ps(4, 512)))
                        M.set(f"b_ptev{h}_{qt}_{pb}", e)
                if True:
                    stage_at(ve, dry, "dve", "b", h)
            for og in range(8):
                for tg in range(4):
                    for tt in range(4):
                        M.wait(ve, f"b2_mm{og}_{tg}_{tt}")
                        li = (og * 4 + tg) * 4 + tt
                        if li >= 1:
                            pl = li - 1
                            M.wait(ve, f"b2_pw{pl // 16}_{pl % 16 // 4}_{pl % 4}")
                        e.inc(ve.tensor_copy(b2_ev[:, :], ps(tt, 512)))
                        M.set(f"b2_ev{og}_{tg}_{tt}", e)
                stage_at(ve, dry, "dve", "b2", og)
            for k in PRE_CG:
                stage_dve(ve, dry, k)
            for tsb in range(16):
                for ic in range(NIC):
                    bk = (ic % 2) * 2
                    M.wait(ve, f"c_sg{tsb}_{ic}")
                    M.wait(ve, f"c_umm{tsb}_{ic}")
                    if tsb + ic > 0:
                        pl = tsb * NIC + ic - 1
                        M.wait(ve, f"c_hh{pl // NIC}_{pl % NIC}")
                    e.inc(ve.tensor_tensor(c_hh[:, :], c_sg[:, :],
                                           ps(bk + 1, 512), ALU.mult))
                    M.set(f"c_hhv{tsb}_{ic}", e)
                    stage_at(ve, dry, "dve", "cg", tsb * NIC + ic)
            for hb in range(4):
                for j in range(8 * hb, 8 * (hb + 1)):
                    stage_dve(ve, dry, len(GU_UNITS) + j)
                for tt in range(TT64):
                    lin = hb * TT64 + tt
                    M.wait(ve, f"c_dmm{hb}_{tt}")
                    M.wait(ve, f"c_dr{hb}_{tt}")
                    if lin > 0:
                        pl = lin - 1
                        M.wait(ve, f"c_dw{pl // TT64}_{pl % TT64}")
                    e.inc(ve.tensor_tensor(c_ev[:, 0:1024],
                                           ps(2 * (lin % 2), 1024),
                                           c_rs[lin % 2].ap(f0=0, fn=1024),
                                           ALU.add))
                    M.set(f"c_dev{hb}_{tt}", e)
            for tt in range(8):
                M.wait(ve, f"d_ld{tt}")
                if tt >= 1:
                    M.wait(ve, f"d_st{tt - 1}")
                e.inc(ve.tensor_tensor(d_o[:, :], d_rs[:, :], d_x[:, :], ALU.add))
                M.set(f"d_o{tt}", e)

        def body_scalar(sc, dry):
            e = e_act
            for t in range(TT64):
                xt = a0_x[t % 2]
                M.wait(sc, f"a0_x{t}")
                if t >= 1:
                    M.wait(sc, f"a0_var{t - 1}")
                e.inc(sc.activation(a0_sq[:, :], xt[:, :], AF.Square,
                                    accum_out=g_s2[:, :]))
                M.set(f"a0_sq{t}", e)
                M.wait(sc, f"a0_var{t}")
                if t == 0:
                    M.wait(sc, "eps_ready")
                e.inc(sc.activation(g_rstd[:, :], g_var[:, :], AF.Ln,
                                    bias=g_eps[:, :], scale=1.0))
                wait_self(dry, sc, e)
                e.inc(sc.activation(g_rstd[:, :], g_rstd[:, :], AF.Exp,
                                    scale=-0.5))
                M.set(f"a0_rstd{t}", e)
                stage_at(sc, dry, "act", "a0", t)
            for oi in range(3):
                for ec in range(EH):
                    for half in range(2):
                        wl = (oi * EH + ec) * 2 + half
                        M.wait(sc, f"a1_w{wl}")
                        if wl >= 1:
                            M.wait(sc, f"a1_mmwb{wl - 1}")
                        e.inc(sc.copy(a1_wb.ap(f0=half * 2048, fn=2048),
                                      a1_wf[:, :]))
                        M.set(f"a1_cast{wl}", e)
                    M.wait(sc, f"a1_mm{oi}_{ec}")
                    if oi + ec > 0:
                        pl = oi * EH + ec - 1
                        M.wait(sc, f"qkv_st{pl // EH}_{pl % EH}")
                    e.inc(sc.activation(a1_out[:, :], ps(0, 2048), AF.Copy,
                                        scale=SCALE if oi == 0 else 1.0))
                    M.set(f"a1_out{oi}_{ec}", e)
                    stage_at(sc, dry, "act", "a1", oi * EH + ec)
            for h in range(EH):
                for qt in range(QT):
                    M.wait(sc, f"b_nmx{h}_{qt}")
                    e.inc(sc.activation(b_p[:, :], ps(0, 2048), AF.Exp,
                                        bias=g_nmx[:, :], scale=1.0,
                                        accum_out=g_srow[:, :]))
                    M.set(f"b_exp{h}_{qt}", e)
                    M.wait(sc, f"b_av{h}_{qt}")
                    e.inc(sc.copy(b_a.ap(f0=h * SEQ + qt * 128, fn=128),
                                  ps(5, 128)))
                    M.set(f"b_avev{h}_{qt}", e)
                    if qt == QT - 1:
                        stage_at(sc, dry, "act", "b", h)
            for og in range(8):
                M.wait(sc, f"wo{og}")
                if og >= 1:
                    M.wait(sc, f"b2_lastmm{og - 1}")
                for q in range(4):
                    e.inc(sc.copy(b2_wb.ap(f0=q * 2048, fn=2048),
                                  b2_wf.ap(f0=q * 2048, fn=2048)))
                M.set(f"b2_cast{og}", e)
                stage_at(sc, dry, "act", "b2", og)
            for k in PRE_CG:
                stage_act(sc, k)
            M.set("gu_strips0", m_sw[0])
            M.set("gu_strips1", m_sw[1])
            for tsb in range(16):
                for ic in range(NIC):
                    bk = (ic % 2) * 2
                    M.wait(sc, f"c_gmm{tsb}_{ic}")
                    e.inc(sc.activation(c_sg[:, :], ps(bk, 512), AF.Silu))
                    M.set(f"c_sg{tsb}_{ic}", e)
                    stage_at(sc, dry, "act", "cg", tsb * NIC + ic)
            for kk in range(len(WD_UNITS)):
                stage_act(sc, len(GU_UNITS) + kk)

        def body_tensor(te, dry):
            e = e_pe
            M.wait(te, "ident")
            for t in range(TT64):
                M.wait(te, f"a0_xn{t}")
                for grp in range(4):
                    half = psb_begin(te)
                    for blk in range(8):
                        hcb = grp * 8 + blk
                        ii = te.transpose(psb(half, 128, f0=blk * 128),
                                          a0_xn.ap(f0=hcb * 128, fn=128),
                                          g_ident[:, :])
                    e.inc(ii)
                    M.set(f"a0_tr{t}_{grp}", e)
                M.set(f"a0_trdone{t}", e)
                stage_at(te, dry, "pe", "a0", t)
            M.wait(te, "xnb")
            for oi in range(3):
                for ec in range(EH):
                    if oi + ec > 0:
                        pl = oi * EH + ec - 1
                        M.wait(te, f"a1_out{pl // EH}_{pl % EH}")
                    for half in range(2):
                        wl = (oi * EH + ec) * 2 + half
                        M.wait(te, f"a1_cast{wl}")
                        for hcb in range(half * 16, half * 16 + 16):
                            for nn in range(4):
                                ii = te.matmul(
                                    ps(nn, 512),
                                    a1_wb.ap(f0=hcb * 128, fn=128),
                                    a1_xnb.ap(f0=hcb * SEQ + nn * 512, fn=512),
                                    start=(hcb == 0), stop=(hcb == HC - 1))
                        e.inc(ii)
                        M.set(f"a1_mmwb{wl}", e)
                    M.set(f"a1_mm{oi}_{ec}", e)
                    stage_at(te, dry, "pe", "a1", oi * EH + ec)
            for h in range(EH):
                M.wait(te, f"b_qkv{h}")
                for grp in range(2):
                    half = psb_begin(te)
                    for blk in range(8):
                        kc = grp * 8 + blk
                        ii = te.transpose(psb(half, 128, f0=blk * 128),
                                          b_v[h % 2].ap(f0=kc * 128, fn=128),
                                          g_ident[:, :])
                    e.inc(ii)
                    M.set(f"b_vtr{h}_{grp}", e)
                M.wait(te, f"b_vtm{h}")
                for qt in range(QT):
                    if h + qt > 0:
                        pq = (h, qt - 1) if qt else (h - 1, QT - 1)
                        M.wait(te, f"b_exp{pq[0]}_{pq[1]}")
                    else:
                        M.wait(te, f"a1_out{2}_{EH - 1}")
                    for nn in range(4):
                        ii = te.matmul(ps(nn, 512),
                                       b_q[h % 2].ap(f0=qt * 128, fn=128),
                                       b_k[h % 2].ap(f0=nn * 512, fn=512),
                                       start=True, stop=True)
                    e.inc(ii)
                    M.set(f"b_s{h}_{qt}", e)
                    M.wait(te, f"b_diag{h}_{qt}")
                    for pb in range(4):
                        if pb >= 1:
                            M.wait(te, f"b_ptev{h}_{qt}_{pb - 1}")
                        for u in range(4):
                            kc = pb * 4 + u
                            ii = te.matmul(ps(4, 128, f0=u * 128),
                                           b_p.ap(f0=kc * 128, fn=128),
                                           g_diag[:, :], start=True, stop=True)
                        e.inc(ii)
                        M.set(f"b_pt{h}_{qt}_{pb}", e)
                    M.wait(te, f"b_ptev{h}_{qt}_3")
                    if h + qt > 0:
                        pq = (h, qt - 1) if qt else (h - 1, QT - 1)
                        M.wait(te, f"b_avev{pq[0]}_{pq[1]}")
                    for kc in range(16):
                        ii = te.matmul(ps(5, 128),
                                       b_vtm.ap(f0=kc * 128, fn=128),
                                       b_pt.ap(f0=kc * 128, fn=128),
                                       start=(kc == 0), stop=(kc == 15))
                    e.inc(ii)
                    M.set(f"b_av{h}_{qt}", e)
                M.set(f"b_done{h}", e)
                stage_at(te, dry, "pe", "b", h)
            for og in range(8):
                M.wait(te, f"b2_cast{og}")
                if og == 0:
                    M.wait(te, f"b_exp{EH - 1}_{QT - 1}")
                for tg in range(4):
                    for tt in range(4):
                        if og + tg > 0:
                            ptg = (og, tg - 1) if tg else (og - 1, 3)
                            M.wait(te, f"b2_ev{ptg[0]}_{ptg[1]}_{tt}")
                        for ec in range(EH):
                            ii = te.matmul(
                                ps(tt, 512),
                                b_a.ap(f0=ec * SEQ + (tg * 4 + tt) * 128, fn=128),
                                b2_wb.ap(f0=ec * 512, fn=512),
                                start=(ec == 0), stop=(ec == EH - 1))
                        e.inc(ii)
                        M.set(f"b2_mm{og}_{tg}_{tt}", e)
                M.set(f"b2_lastmm{og}", e)
                stage_at(te, dry, "pe", "b2", og)
            for k in PRE_CG:
                stage_pe(te, k)
            for tsb in range(16):
                M.wait(te, f"c_xn{tsb}")
                for ic in range(NIC):
                    M.wait(te, f"c_w{tsb}_{ic}")
                    bk = (ic % 2) * 2
                    if tsb * NIC + ic >= 2:
                        pl = tsb * NIC + ic - 2
                        M.wait(te, f"c_hhv{pl // NIC}_{pl % NIC}")
                    else:
                        M.wait(te, "b2_ev7_3_3")
                    for op in range(2):
                        wbuf = (c_wg if op == 0 else c_wu)[ic % 2]
                        for hcb in range(HC):
                            ii = te.matmul(ps(bk + op, 512),
                                           wbuf.ap(f0=hcb * 128, fn=128),
                                           c_xn.ap(f0=hcb * 512, fn=512),
                                           start=(hcb == 0), stop=(hcb == HC - 1))
                        e.inc(ii)
                        if op == 0:
                            M.set(f"c_gmm{tsb}_{ic}", e)
                    M.set(f"c_umm{tsb}_{ic}", e)
                    stage_at(te, dry, "pe", "cg", tsb * NIC + ic)
            M.wait(te, f"c_hhv{15}_{NIC - 1}")
            for hb in range(4):
                for j in range(8 * hb, 8 * (hb + 1)):
                    stage_pe(te, len(GU_UNITS) + j)
                M.wait(te, f"wd_ev{len(GU_UNITS) + 8 * (hb + 1) - 1}")
                for tt in range(TT64):
                    lin = hb * TT64 + tt
                    M.wait(te, f"c_dl{hb}_{tt}")
                    if lin >= 2:
                        pl = lin - 2
                        M.wait(te, f"c_dev{pl // TT64}_{pl % TT64}")
                    for ic in range(NIC):
                        for nn in range(2):
                            ii = te.matmul(
                                ps(2 * (lin % 2) + nn, 512),
                                c_ht[lin % 2].ap(f0=ic * 128, fn=128),
                                bass.AP(c_wdT.h,
                                        c_wdT.c0 + ic * 4096 + hb * 1024 + nn * 512,
                                        [[c_wdT.W, 128], [1, 512]]),
                                start=(ic == 0), stop=(ic == NIC - 1))
                    e.inc(ii)
                    M.set(f"c_dmm{hb}_{tt}", e)

        M.dry = True
        dz = DryEng()
        psb_pe[0] = 0
        psb_dve[0] = 0
        for s in sems:
            s.n = 0
        body_vector(dz, True)
        body_scalar(dz, True)
        body_tensor(dz, True)
        body_gpsimd(dz, True)
        body_sync(dz, True)

        M.dry = False
        psb_pe[0] = 0
        psb_dve[0] = 0
        for s in sems:
            s.n = 0
            s.h = E(nc.semaphore(s.name))

        block = E(nc.Block())

        @block.vector
        def _(ve):
            body_vector(ve, False)

        @block.scalar
        def _(sc):
            body_scalar(sc, False)

        @block.tensor
        def _(te):
            body_tensor(te, False)

        @block.gpsimd
        def _(gp):
            body_gpsimd(gp, False)

        @block.sync
        def _(sy):
            body_sync(sy, False)

    return nc


def kernel(L0_attn_x, L0_attn_w_q, L0_attn_w_k, L0_attn_w_v, L0_attn_w_o,
           L0_ffn_codebook, L0_ffn_idx_gate, L0_ffn_idx_up, L0_ffn_idx_down):
    x = np.ascontiguousarray(np.asarray(L0_attn_x, dtype=np.float32)).reshape(TOK, DM)
    wq_ = np.asarray(L0_attn_w_q, np.float32)
    wk_ = np.asarray(L0_attn_w_k, np.float32)
    wv_ = np.asarray(L0_attn_w_v, np.float32)
    wo_ = np.asarray(L0_attn_w_o, np.float32)
    cb = np.ascontiguousarray(np.asarray(L0_ffn_codebook, np.float32))
    ig = np.asarray(L0_ffn_idx_gate, np.int32)
    iu = np.asarray(L0_ffn_idx_up, np.int32)
    idn = np.asarray(L0_ffn_idx_down, np.int32)

    def blk_w(w, s):
        wt = w.T[:, s * ES:(s + 1) * ES]
        return np.ascontiguousarray(
            wt.reshape(HC, 128, EH, 128).transpose(2, 0, 1, 3))

    def blk_wo(s):
        wt = wo_.T[s * ES:(s + 1) * ES, :]
        return np.ascontiguousarray(
            wt.reshape(EH, 128, 8, 512).transpose(2, 0, 1, 3))

    def shard_gu(idx, c):
        sh = idx[c * IRE:(c + 1) * IRE]
        sh = np.concatenate([sh, np.zeros((ISH - IRE, 512), np.int32)], 0)
        return np.ascontiguousarray(sh.reshape(NIC, 128, 512))

    def shard_d(c):
        sh = idn[:, c * (IRE // CS):(c + 1) * (IRE // CS)]
        sh = np.concatenate([sh, np.zeros((DM, (ISH - IRE) // CS), np.int32)], 1)
        return np.ascontiguousarray(sh.reshape(HC, 128, 176))

    in_maps = []
    for c in range(NCORE):
        s = c % 2
        in_maps.append({
            "xfull": x,
            "xres": np.ascontiguousarray(x[c * HALF:(c + 1) * HALF]),
            "wq": blk_w(wq_, s), "wk": blk_w(wk_, s), "wv": blk_w(wv_, s),
            "wo": blk_wo(s),
            "idxg": shard_gu(ig, c), "idxu": shard_gu(iu, c), "idxd": shard_d(c),
            "cbt": cb,
        })

    nc = build_program()
    res = run_bass_kernel_spmd(nc, in_maps, core_ids=list(range(NCORE)))
    outp = np.empty((TOK, DM), np.float32)
    for c in range(NCORE):
        outp[c * HALF:(c + 1) * HALF] = res.results[c]["o"]
    return outp.reshape(BATCH, SEQ, DM)
